# version 02
# baseline (speedup 1.0000x reference)
import numpy as np
import ml_dtypes
import concourse.bass as bass
import concourse.mybir as mybir
from concourse.bass_utils import run_bass_kernel_spmd

F32 = mybir.dt.float32
BF16 = mybir.dt.bfloat16
I32 = mybir.dt.int32
AF = mybir.ActivationFunctionType
ALU = mybir.AluOpType
AX = mybir.AxisListType
f32 = np.float32

B = 4
T = 2048
C = 1024
H = 16
CCH = 8
NTILE = 16
DFF = 4096
NE = 4
LS = 256
NTOK = B * T

SBUF_BASE = 16384 + 768
SBUF_LIMIT = 16384 + 207 * 1024

ENGS = ("pe", "act", "dve", "pool", "sp")


class Prog:
    def __init__(self, nc):
        self.nc = nc
        self.ops = []
        self.count = {}
        self.sems = {}
        self.bank_tok = {}

    def sem(self, name):
        if name not in self.sems:
            self.sems[name] = self.nc.alloc_semaphore(name)
            self.count[name] = 0
        return self.sems[name]

    def emit(self, eng, fn, waits=(), inc=None, inc_n=1, drain=False):
        ws = []
        for w in waits:
            if w is None:
                continue
            s, v = w
            self.sem(s)
            assert v <= self.count[s], f"future wait {s}:{v}>{self.count[s]}"
            ws.append((s, v))
        self.ops.append((eng, fn, ws, inc, inc_n, drain))
        if inc is not None:
            self.sem(inc)
            self.count[inc] += inc_n
            return (inc, self.count[inc])
        return None

    def tok(self, sem):
        self.sem(sem)
        return (sem, self.count[sem])

    def pe(self, fn, waits=(), inc=None):
        return self.emit("pe", fn, waits, inc)

    def act(self, fn, waits=(), inc="act", drain=False):
        return self.emit("act", fn, waits, inc, 1, drain)

    def dve(self, fn, waits=(), inc="dve", drain=False):
        return self.emit("dve", fn, waits, inc, 1, drain)

    def dma(self, q, fn, waits=(), inc=None):
        inc = inc or ("ld" if q == "sp" else "sw")
        return self.emit(q, fn, waits, inc, 16)

    def realize(self):
        nc = self.nc
        last_wait = {}
        with nc.Block() as blk:
            def run_engine(e_name):
                def body(handle):
                    for (eng, fn, ws, inc, inc_n, drain) in self.ops:
                        if eng != e_name:
                            continue
                        for (s, v) in ws:
                            key = (e_name, s)
                            if last_wait.get(key, -1) >= v:
                                continue
                            handle.wait_ge(self.sems[s], v)
                            last_wait[key] = v
                        instr = fn(handle)
                        if inc is not None:
                            instr.then_inc(self.sems[inc], inc_n)
                        if drain:
                            handle.drain()
                return body
            blk.sync(run_engine("sp"))
            blk.gpsimd(run_engine("pool"))
            blk.tensor(run_engine("pe"))
            blk.scalar(run_engine("act"))
            blk.vector(run_engine("dve"))


def build_program(upto=None):
    nc = bass.Bass("TRN2", target_bir_lowering=False,
                   detect_race_conditions=False)

    xin = nc.dram_tensor("x", [NTOK, C], F32, kind="ExternalInput")
    out = nc.dram_tensor("out", [NTOK, C], F32, kind="ExternalOutput")
    wa, wf = {}, {}
    for n, shp in (("w_qkv", [C, 3 * C]), ("ln_w", [1, C]), ("w_proj", [C, C])):
        wa[n] = nc.dram_tensor("a_" + n, shp, F32, kind="ExternalInput")
    ffshapes = (("Wq", [C, C]), ("bq", [128, 8]), ("Wk", [C, C]), ("bk", [128, 8]),
                ("Wv", [C, C]), ("bv", [1, C]), ("Wo", [C, C]), ("bo", [1, C]),
                ("router_w", [1, C]), ("router_b", [1, 1]),
                ("W1", [NE, C, DFF]), ("b1", [NE, 128, 32]),
                ("W2", [NE, DFF, C]), ("b2", [NE, C]),
                ("ln1_g", [1, C]), ("ln1_b", [1, C]),
                ("ln2_g", [1, C]), ("ln2_b", [1, C]))
    for n, shp in ffshapes:
        wf[n] = nc.dram_tensor("f_" + n, shp, F32, kind="ExternalInput")

    d_resid = nc.dram_tensor("d_resid", [NTOK, C], F32, kind="Internal")
    d_src = nc.dram_tensor("d_src", [NTOK, C], F32, kind="Internal")
    d_x1ff = nc.dram_tensor("d_x1ff", [NTOK, C], F32, kind="Internal")
    d_x1T = nc.dram_tensor("d_x1T", [B, CCH, 128, T], BF16, kind="Internal")
    if upto is not None:
        dbg_slab = nc.dram_tensor("dbg_slab", [3, 128, CCH * T], BF16,
                                  kind="ExternalOutput")
        dbg_va = nc.dram_tensor("dbg_va", [128, NTILE * H * 65], BF16,
                                kind="ExternalOutput")
        dbg_f = nc.dram_tensor("dbg_f", [NTOK, C], F32, kind="ExternalOutput")
        dbg_m = nc.dram_tensor("dbg_m", [128, 256], F32, kind="ExternalOutput")

    ident_dr = nc.inline_tensor(np.eye(128, dtype=ml_dtypes.bfloat16),
                                name="ident_c")
    kk = np.arange(128)[:, None]
    qq = np.arange(LS)[None, :]
    cmask_np = np.concatenate([(kk <= qq), (kk + 128 <= qq)], axis=1)
    cmask_dr = nc.inline_tensor(
        np.ascontiguousarray(cmask_np.astype(ml_dtypes.bfloat16)), name="cmask_c")

    alloc_log = []

    def at(name, shape, dtype, off):
        per = int(np.prod(shape[1:])) * mybir.dt.size(dtype)
        assert SBUF_BASE <= off and off + per <= SBUF_LIMIT, \
            f"sbuf overflow {name}: [{off}, {off + per})"
        alloc_log.append((name, off, off + per))
        return nc.alloc_sbuf_tensor_at(name, list(shape), dtype, offset=off)

    KB = 1024
    O_PERS = SBUF_BASE
    s_ident = at("s_ident", [128, 128], BF16, O_PERS)
    s_cmask = at("s_cmask", [128, 512], BF16, O_PERS + 256)
    s_ones = at("s_ones", [128, 128], BF16, O_PERS + 1280)
    s_cols = at("s_cols", [128, 64], F32, O_PERS + 1536)
    s_tiny = at("s_tiny", [128, 32], F32, O_PERS + 1792)
    O_S0 = O_PERS + 2 * KB
    O_S1 = O_S0 + 32 * KB
    O_S2 = O_S1 + 32 * KB
    O_S3 = O_S2 + 32 * KB
    O_S4 = O_S3 + 34 * KB
    O_S5 = O_S4 + 29952
    O_S6 = O_S5 + 23 * KB
    O_END = O_S6 + 22 * KB
    assert O_END <= SBUF_LIMIT, O_END

    srcT = at("srcT", [128, CCH * T], BF16, O_S0)
    attnT = at("attnT", [128, CCH * T], BF16, O_S0)
    QTt = at("QTt", [128, CCH * T], BF16, O_S1)
    sW1g = [at("sW1g0", [128, CCH * 1024], BF16, O_S1),
            at("sW1g1", [128, CCH * 1024], BF16, O_S2)]
    sW2g = [at("sW2g0", [128, 8 * 1024], BF16, O_S1 + 16 * KB),
            at("sW2g1", [128, 8 * 1024], BF16, O_S2 + 16 * KB)]
    sWo = at("sWo", [128, CCH * 1024], BF16, O_S1)
    KTt = at("KTt", [128, CCH * T], BF16, O_S2)
    x1Tstg = at("x1Tstg", [128, CCH * T], BF16, O_S2)
    VA = at("VA", [128, NTILE * H * 65], BF16, O_S3)
    sxt = [at("sxt0", [128, CCH * 512], BF16, O_S3),
           at("sxt1", [128, CCH * 512], BF16, O_S3 + 8 * KB)]
    s_b2b = at("s_b2b", [128, C], F32, O_S3 + 16 * KB)
    h_sb = at("h_sb", [128, 8 * 512], BF16, O_S3 + 20 * KB)
    s_cb = [at(f"s_cb{i}", [128, C], F32, O_S3 + i * 4 * KB) for i in range(4)]
    sWs = at("sWs", [128, CCH * 1024], BF16, O_S4)
    rrow = at("rrow", [1, 512], F32, O_S4)
    sqh = at("sqh", [64, 512], BF16, O_S4 + 16 * KB)
    lnq2 = at("lnq2", [1, T], BF16, O_S4 + 17 * KB)
    k2r = at("k2r", [1, T], BF16, O_S4 + 21 * KB)
    b8r = at("b8r", [1, T], BF16, O_S4 + 25 * KB)
    s_kv = at("s_kv", [1, 8], F32, O_S4 + 29 * KB)
    ytmp = [at("ytmp0", [128, 4 * C], F32, O_S4),
            at("ytmp1", [128, 4 * C], F32, O_S4 + 16 * KB)]
    stgQ = [at("stgQ0", [65, T], BF16, O_S5),
            at("stgQ1", [65, T], BF16, O_S5 + 4 * KB)]
    stgK = [at("stgK0", [65, T], BF16, O_S5 + 8 * KB),
            at("stgK1", [65, T], BF16, O_S5 + 12 * KB)]
    PTb = [at("PTb0", [128, 512], BF16, O_S5 + 16 * KB),
           at("PTb1", [128, 512], BF16, O_S5 + 17 * KB)]
    rb_sb = at("rb_sb", [64, 512], BF16, O_S5 + 18 * KB)
    rowb = at("rowb", [1, 512], BF16, O_S5 + 19 * KB)
    bv_bf = at("bv_bf", [1, C], BF16, O_S5 + 20 * KB)
    s_masks = at("s_masks", [128, 256], F32, O_S5 + 22 * KB)
    sx_t = [at("sx_t0", [128, C], F32, O_S6),
            at("sx_t1", [128, C], F32, O_S6 + 4 * KB)]
    t_t = at("t_t", [128, C], F32, O_S6 + 8 * KB)
    u_t = at("u_t", [128, C], F32, O_S6 + 12 * KB)
    w_t = at("w_t", [128, C], F32, O_S6 + 16 * KB)
    xb_t = at("xb_t", [128, C], BF16, O_S6 + 20 * KB)

    psum = [nc.alloc_psum_tensor(f"pb{i}", [128, 512], F32) for i in range(8)]

    P = Prog(nc)
    BT = P.bank_tok

    def bank_wait(b):
        return BT.get(b)

    def bank_done(b, tok):
        BT[b] = tok

    def ld_bcast_row(dram_row, dest, waits=()):
        n = dram_row.shape[-1]
        return P.dma("sp", lambda e, d=dram_row, o=dest, n=n: e.dma_start(
            out=o[:, 0:n], in_=d[0:1, 0:n].broadcast_to([128, n])), waits=waits)

    def ln_cols(x_ap, eps, waits=()):
        mu = s_tiny[:, 0:1]
        s2 = s_tiny[:, 8:9]
        l2 = s_tiny[:, 16:17]
        rstd = s_tiny[:, 24:25]
        P.dve(lambda e: e.reduce_sum(mu, x_ap, axis=AX.X), waits=list(waits),
              drain=True)
        P.dve(lambda e: e.tensor_scalar_mul(mu, mu, float(1.0 / C)), drain=True)
        P.dve(lambda e: e.tensor_scalar(t_t[:], x_ap, mu, None, op0=ALU.subtract),
              drain=True)
        tv = P.tok("dve")
        tq = P.act(lambda e: e.activation(u_t[:], t_t[:], AF.Square), waits=[tv])
        tr = P.dve(lambda e: e.reduce_sum(s2, u_t[:], axis=AX.X), waits=[tq])
        P.act(lambda e: e.activation(l2, s2, AF.Ln, bias=f32(eps),
                                     scale=f32(1.0 / C)), waits=[tr], drain=True)
        P.act(lambda e: e.activation(rstd, l2, AF.Exp, scale=f32(-0.5)),
              drain=True)
        return P.tok("act")

    def transpose_chunks(src_bf, dst, dst_col0, waits):
        for cc in range(CCH):
            b = 6 + (cc % 2)
            pv = psum[b][:, 0:128].bitcast(BF16)[:, 0:128]
            w = [bank_wait(b)] + list(waits)
            tk = P.pe(lambda e, pv=pv, cc=cc, s=src_bf: e.transpose(
                pv, s[:, cc * 128:(cc + 1) * 128], s_ident[:]),
                waits=w, inc="pe")
            tc = P.dve(lambda e, pv=pv, c=dst_col0 + cc * T, d=dst:
                       e.tensor_copy(d[:, c:c + 128], pv), waits=[tk])
            bank_done(b, tc)
        return P.tok("dve")

    def stream_w(dram_w, dest, ncols=1024, col0=0, waits=()):
        return P.dma("pool", lambda e: e.dma_start(
            out=dest[:, 0:CCH * ncols].rearrange("p (c f) -> p c f", c=CCH),
            in_=dram_w[:, col0:col0 + ncols].rearrange("(c p) f -> p c f", p=128)),
            waits=waits)

    def emit_lnp_src(src_dram, be, spill, first_wait=None):
        prev_spill = None
        for i in range(NTILE):
            r0 = (be * NTILE + i) * 128
            p = i % 2
            lw = [P.tok("dve")]
            if first_wait is not None and i == 0:
                lw.append(first_wait)
            tld = P.dma("sp", lambda e, r=r0, p=p: e.dma_start(
                out=sx_t[p][:], in_=src_dram[r:r + 128, :]), waits=lw)
            ta = ln_cols(sx_t[p][:], 1e-5, waits=[tld])
            ppe = P.tok("pe")
            P.dve(lambda e: e.tensor_scalar_mul(xb_t[:], t_t[:], s_tiny[:, 24:25]),
                  waits=[tld, ta, ppe])
            if spill:
                tsp = P.dve(lambda e: e.tensor_scalar_mul(
                    w_t[:], t_t[:], s_tiny[:, 24:25]),
                    waits=[prev_spill] if prev_spill else [])
                prev_spill = P.dma("sp", lambda e, r=r0: e.dma_start(
                    out=d_src[r:r + 128, :], in_=w_t[:]), waits=[tsp])
            tv = P.tok("dve")
            import os
            if not os.environ.get("DBG_NO_TPOSE"):
                transpose_chunks(xb_t[:], srcT, i * 128, [tv])
            if upto == "f1_t0" and i == 0:
                P.dma("sp", lambda e: e.dma_start(
                    out=dbg_m[:, 0:32], in_=s_tiny[:]), waits=[tv], inc="st")
                P.dma("sp", lambda e: e.dma_start(
                    out=dbg_f[0:128, :], in_=t_t[:]), waits=[tv], inc="st")
                P.dma("sp", lambda e: e.dma_start(
                    out=dbg_f[128:256, :], in_=w_t[:]), waits=[tv], inc="st")
                P.emit("sp", lambda e: e.nop(), waits=[P.tok("st")])
                P.realize()
                return None
        return P.tok("dve")

    def emit_ffn_qkv(be):
        pe0 = P.tok("pe")
        bvl = P.dma("pool", lambda e: e.dma_start(out=bv_bf[:], in_=wf["bv"][:]),
                    waits=[pe0])
        c0 = P.dma("sp", lambda e: e.dma_start(
            out=s_cols[:, 0:8], in_=wf["bq"][:]), waits=[pe0])
        c1 = P.dma("sp", lambda e: e.dma_start(
            out=s_cols[:, 8:16], in_=wf["bk"][:]), waits=[pe0])
        srcT_tok = P.tok("dve")
        for (wname, dst, b0) in (("Wq", QTt, 0), ("Wk", KTt, 8)):
            tw = stream_w(wf[wname], sWs, waits=[P.tok("pe")])
            for fc in range(CCH):
                for tb in range(4):
                    bnk = (fc * 4 + tb) % 4
                    w0 = [bank_wait(bnk)]
                    if fc == 0 and tb == 0:
                        w0 += [tw, srcT_tok, c0, c1]
                    for cc in range(CCH):
                        P.pe(lambda e, fc=fc, tb=tb, cc=cc, bnk=bnk:
                             e.matmul(psum[bnk][:],
                                      sWs[:, cc * 1024 + fc * 128:
                                          cc * 1024 + fc * 128 + 128],
                                      srcT[:, cc * T + tb * 512:
                                           cc * T + tb * 512 + 512],
                                      start=(cc == 0), stop=(cc == CCH - 1)),
                             waits=w0 if cc == 0 else [],
                             inc="pe" if cc == CCH - 1 else None)
                    tp = P.tok("pe")
                    te = P.dve(lambda e, fc=fc, tb=tb, bnk=bnk, dst=dst, b0=b0:
                               e.tensor_scalar(
                                   dst[:, fc * T + tb * 512:fc * T + tb * 512 + 512],
                                   psum[bnk][:], s_cols[:, b0 + fc:b0 + fc + 1],
                                   None, op0=ALU.add), waits=[tp])
                    bank_done(bnk, te)
        tw = stream_w(wf["Wv"], sWs, waits=[P.tok("pe")])
        tmo = P.dve(lambda e: e.memset(
            VA[:].rearrange("p (t h e2) -> p t h e2", t=NTILE, h=H)
            [:, :, :, 64:65], 1.0))
        for i in range(NTILE):
            for fb in range(2):
                bnk = (i * 2 + fb) % 4
                w0 = [bank_wait(bnk)]
                if i == 0 and fb == 0:
                    w0 += [tw, tmo, bvl]
                for cc in range(CCH):
                    P.pe(lambda e, i=i, fb=fb, cc=cc, bnk=bnk:
                         e.matmul(psum[bnk][:],
                                  srcT[:, cc * T + i * 128:cc * T + i * 128 + 128],
                                  sWs[:, cc * 1024 + fb * 512:
                                      cc * 1024 + fb * 512 + 512],
                                  start=(cc == 0), stop=False),
                         waits=w0 if cc == 0 else [],
                         inc=None)
                P.pe(lambda e, fb=fb, bnk=bnk:
                     e.matmul(psum[bnk][:], s_ones[0:1, 0:128],
                              bv_bf[0:1, fb * 512:fb * 512 + 512],
                              start=False, stop=True), inc="pe")
                tp = P.tok("pe")
                te = P.dve(lambda e, i=i, fb=fb, bnk=bnk: e.tensor_copy(
                    VA[:].rearrange("p (t h e2) -> p t h e2", t=NTILE, h=H)
                    [:, i, fb * 8:fb * 8 + 8, 0:64],
                    psum[bnk][:].rearrange("p (h d) -> p h d", h=8)),
                    waits=[tp])
                bank_done(bnk, te)
        return P.tok("dve")

    def emit_stage_head(h, par, shift=20.0, prev=None):
        cc = h // 2
        half = 64 * (h % 2)
        if prev is None:
            prev = P.tok("pe")
        P.dve(lambda e: e.tensor_copy(
            stgQ[par][0:64, :], QTt[half:half + 64, cc * T:(cc + 1) * T]),
            waits=[prev])
        P.dve(lambda e: e.tensor_copy(
            stgK[par][0:64, :], KTt[half:half + 64, cc * T:(cc + 1) * T]))
        P.dve(lambda e: e.memset(stgK[par][64:65, :], 1.0))
        tcp = P.tok("dve")
        for (stg, dstrow, ln_out) in ((stgQ, lnq2, True), (stgK, k2r, False)):
            for jq in range(4):
                tsq = P.act(lambda e, stg=stg, jq=jq: e.activation(
                    sqh[0:64, :], stg[par][0:64, jq * 512:(jq + 1) * 512],
                    AF.Square), waits=[tcp, P.tok("pe")])
                pm = P.pe(lambda e: e.matmul(
                    psum[4][0:1, 0:512], s_ones[0:64, 0:1], sqh[0:64, :],
                    start=True, stop=True),
                    waits=[tsq, bank_wait(4)], inc="pe")
                if ln_out:
                    tc = P.act(lambda e, jq=jq: e.activation(
                        lnq2[0:1, jq * 512:(jq + 1) * 512], psum[4][0:1, 0:512],
                        AF.Ln), waits=[pm], drain=(jq == 3))
                else:
                    tc = P.act(lambda e, jq=jq: e.activation(
                        k2r[0:1, jq * 512:(jq + 1) * 512], psum[4][0:1, 0:512],
                        AF.Copy, bias=0.0), waits=[pm], drain=(jq == 3))
                bank_done(4, tc)
        tk2 = P.tok("act")
        k2m = s_kv[0:1, 0:1]
        hlf = s_kv[0:1, 1:2]
        P.dve(lambda e: e.reduce_max(k2m, k2r[:], axis=AX.X), waits=[tk2], drain=True)
        tvm = P.tok("dve")
        P.act(lambda e: e.activation(hlf, k2m, AF.Ln), waits=[tvm], drain=True)
        ta = P.tok("act")
        P.dve(lambda e: e.tensor_scalar_mul(hlf, hlf, float(0.5)), waits=[ta], drain=True)
        tvh = P.tok("dve")
        P.act(lambda e: e.activation(b8r[:], lnq2[:], AF.Exp, scale=f32(0.5),
                                     bias=hlf), waits=[tvh], drain=True)
        tb8 = P.tok("act")
        P.dve(lambda e: e.tensor_scalar(
            stgQ[par][64:65, :], b8r[:], float(-1.0), float(8.0 * shift),
            op0=ALU.mult, op1=ALU.add), waits=[tb8], drain=True)
        return P.tok("dve")

    def emit_flash_head(h, par, stg_tok, local):
        cc = h // 2
        half = 64 * (h % 2)
        if local:
            units = [(g, g * LS, LS, [g * 2, g * 2 + 1]) for g in range(8)]
        else:
            units = [(qb, qb * 512, 512, list(range(16))) for qb in range(4)]

        def make_evac(ob, qlo, N, tpv):
            def emit_ev():
                P.dve(lambda e, ob=ob, N=N: e.reciprocal(
                    rrow[0:1, 0:N], psum[ob][64:65, 0:N]), waits=[tpv], drain=True)
                P.dve(lambda e, N=N: e.tensor_copy(rowb[0:1, 0:N], rrow[0:1, 0:N]),
                      drain=True)
                tvr = P.tok("dve")
                trb = P.pe(lambda e, N=N: e.matmul(
                    psum[5][0:64, 0:N], s_ones[0:1, 0:64], rowb[0:1, 0:N],
                    start=True, stop=True), waits=[tvr, bank_wait(5)], inc="pe")
                tcb = P.dve(lambda e, N=N: e.tensor_copy(
                    rb_sb[:, 0:N], psum[5][0:64, 0:N]), waits=[trb], drain=True)
                bank_done(5, tcb)
                ev = P.dve(lambda e, ob=ob, N=N, qlo=qlo: e.tensor_tensor(
                    attnT[half:half + 64, cc * T + qlo:cc * T + qlo + N],
                    psum[ob][0:64, 0:N], rb_sb[:, 0:N], op=ALU.mult),
                    waits=[tcb])
                bank_done(ob, ev)
            return emit_ev

        pending = None
        for (u, qlo, N, kcs) in units:
            ob = 2 + (u % 2)
            exp_toks = {}
            pv_toks = {}
            for j, kc in enumerate(kcs):
                sb = j % 2
                stw = [bank_wait(sb)]
                if j == 0:
                    stw.append(stg_tok)
                st_tok = P.pe(
                    lambda e, kc=kc, sb=sb, qlo=qlo, N=N: e.matmul(
                        psum[sb][0:128, 0:N],
                        stgK[par][:, kc * 128:(kc + 1) * 128],
                        stgQ[par][:, qlo:qlo + N],
                        start=True, stop=True),
                    waits=stw, inc="pe")
                ew = [st_tok]
                if j >= 2:
                    ew.append(pv_toks[kcs[j - 2]])
                tE = P.act(lambda e, sb=sb, N=N: e.activation(
                    PTb[sb][:, 0:N], psum[sb][0:128, 0:N], AF.Exp,
                    scale=f32(0.125)), waits=ew)
                bank_done(sb, tE)
                if local:
                    tE = P.dve(lambda e, sb=sb, j=j, N=N: e.tensor_tensor(
                        PTb[sb][:, 0:N], PTb[sb][:, 0:N],
                        s_cmask[:, j * 256:j * 256 + N], op=ALU.mult),
                        waits=[tE])
                exp_toks[kc] = tE
                pw = [tE]
                if j == 0:
                    pw.append(bank_wait(ob))
                pv_toks[kc] = P.pe(
                    lambda e, kc=kc, sb=sb, ob=ob, N=N,
                    st_=(kc == kcs[0]), sp_=(kc == kcs[-1]): e.matmul(
                        psum[ob][0:65, 0:N],
                        VA[:, kc * (H * 65) + h * 65:kc * (H * 65) + h * 65 + 65],
                        PTb[sb][:, 0:N],
                        start=st_, stop=sp_),
                    waits=pw, inc="pe")
            tpv = P.tok("pe")
            if pending is not None:
                pending()
            pending = make_evac(ob, qlo, N, tpv)
        pending()
        return P.tok("dve")

    def emit_ffn_post(be):
        tw = stream_w(wf["Wo"], sWo, waits=[P.tok("pe"), P.tok("dve")])
        bo_b = ld_bcast_row(wf["bo"], s_cb[0], waits=[P.tok("pe")])
        g1_b = ld_bcast_row(wf["ln1_g"], s_cb[1])
        b1_b = ld_bcast_row(wf["ln1_b"], s_cb[2])
        rw_b = ld_bcast_row(wf["router_w"], s_cb[3])
        rbc = P.dma("sp", lambda e: e.dma_start(
            out=s_cols[:, 48:49],
            in_=wf["router_b"][0:1, 0:1].broadcast_to([128, 1])))
        attnT_tok = P.tok("dve")
        prev_st = None
        for i in range(NTILE):
            r0 = (be * NTILE + i) * 128
            gt = be * NTILE + i
            p = i % 2
            tld = P.dma("sp", lambda e, r=r0, p=p: e.dma_start(
                out=sx_t[p][:], in_=d_src[r:r + 128, :]),
                waits=[P.tok("dve"), P.tok("pe")] if i == 0 else [P.tok("dve")])
            for cb in range(2):
                bnk = cb
                w0 = [bank_wait(bnk)]
                if i == 0 and cb == 0:
                    w0 += [tw, bo_b, g1_b, b1_b, rw_b, rbc, attnT_tok]
                for ccc in range(CCH):
                    P.pe(lambda e, i=i, cb=cb, ccc=ccc, bnk=bnk:
                         e.matmul(psum[bnk][:],
                                  attnT[:, ccc * T + i * 128:ccc * T + i * 128 + 128],
                                  sWo[:, ccc * 1024 + cb * 512:
                                      ccc * 1024 + cb * 512 + 512],
                                  start=(ccc == 0), stop=(ccc == CCH - 1)),
                         waits=w0 if ccc == 0 else [],
                         inc="pe" if ccc == CCH - 1 else None)
                tp = P.tok("pe")
                te = P.dve(lambda e, cb=cb, bnk=bnk: e.tensor_tensor(
                    t_t[:, cb * 512:cb * 512 + 512], psum[bnk][:],
                    s_cb[0][:, cb * 512:cb * 512 + 512], op=ALU.add),
                    waits=[tp] + ([prev_st] if (cb == 0 and prev_st) else []),
                    drain=True)
                bank_done(bnk, te)
            P.dve(lambda e, p=p: e.tensor_tensor(t_t[:], t_t[:], sx_t[p][:],
                                                 op=ALU.add), waits=[tld],
                  drain=True)
            mu = s_tiny[:, 0:1]; s2 = s_tiny[:, 1:2]
            l2 = s_tiny[:, 2:3]; rstd = s_tiny[:, 3:4]
            P.dve(lambda e: e.reduce_sum(mu, t_t[:], axis=AX.X), drain=True)
            P.dve(lambda e: e.tensor_scalar_mul(mu, mu, float(1.0 / C)), drain=True)
            P.dve(lambda e: e.tensor_scalar(u_t[:], t_t[:], mu, None,
                                            op0=ALU.subtract), drain=True)
            tv = P.tok("dve")
            pw = [tv] + ([prev_st] if prev_st else [])
            tq = P.act(lambda e: e.activation(w_t[:], u_t[:], AF.Square), waits=pw)
            tr = P.dve(lambda e: e.reduce_sum(s2, w_t[:], axis=AX.X), waits=[tq],
                       drain=True)
            P.act(lambda e: e.activation(l2, s2, AF.Ln, bias=f32(1e-5),
                                         scale=f32(1.0 / C)), waits=[tr], drain=True)
            P.act(lambda e: e.activation(rstd, l2, AF.Exp, scale=f32(-0.5)),
                  drain=True)
            ta = P.tok("act")
            P.dve(lambda e: e.tensor_scalar_mul(u_t[:], u_t[:], rstd), waits=[ta],
                  drain=True)
            P.dve(lambda e: e.tensor_tensor(u_t[:], u_t[:], s_cb[1][:],
                                            op=ALU.mult), drain=True)
            P.dve(lambda e: e.tensor_tensor(w_t[:], u_t[:], s_cb[2][:],
                                            op=ALU.add), drain=True)
            keys = s_tiny[:, 4:5]
            P.dve(lambda e: e.tensor_tensor(u_t[:], w_t[:], s_cb[3][:],
                                            op=ALU.mult), drain=True)
            P.dve(lambda e: e.reduce_sum(keys, u_t[:], axis=AX.X), drain=True)
            P.dve(lambda e: e.tensor_scalar_add(keys, keys, s_cols[:, 48:49]),
                  drain=True)
            kq = s_tiny[:, 5:6]; fi = s_tiny[:, 6:7]
            fc_ = s_tiny[:, 7:8]; km = s_tiny[:, 8:9]
            ii = s_tiny[:, 9:10].bitcast(I32)[:, 0:1]
            a1 = s_tiny[:, 10:11]; a2 = s_tiny[:, 11:12]; a3 = s_tiny[:, 12:13]
            P.dve(lambda e: e.tensor_scalar_mul(kq, keys, float(0.25)), drain=True)
            P.dve(lambda e: e.tensor_copy(ii, kq), drain=True)
            P.dve(lambda e: e.tensor_copy(fi, ii), drain=True)
            P.dve(lambda e: e.tensor_tensor(fc_, fi, kq, op=ALU.is_gt), drain=True)
            P.dve(lambda e: e.tensor_sub(fi, fi, fc_), drain=True)
            P.dve(lambda e: e.tensor_scalar(fi, fi, float(-4.0), None, op0=ALU.mult), drain=True)
            P.dve(lambda e: e.tensor_add(km, keys, fi), drain=True)
            P.dve(lambda e: e.tensor_single_scalar(a1, km, float(1.0), op=ALU.is_ge), drain=True)
            P.dve(lambda e: e.tensor_single_scalar(a2, km, float(2.0), op=ALU.is_ge), drain=True)
            P.dve(lambda e: e.tensor_single_scalar(a3, km, float(3.0), op=ALU.is_ge), drain=True)
            P.dve(lambda e, g=gt: e.tensor_scalar(
                s_masks[:, g * 4:g * 4 + 1], a1, float(-1.0), float(1.0),
                op0=ALU.mult, op1=ALU.add))
            P.dve(lambda e, g=gt: e.tensor_sub(
                s_masks[:, g * 4 + 1:g * 4 + 2], a1, a2))
            P.dve(lambda e, g=gt: e.tensor_sub(
                s_masks[:, g * 4 + 2:g * 4 + 3], a2, a3))
            P.dve(lambda e, g=gt: e.tensor_copy(
                s_masks[:, g * 4 + 3:g * 4 + 4], a3))
            tx = P.tok("dve")
            prev_st = P.dma("sp", lambda e, r=r0: e.dma_start(
                out=d_x1ff[r:r + 128, :], in_=w_t[:]), waits=[tx])
            ppe = P.tok("pe")
            P.dve(lambda e: e.tensor_copy(xb_t[:], w_t[:]), waits=[ppe])
            tv2 = P.tok("dve")
            transpose_chunks(xb_t[:], x1Tstg, i * 128, [tv2])
        tdone = P.tok("dve")
        return P.dma("sp", lambda e, b=be: e.dma_start(
            out=d_x1T[b].rearrange("c p t -> p c t"),
            in_=x1Tstg[:].rearrange("p (c t) -> p c t", c=CCH)), waits=[tdone])

    def emit_moe():
        x1t_tok = P.tok("ld")
        mask_tok = P.tok("dve")
        pe_flush = P.tok("pe")
        last_acc = None
        gidx = 0
        for e_i in range(NE):
            b2tok = ld_bcast_row(wf["b2"][e_i:e_i + 1, :], s_b2b,
                                 waits=[pe_flush if e_i == 0 else P.tok("dve")])
            b1tok = P.dma("sp", lambda e, e_i=e_i: e.dma_start(
                out=s_cols[:, 16:48], in_=wf["b1"][e_i]),
                waits=[pe_flush] if e_i == 0 else [])
            for g in range(NE):
                par = gidx % 2
                wv = [P.tok("pe"), x1t_tok]
                tw1 = P.dma("pool", lambda e, e_i=e_i, g=g, par=par: e.dma_start(
                    out=sW1g[par][:].rearrange("p (c f) -> p c f", c=CCH),
                    in_=wf["W1"][e_i].rearrange("(c p) f -> p c f", p=128)
                    [:, :, g * 1024:(g + 1) * 1024]), waits=wv)
                tw2 = P.dma("pool", lambda e, e_i=e_i, g=g, par=par: e.dma_start(
                    out=sW2g[par][:].rearrange("p (f c) -> p f c", f=8),
                    in_=wf["W2"][e_i][g * 1024:(g + 1) * 1024, :]
                    .rearrange("(fc p) c -> p fc c", p=128)), waits=wv)
                for tb in range(16):
                    be = tb // 4
                    tl = tb % 4
                    xp = tb % 2
                    tx = P.dma("sp", lambda e, be=be, tl=tl, xp=xp: e.dma_start(
                        out=sxt[xp][:].rearrange("p (c t) -> p c t", c=CCH),
                        in_=d_x1T[be].rearrange("c p t -> p c t")
                        [:, :, tl * 512:tl * 512 + 512]),
                        waits=[P.tok("pe"), x1t_tok])
                    for ffc in range(8):
                        bnk = 6 + (ffc % 2)
                        w0 = [bank_wait(bnk), tw1, tx, b1tok]
                        for ccc in range(CCH):
                            P.pe(lambda e, ffc=ffc, ccc=ccc, bnk=bnk, par=par, xp=xp:
                                 e.matmul(psum[bnk][:],
                                          sW1g[par][:, ccc * 1024 + ffc * 128:
                                                    ccc * 1024 + ffc * 128 + 128],
                                          sxt[xp][:, ccc * 512:ccc * 512 + 512],
                                          start=(ccc == 0), stop=(ccc == CCH - 1)),
                                 waits=w0 if ccc == 0 else [],
                                 inc="pe" if ccc == CCH - 1 else None)
                        tp = P.tok("pe")
                        th = P.act(lambda e, ffc=ffc, bnk=bnk, g=g:
                                   e.activation(h_sb[:, ffc * 512:ffc * 512 + 512],
                                                psum[bnk][:], AF.Relu,
                                                bias=s_cols[:, 16 + g * 8 + ffc:
                                                            17 + g * 8 + ffc],
                                                scale=f32(1.0)), waits=[tp])
                        bank_done(bnk, th)
                    th_all = P.tok("act")
                    for tsub in range(4):
                        for cb in range(2):
                            bnk = (tsub * 2 + cb) % 4
                            w0 = [bank_wait(bnk), tw2, th_all]
                            if tsub == 0 and cb == 0:
                                w0.append(P.tok("sw"))
                            for ffc in range(8):
                                P.pe(lambda e, ts=tsub, cb=cb, ffc=ffc, bnk=bnk, par=par:
                                     e.matmul(psum[bnk][:],
                                              h_sb[:, ffc * 512 + ts * 128:
                                                   ffc * 512 + ts * 128 + 128],
                                              sW2g[par][:, ffc * 1024 + cb * 512:
                                                        ffc * 1024 + cb * 512 + 512],
                                              start=(ffc == 0), stop=(ffc == 7)),
                                     waits=w0 if ffc == 0 else [],
                                     inc="pe" if ffc == 7 else None)
                            tp = P.tok("pe")
                            gt = tb * 4 + tsub
                            mcol = s_masks[:, gt * 4 + e_i:gt * 4 + e_i + 1]
                            dsl = (tsub * C + cb * 512, tsub * C + cb * 512 + 512)
                            if g == NE - 1:
                                P.dve(lambda e, bnk=bnk, xp=xp, dsl=dsl:
                                      e.tensor_tensor(
                                          ytmp[xp][:, dsl[0]:dsl[1]], psum[bnk][:],
                                          s_b2b[:, (dsl[0] % C):(dsl[0] % C) + 512],
                                          op=ALU.add),
                                      waits=[tp, b2tok, mask_tok], drain=True)
                                te = P.dve(lambda e, xp=xp, dsl=dsl, m=mcol:
                                           e.tensor_scalar_mul(
                                               ytmp[xp][:, dsl[0]:dsl[1]],
                                               ytmp[xp][:, dsl[0]:dsl[1]], m))
                            else:
                                te = P.dve(lambda e, bnk=bnk, xp=xp, dsl=dsl, m=mcol:
                                           e.tensor_scalar(
                                               ytmp[xp][:, dsl[0]:dsl[1]],
                                               psum[bnk][:], m, None, op0=ALU.mult),
                                           waits=[tp, mask_tok])
                            bank_done(bnk, te)
                    ty = P.tok("dve")
                    r0 = tb * 512
                    last_acc = P.dma("pool", lambda e, r=r0, xp=xp: e.dma_start(
                        out=d_x1ff[r:r + 512, :].rearrange("(s p) c -> p s c", p=128),
                        in_=ytmp[xp][:].rearrange("p (s c) -> p s c", s=4),
                        accum_op=ALU.add), waits=[ty, x1t_tok])
                gidx += 1
        return last_acc

    def emit_ln2_resid(res_src, res_dst):
        acc_done = P.tok("sw")
        g2_b = ld_bcast_row(wf["ln2_g"], s_cb[1], waits=[P.tok("dve")])
        b2_b = ld_bcast_row(wf["ln2_b"], s_cb[2])
        prev_st = None
        last = None
        for i in range(B * NTILE):
            r0 = i * 128
            p = i % 2
            tld = P.dma("sp", lambda e, r=r0, p=p: e.dma_start(
                out=sx_t[p][:], in_=d_x1ff[r:r + 128, :]),
                waits=[acc_done, g2_b, b2_b] if i == 0 else [P.tok("dve")])
            txl = P.dma("sp", lambda e, r=r0: e.dma_start(
                out=u_t[:], in_=res_src[r:r + 128, :]), waits=[P.tok("dve")])
            mu = s_tiny[:, 0:1]; s2 = s_tiny[:, 1:2]
            l2 = s_tiny[:, 2:3]; rstd = s_tiny[:, 3:4]
            pw = [tld] + ([prev_st] if prev_st else [])
            P.dve(lambda e, p=p: e.reduce_sum(mu, sx_t[p][:], axis=AX.X), waits=pw,
                  drain=True)
            P.dve(lambda e: e.tensor_scalar_mul(mu, mu, float(1.0 / C)), drain=True)
            P.dve(lambda e, p=p: e.tensor_scalar(t_t[:], sx_t[p][:], mu, None,
                                                 op0=ALU.subtract), drain=True)
            tv = P.tok("dve")
            tq = P.act(lambda e: e.activation(w_t[:], t_t[:], AF.Square), waits=[tv])
            tr = P.dve(lambda e: e.reduce_sum(s2, w_t[:], axis=AX.X), waits=[tq],
                       drain=True)
            P.act(lambda e: e.activation(l2, s2, AF.Ln, bias=f32(1e-5),
                                         scale=f32(1.0 / C)), waits=[tr], drain=True)
            P.act(lambda e: e.activation(rstd, l2, AF.Exp, scale=f32(-0.5)),
                  drain=True)
            ta = P.tok("act")
            P.dve(lambda e: e.tensor_scalar_mul(t_t[:], t_t[:], rstd), waits=[ta],
                  drain=True)
            P.dve(lambda e: e.tensor_tensor(t_t[:], t_t[:], s_cb[1][:],
                                            op=ALU.mult), drain=True)
            P.dve(lambda e: e.tensor_tensor(t_t[:], t_t[:], s_cb[2][:],
                                            op=ALU.add), drain=True)
            tdv = P.dve(lambda e: e.scalar_tensor_tensor(
                t_t[:], t_t[:], 0.5, u_t[:], op0=ALU.mult, op1=ALU.add),
                waits=[txl], drain=True)
            prev_st = P.dma("sp", lambda e, r=r0: e.dma_start(
                out=res_dst[r:r + 128, :], in_=t_t[:]), waits=[tdv], inc="st")
            last = prev_st
        return last

    def emit_attn_qkv(be):
        lnw_b = ld_bcast_row(wa["ln_w"], s_cb[0], waits=[P.tok("pe")])
        srcT_tok = P.tok("dve")
        for (col0, dst) in ((0, QTt), (C, KTt)):
            tw = stream_w(wa["w_qkv"], sWs, col0=col0,
                          waits=[P.tok("pe"), P.tok("dve")])
            for i in range(NTILE):
                for fb in range(2):
                    bnk = fb
                    w0 = [bank_wait(bnk)]
                    if i == 0 and fb == 0:
                        w0 += [tw, srcT_tok, lnw_b]
                    for ccc in range(CCH):
                        P.pe(lambda e, i=i, fb=fb, ccc=ccc, bnk=bnk:
                             e.matmul(psum[bnk][:],
                                      srcT[:, ccc * T + i * 128:ccc * T + i * 128 + 128],
                                      sWs[:, ccc * 1024 + fb * 512:
                                          ccc * 1024 + fb * 512 + 512],
                                      start=(ccc == 0), stop=(ccc == CCH - 1)),
                             waits=w0 if ccc == 0 else [],
                             inc="pe" if ccc == CCH - 1 else None)
                    tp = P.tok("pe")
                    te = P.dve(lambda e, fb=fb, bnk=bnk: e.tensor_copy(
                        t_t[:, fb * 512:fb * 512 + 512], psum[bnk][:]),
                        waits=[tp], drain=True)
                    bank_done(bnk, te)
                s2 = s_tiny[:, 1:2]; l2 = s_tiny[:, 2:3]; rstd = s_tiny[:, 3:4]
                tv = P.tok("dve")
                tq = P.act(lambda e: e.activation(u_t[:], t_t[:], AF.Square),
                           waits=[tv])
                tr = P.dve(lambda e: e.reduce_sum(s2, u_t[:], axis=AX.X),
                           waits=[tq], drain=True)
                P.act(lambda e: e.activation(l2, s2, AF.Ln, bias=f32(1e-6),
                                             scale=f32(1.0 / C)), waits=[tr],
                      drain=True)
                P.act(lambda e: e.activation(rstd, l2, AF.Exp, scale=f32(-0.5)),
                      drain=True)
                ta = P.tok("act")
                ppe = P.tok("pe")
                P.dve(lambda e: e.scalar_tensor_tensor(
                    xb_t[:], t_t[:], s_tiny[:, 3:4], s_cb[0][:],
                    op0=ALU.mult, op1=ALU.mult), waits=[ta, ppe], drain=True)
                tv2 = P.tok("dve")
                transpose_chunks(xb_t[:], dst, i * 128, [tv2])
        tw = stream_w(wa["w_qkv"], sWs, col0=2 * C,
                      waits=[P.tok("pe"), P.tok("dve")])
        tmo = P.dve(lambda e: e.memset(
            VA[:].rearrange("p (t h e2) -> p t h e2", t=NTILE, h=H)
            [:, :, :, 64:65], 1.0))
        for i in range(NTILE):
            for fb in range(2):
                bnk = (i * 2 + fb) % 4
                w0 = [bank_wait(bnk)]
                if i == 0 and fb == 0:
                    w0 += [tw, tmo]
                for ccc in range(CCH):
                    P.pe(lambda e, i=i, fb=fb, ccc=ccc, bnk=bnk:
                         e.matmul(psum[bnk][:],
                                  srcT[:, ccc * T + i * 128:ccc * T + i * 128 + 128],
                                  sWs[:, ccc * 1024 + fb * 512:
                                      ccc * 1024 + fb * 512 + 512],
                                  start=(ccc == 0), stop=(ccc == CCH - 1)),
                         waits=w0 if ccc == 0 else [],
                         inc="pe" if ccc == CCH - 1 else None)
                tp = P.tok("pe")
                te = P.dve(lambda e, i=i, fb=fb, bnk=bnk: e.tensor_copy(
                    VA[:].rearrange("p (t h e2) -> p t h e2", t=NTILE, h=H)
                    [:, i, fb * 8:fb * 8 + 8, 0:64],
                    psum[bnk][:].rearrange("p (h d) -> p h d", h=8)),
                    waits=[tp])
                bank_done(bnk, te)
        return P.tok("dve")

    def emit_attn_proj(be):
        tw = stream_w(wa["w_proj"], sWo, waits=[P.tok("pe"), P.tok("dve")])
        attnT_tok = P.tok("dve")
        prev_st = None
        for i in range(NTILE):
            r0 = (be * NTILE + i) * 128
            p = i % 2
            tld = P.dma("sp", lambda e, r=r0, p=p: e.dma_start(
                out=sx_t[p][:], in_=d_resid[r:r + 128, :]), waits=[P.tok("dve")])
            for cb in range(2):
                bnk = cb
                w0 = [bank_wait(bnk)]
                if i == 0 and cb == 0:
                    w0 += [tw, attnT_tok]
                for ccc in range(CCH):
                    P.pe(lambda e, i=i, cb=cb, ccc=ccc, bnk=bnk:
                         e.matmul(psum[bnk][:],
                                  attnT[:, ccc * T + i * 128:ccc * T + i * 128 + 128],
                                  sWo[:, ccc * 1024 + cb * 512:
                                      ccc * 1024 + cb * 512 + 512],
                                  start=(ccc == 0), stop=(ccc == CCH - 1)),
                         waits=w0 if ccc == 0 else [],
                         inc="pe" if ccc == CCH - 1 else None)
                tp = P.tok("pe")
                pw = [tp, tld]
                if prev_st is not None:
                    pw.append(prev_st)
                te = P.dve(lambda e, cb=cb, bnk=bnk, p=p: e.tensor_tensor(
                    t_t[:, cb * 512:cb * 512 + 512], psum[bnk][:],
                    sx_t[p][:, cb * 512:cb * 512 + 512], op=ALU.add), waits=pw,
                    drain=True)
                bank_done(bnk, te)
            tdv = P.tok("dve")
            prev_st = P.dma("sp", lambda e, r=r0: e.dma_start(
                out=d_resid[r:r + 128, :], in_=t_t[:]), waits=[tdv], inc="st")
        return prev_st

    def dbg_dump(slabs=(), va=False, dram=None, masks=False):
        w = [P.tok("pe"), P.tok("dve"), P.tok("act"), P.tok("ld"), P.tok("sw")]
        for idx, s in enumerate(slabs):
            P.dma("sp", lambda e, s=s, i=idx: e.dma_start(
                out=dbg_slab[i], in_=s[:]), waits=w, inc="st")
        if va:
            P.dma("sp", lambda e: e.dma_start(out=dbg_va[:], in_=VA[:]),
                  waits=w, inc="st")
        if dram is not None:
            P.dma("sp", lambda e, d=dram: e.dma_start(out=dbg_f[:], in_=d[:]),
                  waits=w, inc="st")
        if masks:
            P.dma("sp", lambda e: e.dma_start(out=dbg_m[:], in_=s_masks[:]),
                  waits=w, inc="st")
        P.emit("sp", lambda e: e.nop(), waits=[P.tok("st")])
        P.realize()
        return nc

    P.dma("sp", lambda e: e.dma_start(out=s_ident[:], in_=ident_dr[:]))
    P.dma("sp", lambda e: e.dma_start(out=s_cmask[:], in_=cmask_dr[:]))
    ic = P.tok("ld")
    P.dve(lambda e: e.memset(s_ones[:], 1.0), waits=[ic])

    def ffn_call(res_src, res_dst, first_wait, tag=""):
        fw = first_wait
        for be in range(B):
            r0 = emit_lnp_src(res_src, be, spill=True, first_wait=fw)
            if upto == "f1_t0":
                return nc
            fw = None
            if upto == tag + "lnp":
                return dbg_dump(slabs=[srcT], dram=d_src)
            emit_ffn_qkv(be)
            if upto == tag + "qkv":
                return dbg_dump(slabs=[QTt, KTt], va=True)
            for h in range(H):
                stg = emit_stage_head(h, h % 2)
                emit_flash_head(h, h % 2, stg, local=False)
            if upto == tag + "flash":
                return dbg_dump(slabs=[attnT])
            emit_ffn_post(be)
            if upto == tag + "post":
                return dbg_dump(dram=d_x1ff, masks=True)
        if upto == tag + "post4":
            return dbg_dump(dram=d_x1ff, masks=True)
        emit_moe()
        if upto == tag + "moe":
            return dbg_dump(dram=d_x1ff)
        r = emit_ln2_resid(res_src, res_dst)
        if upto == tag + "ln2":
            return dbg_dump(dram=res_dst)
        return r

    def attn_call(first_wait):
        fw = first_wait
        for be in range(B):
            emit_lnp_src(d_resid, be, spill=False, first_wait=fw)
            fw = None
            if upto == "a_lnp":
                return dbg_dump(slabs=[srcT])
            emit_attn_qkv(be)
            if upto == "a_qkv":
                return dbg_dump(slabs=[QTt, KTt], va=True)
            flash_done = {0: None, 1: None}
            for h in range(H):
                stg = emit_stage_head(h, h % 2, prev=flash_done[h % 2])
                rr = emit_flash_head(h, h % 2, stg, local=True)
                flash_done[h % 2] = P.tok("pe")
                if rr is None:
                    return nc
            if upto == "a_flash":
                return dbg_dump(slabs=[attnT])
            emit_attn_proj(be)
        if upto == "a_proj":
            return dbg_dump(dram=d_resid)

    if upto is not None and upto == "aonly":
        import os as _os
        nheads = int(_os.environ.get("DBG_HEADS", "3"))
        tcp0 = P.dma("sp", lambda e: e.dma_start(out=d_resid[:], in_=xin[:]))
        emit_lnp_src(d_resid, 0, spill=False, first_wait=tcp0)
        emit_attn_qkv(0)
        for h in range(nheads):
            stg = emit_stage_head(h, h % 2)
            rr = emit_flash_head(h, h % 2, stg, local=True)
            if rr is None:
                return nc
        return dbg_dump(slabs=[attnT, QTt, KTt], va=True)

    r = ffn_call(xin, d_resid, P.tok("dve"), tag="f1_")
    if upto is not None and upto.startswith("f1_"):
        return r
    r = attn_call(P.tok("st"))
    if upto is not None and upto.startswith("a_"):
        return r
    r = ffn_call(d_resid, out, P.tok("st"), tag="f2_")
    if upto is not None and upto.startswith("f2_"):
        return r

    P.emit("sp", lambda e: e.nop(), waits=[P.tok("st"), P.tok("sw"), P.tok("ld")])
    P.realize()
    return nc


_NC_CACHE = {}


def kernel(**inputs):
    n = 8
    x = np.ascontiguousarray(np.asarray(inputs["L0_attn_x"], dtype=np.float32))
    shards = [np.ascontiguousarray(x[i * B:(i + 1) * B].reshape(NTOK, C))
              for i in range(n)]

    def g(name):
        return np.ascontiguousarray(np.asarray(inputs[name], dtype=np.float32))

    base = {}
    base["a_w_qkv"] = g("L0_attn_w_qkv")
    base["a_w_proj"] = g("L0_attn_w_proj")
    base["a_ln_w"] = g("L0_attn_ln_w").reshape(1, C)
    for nm in ("Wq", "Wk", "Wv", "Wo"):
        base["f_" + nm] = g("L0_ffn_" + nm)
    for nm in ("bv", "bo", "ln1_g", "ln1_b", "ln2_g", "ln2_b"):
        base["f_" + nm] = g("L0_ffn_" + nm).reshape(1, C)
    base["f_bq"] = np.ascontiguousarray(
        g("L0_ffn_bq").reshape(8, 128).T)
    base["f_bk"] = np.ascontiguousarray(
        g("L0_ffn_bk").reshape(8, 128).T)
    base["f_b1"] = np.ascontiguousarray(
        g("L0_ffn_b1").reshape(NE, 32, 128).transpose(0, 2, 1))
    base["f_router_w"] = g("L0_ffn_router_w").reshape(1, C)
    base["f_router_b"] = g("L0_ffn_router_b").reshape(1, 1)
    base["f_W1"] = g("L0_ffn_W1")
    base["f_W2"] = g("L0_ffn_W2")
    base["f_b2"] = g("L0_ffn_b2")

    if "nc" not in _NC_CACHE:
        _NC_CACHE["nc"] = build_program()
    nc = _NC_CACHE["nc"]
    in_maps = [dict(base, x=s) for s in shards]
    res = run_bass_kernel_spmd(nc, in_maps, core_ids=list(range(n)))
    outs = [np.asarray(res.results[i]["out"]).reshape(B, T, C) for i in range(n)]
    return np.concatenate(outs, axis=0).astype(np.float32)


# version 03
# speedup vs baseline: 1.0429x; 1.0429x over previous
import numpy as np
import ml_dtypes
import concourse.bass as bass
import concourse.mybir as mybir
from concourse.bass_utils import run_bass_kernel_spmd

F32 = mybir.dt.float32
BF16 = mybir.dt.bfloat16
I32 = mybir.dt.int32
AF = mybir.ActivationFunctionType
ALU = mybir.AluOpType
AX = mybir.AxisListType
f32 = np.float32

B = 4
T = 2048
C = 1024
H = 16
CCH = 8
NTILE = 16
DFF = 4096
NE = 4
LS = 256
NTOK = B * T

SBUF_BASE = 16384 + 768
SBUF_LIMIT = 16384 + 207 * 1024

ENGS = ("pe", "act", "dve", "pool", "sp")


class Prog:
    def __init__(self, nc):
        self.nc = nc
        self.ops = []
        self.count = {}
        self.sems = {}
        self.bank_tok = {}

    def sem(self, name):
        if name not in self.sems:
            self.sems[name] = self.nc.alloc_semaphore(name)
            self.count[name] = 0
        return self.sems[name]

    def emit(self, eng, fn, waits=(), inc=None, inc_n=1, drain=False):
        ws = []
        for w in waits:
            if w is None:
                continue
            s, v = w
            self.sem(s)
            assert v <= self.count[s], f"future wait {s}:{v}>{self.count[s]}"
            ws.append((s, v))
        self.ops.append((eng, fn, ws, inc, inc_n, drain))
        if inc is not None:
            self.sem(inc)
            self.count[inc] += inc_n
            return (inc, self.count[inc])
        return None

    def tok(self, sem):
        self.sem(sem)
        return (sem, self.count[sem])

    def pe(self, fn, waits=(), inc=None):
        return self.emit("pe", fn, waits, inc)

    def act(self, fn, waits=(), inc="act", drain=False):
        return self.emit("act", fn, waits, inc, 1, drain)

    def dve(self, fn, waits=(), inc="dve", drain=False):
        return self.emit("dve", fn, waits, inc, 1, drain)

    def dma(self, q, fn, waits=(), inc=None):
        inc = inc or ("ld" if q == "sp" else "sw")
        return self.emit(q, fn, waits, inc, 16)

    def realize(self):
        nc = self.nc
        last_wait = {}
        with nc.Block() as blk:
            def run_engine(e_name):
                def body(handle):
                    for (eng, fn, ws, inc, inc_n, drain) in self.ops:
                        if eng != e_name:
                            continue
                        for (s, v) in ws:
                            key = (e_name, s)
                            if last_wait.get(key, -1) >= v:
                                continue
                            handle.wait_ge(self.sems[s], v)
                            last_wait[key] = v
                        instr = fn(handle)
                        if inc is not None:
                            instr.then_inc(self.sems[inc], inc_n)
                        if drain:
                            handle.drain()
                return body
            blk.sync(run_engine("sp"))
            blk.gpsimd(run_engine("pool"))
            blk.tensor(run_engine("pe"))
            blk.scalar(run_engine("act"))
            blk.vector(run_engine("dve"))


def build_program(upto=None):
    nc = bass.Bass("TRN2", target_bir_lowering=False,
                   detect_race_conditions=False)

    xin = nc.dram_tensor("x", [NTOK, C], F32, kind="ExternalInput")
    out = nc.dram_tensor("out", [NTOK, C], F32, kind="ExternalOutput")
    wa, wf = {}, {}
    for n, shp in (("w_qkv", [C, 3 * C]), ("ln_w", [1, C]), ("w_proj", [C, C])):
        wa[n] = nc.dram_tensor("a_" + n, shp, F32, kind="ExternalInput")
    ffshapes = (("Wq", [C, C]), ("bq", [128, 8]), ("Wk", [C, C]), ("bk", [128, 8]),
                ("Wv", [C, C]), ("bv", [1, C]), ("Wo", [C, C]), ("bo", [1, C]),
                ("router_w", [1, C]), ("router_b", [1, 1]),
                ("W1", [NE, C, DFF]), ("b1", [NE, 128, 32]),
                ("W2", [NE, DFF, C]), ("b2", [NE, C]),
                ("ln1_g", [1, C]), ("ln1_b", [1, C]),
                ("ln2_g", [1, C]), ("ln2_b", [1, C]))
    for n, shp in ffshapes:
        wf[n] = nc.dram_tensor("f_" + n, shp, F32, kind="ExternalInput")

    d_resid = nc.dram_tensor("d_resid", [NTOK, C], F32, kind="Internal")
    d_src = nc.dram_tensor("d_src", [NTOK, C], F32, kind="Internal")
    d_x1ff = nc.dram_tensor("d_x1ff", [NTOK, C], F32, kind="Internal")
    d_x1T = nc.dram_tensor("d_x1T", [B, CCH, 128, T], BF16, kind="Internal")
    if upto is not None:
        dbg_slab = nc.dram_tensor("dbg_slab", [3, 128, CCH * T], BF16,
                                  kind="ExternalOutput")
        dbg_va = nc.dram_tensor("dbg_va", [128, NTILE * H * 65], BF16,
                                kind="ExternalOutput")
        dbg_f = nc.dram_tensor("dbg_f", [NTOK, C], F32, kind="ExternalOutput")
        dbg_m = nc.dram_tensor("dbg_m", [128, 256], F32, kind="ExternalOutput")

    ident_dr = nc.inline_tensor(np.eye(128, dtype=ml_dtypes.bfloat16),
                                name="ident_c")
    kk = np.arange(128)[:, None]
    qq = np.arange(LS)[None, :]
    cmask_np = np.concatenate([(kk <= qq), (kk + 128 <= qq)], axis=1)
    cmask_dr = nc.inline_tensor(
        np.ascontiguousarray(cmask_np.astype(ml_dtypes.bfloat16)), name="cmask_c")

    alloc_log = []

    def at(name, shape, dtype, off):
        per = int(np.prod(shape[1:])) * mybir.dt.size(dtype)
        assert SBUF_BASE <= off and off + per <= SBUF_LIMIT, \
            f"sbuf overflow {name}: [{off}, {off + per})"
        alloc_log.append((name, off, off + per))
        return nc.alloc_sbuf_tensor_at(name, list(shape), dtype, offset=off)

    KB = 1024
    O_PERS = SBUF_BASE
    s_ident = at("s_ident", [128, 128], BF16, O_PERS)
    s_cmask = at("s_cmask", [128, 512], BF16, O_PERS + 256)
    s_ones = at("s_ones", [128, 128], BF16, O_PERS + 1280)
    s_cols = at("s_cols", [128, 64], F32, O_PERS + 1536)
    s_tiny = at("s_tiny", [128, 32], F32, O_PERS + 1792)
    O_S0 = O_PERS + 2 * KB
    O_S1 = O_S0 + 32 * KB
    O_S2 = O_S1 + 32 * KB
    O_S3 = O_S2 + 32 * KB
    O_S4 = O_S3 + 34 * KB
    O_S5 = O_S4 + 29952
    O_S6 = O_S5 + 23 * KB
    O_END = O_S6 + 22 * KB
    assert O_END <= SBUF_LIMIT, O_END

    srcT = at("srcT", [128, CCH * T], BF16, O_S0)
    attnT = at("attnT", [128, CCH * T], BF16, O_S0)
    QTt = at("QTt", [128, CCH * T], BF16, O_S1)
    sW1g = [at("sW1g0", [128, CCH * 1024], BF16, O_S1),
            at("sW1g1", [128, CCH * 1024], BF16, O_S2)]
    sW2g = [at("sW2g0", [128, 8 * 1024], BF16, O_S1 + 16 * KB),
            at("sW2g1", [128, 8 * 1024], BF16, O_S2 + 16 * KB)]
    sWo = at("sWo", [128, CCH * 1024], BF16, O_S1)
    KTt = at("KTt", [128, CCH * T], BF16, O_S2)
    x1Tstg = at("x1Tstg", [128, CCH * T], BF16, O_S2)
    VA = at("VA", [128, NTILE * H * 65], BF16, O_S3)
    sxt = [at("sxt0", [128, CCH * 512], BF16, O_S3),
           at("sxt1", [128, CCH * 512], BF16, O_S3 + 8 * KB)]
    s_b2b = at("s_b2b", [128, C], F32, O_S3 + 16 * KB)
    h_sb = at("h_sb", [128, 8 * 512], BF16, O_S3 + 20 * KB)
    s_cb = [at(f"s_cb{i}", [128, C], F32, O_S3 + i * 4 * KB) for i in range(4)]
    sWs = at("sWs", [128, CCH * 1024], BF16, O_S4)
    rrow = at("rrow", [1, 512], F32, O_S4)
    sqh = at("sqh", [64, 512], BF16, O_S4 + 16 * KB)
    lnq2 = at("lnq2", [1, T], BF16, O_S4 + 17 * KB)
    k2r = at("k2r", [1, T], BF16, O_S4 + 21 * KB)
    b8r = at("b8r", [1, T], BF16, O_S4 + 25 * KB)
    s_kv = at("s_kv", [1, 8], F32, O_S4 + 29 * KB)
    ytmp = [at("ytmp0", [128, 4 * C], F32, O_S4),
            at("ytmp1", [128, 4 * C], F32, O_S4 + 16 * KB)]
    stgQ = [at("stgQ0", [65, T], BF16, O_S5),
            at("stgQ1", [65, T], BF16, O_S5 + 4 * KB)]
    stgK = [at("stgK0", [65, T], BF16, O_S5 + 8 * KB),
            at("stgK1", [65, T], BF16, O_S5 + 12 * KB)]
    PTb = [at("PTb0", [128, 512], BF16, O_S5 + 16 * KB),
           at("PTb1", [128, 512], BF16, O_S5 + 17 * KB)]
    rb_sb = at("rb_sb", [64, 512], BF16, O_S5 + 18 * KB)
    rowb = at("rowb", [1, 512], BF16, O_S5 + 19 * KB)
    bv_bf = at("bv_bf", [1, C], BF16, O_S5 + 20 * KB)
    s_masks = at("s_masks", [128, 256], F32, O_S5 + 22 * KB)
    sx_t = [at("sx_t0", [128, C], F32, O_S6),
            at("sx_t1", [128, C], F32, O_S6 + 4 * KB)]
    t_t = at("t_t", [128, C], F32, O_S6 + 8 * KB)
    u_t = at("u_t", [128, C], F32, O_S6 + 12 * KB)
    w_t = at("w_t", [128, C], F32, O_S6 + 16 * KB)
    xb_t = at("xb_t", [128, C], BF16, O_S6 + 20 * KB)
    t_t2 = at("t_t2", [128, C], F32, O_S4)
    u_t2 = at("u_t2", [128, C], F32, O_S4 + 4 * KB)
    w_t2 = at("w_t2", [128, C], F32, O_S4 + 8 * KB)
    xb_t2 = at("xb_t2", [128, C], BF16, O_S4 + 12 * KB)
    qt_t2 = at("qt_t2", [128, C], F32, O_S4 + 17 * KB)
    qu_t2 = at("qu_t2", [128, C], F32, O_S4 + 21 * KB)
    qxb_t2 = at("qxb_t2", [128, C], BF16, O_S4 + 25 * KB)
    rb_f32 = at("rb_f32", [64, 512], F32, O_S4 + 14 * KB)
    SCR = [dict(t=t_t, u=u_t, w=w_t, xb=xb_t),
           dict(t=t_t2, u=u_t2, w=w_t2, xb=xb_t2)]
    QSCR = [dict(t=t_t, u=u_t, xb=xb_t),
            dict(t=qt_t2, u=qu_t2, xb=qxb_t2)]

    psum = [nc.alloc_psum_tensor(f"pb{i}", [128, 512], F32) for i in range(8)]

    P = Prog(nc)
    BT = P.bank_tok

    def bank_wait(b):
        return BT.get(b)

    def bank_done(b, tok):
        BT[b] = tok

    def ld_bcast_row(dram_row, dest, waits=()):
        n = dram_row.shape[-1]
        return P.dma("sp", lambda e, d=dram_row, o=dest, n=n: e.dma_start(
            out=o[:, 0:n], in_=d[0:1, 0:n].broadcast_to([128, n])), waits=waits)

    def ln_cols(x_ap, eps, waits=()):
        mu = s_tiny[:, 0:1]
        s2 = s_tiny[:, 8:9]
        l2 = s_tiny[:, 16:17]
        rstd = s_tiny[:, 24:25]
        P.dve(lambda e: e.reduce_sum(mu, x_ap, axis=AX.X), waits=list(waits),
              drain=True)
        P.dve(lambda e: e.tensor_scalar_mul(mu, mu, float(1.0 / C)), drain=True)
        P.dve(lambda e: e.tensor_scalar(t_t[:], x_ap, mu, None, op0=ALU.subtract),
              drain=True)
        tv = P.tok("dve")
        tq = P.act(lambda e: e.activation(u_t[:], t_t[:], AF.Square), waits=[tv])
        tr = P.dve(lambda e: e.reduce_sum(s2, u_t[:], axis=AX.X), waits=[tq])
        P.act(lambda e: e.activation(l2, s2, AF.Ln, bias=f32(eps),
                                     scale=f32(1.0 / C)), waits=[tr], drain=True)
        P.act(lambda e: e.activation(rstd, l2, AF.Exp, scale=f32(-0.5)),
              drain=True)
        return P.tok("act")

    def transpose_chunks(src_bf, dst, dst_col0, waits):
        for cc in range(CCH):
            b = 6 + (cc % 2)
            pv = psum[b][:, 0:128].bitcast(BF16)[:, 0:128]
            w = [bank_wait(b)] + list(waits)
            tk = P.pe(lambda e, pv=pv, cc=cc, s=src_bf: e.transpose(
                pv, s[:, cc * 128:(cc + 1) * 128], s_ident[:]),
                waits=w, inc="pe")
            tc = P.dve(lambda e, pv=pv, c=dst_col0 + cc * T, d=dst:
                       e.tensor_copy(d[:, c:c + 128], pv), waits=[tk])
            bank_done(b, tc)
        return P.tok("dve")

    def stream_w(dram_w, dest, ncols=1024, col0=0, waits=()):
        return P.dma("pool", lambda e: e.dma_start(
            out=dest[:, 0:CCH * ncols].rearrange("p (c f) -> p c f", c=CCH),
            in_=dram_w[:, col0:col0 + ncols].rearrange("(c p) f -> p c f", p=128)),
            waits=waits)

    def emit_lnp_src(src_dram, be, spill, first_wait=None):
        backs = []

        def emit_back(st):
            par = st["par"]
            S = SCR[par]
            co = 16 * par
            rstd = s_tiny[:, co + 3:co + 4]
            P.dve(lambda e, S=S, rstd=rstd: e.tensor_scalar_mul(
                S["xb"][:], S["t"][:], rstd), waits=[st["ta"]])
            if spill:
                tsp = P.dve(lambda e, S=S, rstd=rstd: e.tensor_scalar_mul(
                    S["w"][:], S["t"][:], rstd))
                P.dma("sp", lambda e, r=st["r0"], S=S: e.dma_start(
                    out=d_src[r:r + 128, :], in_=S["w"][:]), waits=[tsp])
            tv = P.tok("dve")
            transpose_chunks(S["xb"][:], srcT, st["i"] * 128, [tv])

        for i in range(NTILE):
            r0 = (be * NTILE + i) * 128
            par = i % 2
            S = SCR[par]
            co = 16 * par
            mu = s_tiny[:, co + 0:co + 1]
            s2 = s_tiny[:, co + 1:co + 2]
            l2 = s_tiny[:, co + 2:co + 3]
            rstd = s_tiny[:, co + 3:co + 4]
            lw = [P.tok("dve")]
            if first_wait is not None and i == 0:
                lw.append(first_wait)
            tld = P.dma("sp", lambda e, r=r0, p=par: e.dma_start(
                out=sx_t[p][:], in_=src_dram[r:r + 128, :]), waits=lw)
            P.dve(lambda e, p=par, mu=mu: e.reduce_sum(mu, sx_t[p][:], axis=AX.X),
                  waits=[tld], drain=True)
            P.dve(lambda e, mu=mu: e.tensor_scalar_mul(mu, mu, float(1.0 / C)),
                  drain=True)
            P.dve(lambda e, p=par, S=S, mu=mu: e.tensor_scalar(
                S["t"][:], sx_t[p][:], mu, None, op0=ALU.subtract), drain=True)
            tv = P.tok("dve")
            tq = P.act(lambda e, S=S: e.activation(S["u"][:], S["t"][:],
                                                   AF.Square), waits=[tv])
            if backs:
                emit_back(backs.pop(0))
            tr = P.dve(lambda e, S=S, s2=s2: e.reduce_sum(
                s2, S["u"][:], axis=AX.X), waits=[tq], drain=True)
            P.act(lambda e, s2=s2, l2=l2: e.activation(
                l2, s2, AF.Ln, bias=f32(1e-5), scale=f32(1.0 / C)),
                waits=[tr], drain=True)
            P.act(lambda e, l2=l2, rstd=rstd: e.activation(
                rstd, l2, AF.Exp, scale=f32(-0.5)), drain=True)
            ta = P.tok("act")
            backs.append(dict(i=i, par=par, ta=ta, r0=r0))
        while backs:
            emit_back(backs.pop(0))
        return P.tok("dve")

    def emit_ffn_qkv(be):
        pe0 = P.tok("pe")
        bvl = P.dma("pool", lambda e: e.dma_start(out=bv_bf[:], in_=wf["bv"][:]),
                    waits=[pe0])
        c0 = P.dma("sp", lambda e: e.dma_start(
            out=s_cols[:, 0:8], in_=wf["bq"][:]), waits=[pe0])
        c1 = P.dma("sp", lambda e: e.dma_start(
            out=s_cols[:, 8:16], in_=wf["bk"][:]), waits=[pe0])
        srcT_tok = P.tok("dve")
        for (wname, dst, b0) in (("Wq", QTt, 0), ("Wk", KTt, 8)):
            tw = stream_w(wf[wname], sWs, waits=[P.tok("pe")])
            for fc in range(CCH):
                for tb in range(4):
                    bnk = (fc * 4 + tb) % 4
                    w0 = [bank_wait(bnk)]
                    if fc == 0 and tb == 0:
                        w0 += [tw, srcT_tok, c0, c1]
                    for cc in range(CCH):
                        P.pe(lambda e, fc=fc, tb=tb, cc=cc, bnk=bnk:
                             e.matmul(psum[bnk][:],
                                      sWs[:, cc * 1024 + fc * 128:
                                          cc * 1024 + fc * 128 + 128],
                                      srcT[:, cc * T + tb * 512:
                                           cc * T + tb * 512 + 512],
                                      start=(cc == 0), stop=(cc == CCH - 1)),
                             waits=w0 if cc == 0 else [],
                             inc="pe" if cc == CCH - 1 else None)
                    tp = P.tok("pe")
                    te = P.dve(lambda e, fc=fc, tb=tb, bnk=bnk, dst=dst, b0=b0:
                               e.tensor_scalar(
                                   dst[:, fc * T + tb * 512:fc * T + tb * 512 + 512],
                                   psum[bnk][:], s_cols[:, b0 + fc:b0 + fc + 1],
                                   None, op0=ALU.add), waits=[tp])
                    bank_done(bnk, te)
        tw = stream_w(wf["Wv"], sWs, waits=[P.tok("pe")])
        tmo = P.dve(lambda e: e.memset(
            VA[:].rearrange("p (t h e2) -> p t h e2", t=NTILE, h=H)
            [:, :, :, 64:65], 1.0))
        for i in range(NTILE):
            for fb in range(2):
                bnk = (i * 2 + fb) % 4
                w0 = [bank_wait(bnk)]
                if i == 0 and fb == 0:
                    w0 += [tw, tmo, bvl]
                for cc in range(CCH):
                    P.pe(lambda e, i=i, fb=fb, cc=cc, bnk=bnk:
                         e.matmul(psum[bnk][:],
                                  srcT[:, cc * T + i * 128:cc * T + i * 128 + 128],
                                  sWs[:, cc * 1024 + fb * 512:
                                      cc * 1024 + fb * 512 + 512],
                                  start=(cc == 0), stop=False),
                         waits=w0 if cc == 0 else [],
                         inc=None)
                P.pe(lambda e, fb=fb, bnk=bnk:
                     e.matmul(psum[bnk][:], s_ones[0:1, 0:128],
                              bv_bf[0:1, fb * 512:fb * 512 + 512],
                              start=False, stop=True), inc="pe")
                tp = P.tok("pe")
                te = P.dve(lambda e, i=i, fb=fb, bnk=bnk: e.tensor_copy(
                    VA[:].rearrange("p (t h e2) -> p t h e2", t=NTILE, h=H)
                    [:, i, fb * 8:fb * 8 + 8, 0:64],
                    psum[bnk][:].rearrange("p (h d) -> p h d", h=8)),
                    waits=[tp])
                bank_done(bnk, te)
        return P.tok("dve")

    def emit_stage_head(h, par, shift=20.0, prev=None):
        cc = h // 2
        half = 64 * (h % 2)
        if prev is None:
            prev = P.tok("pe")
        P.dve(lambda e: e.tensor_copy(
            stgQ[par][0:64, :], QTt[half:half + 64, cc * T:(cc + 1) * T]),
            waits=[prev])
        P.dve(lambda e: e.tensor_copy(
            stgK[par][0:64, :], KTt[half:half + 64, cc * T:(cc + 1) * T]))
        P.dve(lambda e: e.memset(stgK[par][64:65, :], 1.0))
        tcp = P.tok("dve")
        for (stg, dstrow, ln_out) in ((stgQ, lnq2, True), (stgK, k2r, False)):
            for jq in range(4):
                tsq = P.act(lambda e, stg=stg, jq=jq: e.activation(
                    sqh[0:64, :], stg[par][0:64, jq * 512:(jq + 1) * 512],
                    AF.Square), waits=[tcp, P.tok("pe")])
                pm = P.pe(lambda e: e.matmul(
                    psum[4][0:1, 0:512], s_ones[0:64, 0:1], sqh[0:64, :],
                    start=True, stop=True),
                    waits=[tsq, bank_wait(4)], inc="pe")
                if ln_out:
                    tc = P.act(lambda e, jq=jq: e.activation(
                        lnq2[0:1, jq * 512:(jq + 1) * 512], psum[4][0:1, 0:512],
                        AF.Ln), waits=[pm], drain=(jq == 3))
                else:
                    tc = P.act(lambda e, jq=jq: e.activation(
                        k2r[0:1, jq * 512:(jq + 1) * 512], psum[4][0:1, 0:512],
                        AF.Copy, bias=0.0), waits=[pm], drain=(jq == 3))
                bank_done(4, tc)
        tk2 = P.tok("act")
        k2m = s_kv[0:1, 0:1]
        hlf = s_kv[0:1, 1:2]
        P.dve(lambda e: e.reduce_max(k2m, k2r[:], axis=AX.X), waits=[tk2], drain=True)
        tvm = P.tok("dve")
        P.act(lambda e: e.activation(hlf, k2m, AF.Ln), waits=[tvm], drain=True)
        ta = P.tok("act")
        P.dve(lambda e: e.tensor_scalar_mul(hlf, hlf, float(0.5)), waits=[ta], drain=True)
        tvh = P.tok("dve")
        P.act(lambda e: e.activation(b8r[:], lnq2[:], AF.Exp, scale=f32(0.5),
                                     bias=hlf), waits=[tvh], drain=True)
        tb8 = P.tok("act")
        P.dve(lambda e: e.tensor_scalar(
            stgQ[par][64:65, :], b8r[:], float(-1.0), float(8.0 * shift),
            op0=ALU.mult, op1=ALU.add), waits=[tb8], drain=True)
        return P.tok("dve")

    def emit_flash_head(h, par, stg_tok, local):
        cc = h // 2
        half = 64 * (h % 2)
        if local:
            units = [(g, g * LS, LS, [g * 2, g * 2 + 1]) for g in range(8)]
        else:
            units = [(qb, qb * 512, 512, list(range(16))) for qb in range(4)]

        def make_evac(ob, qlo, N, tpv):
            def emit_ev():
                tvr = P.dve(lambda e, ob=ob, N=N: e.tensor_copy(
                    rowb[0:1, 0:N], psum[ob][64:65, 0:N]), waits=[tpv], drain=True)
                trb = P.pe(lambda e, N=N: e.matmul(
                    psum[5][0:64, 0:N], s_ones[0:1, 0:64], rowb[0:1, 0:N],
                    start=True, stop=True), waits=[tvr, bank_wait(5)], inc="pe")
                tcb = P.dve(lambda e, N=N: e.reciprocal(
                    rb_f32[:, 0:N], psum[5][0:64, 0:N]), waits=[trb], drain=True)
                bank_done(5, tcb)
                ev = P.dve(lambda e, ob=ob, N=N, qlo=qlo: e.tensor_tensor(
                    attnT[half:half + 64, cc * T + qlo:cc * T + qlo + N],
                    psum[ob][0:64, 0:N], rb_f32[:, 0:N], op=ALU.mult),
                    waits=[tcb])
                bank_done(ob, ev)
            return emit_ev

        pending = None
        for (u, qlo, N, kcs) in units:
            ob = 2 + (u % 2)
            exp_toks = {}
            pv_toks = {}
            for j, kc in enumerate(kcs):
                sb = j % 2
                stw = [bank_wait(sb)]
                if j == 0:
                    stw.append(stg_tok)
                st_tok = P.pe(
                    lambda e, kc=kc, sb=sb, qlo=qlo, N=N: e.matmul(
                        psum[sb][0:128, 0:N],
                        stgK[par][:, kc * 128:(kc + 1) * 128],
                        stgQ[par][:, qlo:qlo + N],
                        start=True, stop=True),
                    waits=stw, inc="pe")
                ew = [st_tok]
                if j >= 2:
                    ew.append(pv_toks[kcs[j - 2]])
                tE = P.act(lambda e, sb=sb, N=N: e.activation(
                    PTb[sb][:, 0:N], psum[sb][0:128, 0:N], AF.Exp,
                    scale=f32(0.125)), waits=ew)
                bank_done(sb, tE)
                if local:
                    tE = P.dve(lambda e, sb=sb, j=j, N=N: e.tensor_tensor(
                        PTb[sb][:, 0:N], PTb[sb][:, 0:N],
                        s_cmask[:, j * 256:j * 256 + N], op=ALU.mult),
                        waits=[tE])
                exp_toks[kc] = tE
                pw = [tE]
                if j == 0:
                    pw.append(bank_wait(ob))
                pv_toks[kc] = P.pe(
                    lambda e, kc=kc, sb=sb, ob=ob, N=N,
                    st_=(kc == kcs[0]), sp_=(kc == kcs[-1]): e.matmul(
                        psum[ob][0:65, 0:N],
                        VA[:, kc * (H * 65) + h * 65:kc * (H * 65) + h * 65 + 65],
                        PTb[sb][:, 0:N],
                        start=st_, stop=sp_),
                    waits=pw, inc="pe")
            tpv = P.tok("pe")
            if pending is not None:
                pending()
            pending = make_evac(ob, qlo, N, tpv)
        pending()
        return P.tok("dve")

    def emit_ffn_post(be):
        tw = stream_w(wf["Wo"], sWo, waits=[P.tok("pe"), P.tok("dve")])
        bo_b = ld_bcast_row(wf["bo"], s_cb[0], waits=[P.tok("pe")])
        g1_b = ld_bcast_row(wf["ln1_g"], s_cb[1])
        b1_b = ld_bcast_row(wf["ln1_b"], s_cb[2])
        rw_b = ld_bcast_row(wf["router_w"], s_cb[3])
        rbc = P.dma("sp", lambda e: e.dma_start(
            out=s_cols[:, 48:49],
            in_=wf["router_b"][0:1, 0:1].broadcast_to([128, 1])))
        attnT_tok = P.tok("dve")
        prev_st = [None, None]
        backs = []

        def emit_back(st):
            par = st["par"]
            S = SCR[par]
            co = 16 * par
            rstd = s_tiny[:, co + 3:co + 4]
            keys = s_tiny[:, co + 4:co + 5]
            kq = s_tiny[:, co + 5:co + 6]; fi = s_tiny[:, co + 6:co + 7]
            fc_ = s_tiny[:, co + 7:co + 8]; km = s_tiny[:, co + 8:co + 9]
            ii = s_tiny[:, co + 9:co + 10].bitcast(I32)[:, 0:1]
            a1 = s_tiny[:, co + 10:co + 11]; a2 = s_tiny[:, co + 11:co + 12]
            a3 = s_tiny[:, co + 12:co + 13]
            gt = st["gt"]
            P.dve(lambda e, S=S: e.tensor_scalar_mul(S["u"][:], S["u"][:], rstd),
                  waits=[st["ta"]], drain=True)
            P.dve(lambda e, S=S: e.tensor_tensor(S["u"][:], S["u"][:], s_cb[1][:],
                                                 op=ALU.mult), drain=True)
            P.dve(lambda e, S=S: e.tensor_tensor(S["w"][:], S["u"][:], s_cb[2][:],
                                                 op=ALU.add), drain=True)
            P.dve(lambda e, S=S: e.tensor_tensor(S["u"][:], S["w"][:], s_cb[3][:],
                                                 op=ALU.mult), drain=True)
            P.dve(lambda e, S=S: e.reduce_sum(keys, S["u"][:], axis=AX.X),
                  drain=True)
            P.dve(lambda e: e.tensor_scalar_add(keys, keys, s_cols[:, 48:49]),
                  drain=True)
            P.dve(lambda e: e.tensor_scalar_mul(kq, keys, float(0.25)), drain=True)
            P.dve(lambda e: e.tensor_copy(ii, kq), drain=True)
            P.dve(lambda e: e.tensor_copy(fi, ii), drain=True)
            P.dve(lambda e: e.tensor_tensor(fc_, fi, kq, op=ALU.is_gt), drain=True)
            P.dve(lambda e: e.tensor_sub(fi, fi, fc_), drain=True)
            P.dve(lambda e: e.tensor_scalar(fi, fi, float(-4.0), None,
                                            op0=ALU.mult), drain=True)
            P.dve(lambda e: e.tensor_add(km, keys, fi), drain=True)
            P.dve(lambda e: e.tensor_single_scalar(a1, km, float(1.0),
                                                   op=ALU.is_ge), drain=True)
            P.dve(lambda e: e.tensor_single_scalar(a2, km, float(2.0),
                                                   op=ALU.is_ge), drain=True)
            P.dve(lambda e: e.tensor_single_scalar(a3, km, float(3.0),
                                                   op=ALU.is_ge), drain=True)
            P.dve(lambda e, g=gt: e.tensor_scalar(
                s_masks[:, g * 4:g * 4 + 1], a1, float(-1.0), float(1.0),
                op0=ALU.mult, op1=ALU.add), drain=True)
            P.dve(lambda e, g=gt: e.tensor_sub(
                s_masks[:, g * 4 + 1:g * 4 + 2], a1, a2), drain=True)
            P.dve(lambda e, g=gt: e.tensor_sub(
                s_masks[:, g * 4 + 2:g * 4 + 3], a2, a3), drain=True)
            P.dve(lambda e, g=gt: e.tensor_copy(
                s_masks[:, g * 4 + 3:g * 4 + 4], a3), drain=True)
            tx = P.tok("dve")
            prev_st[par] = P.dma("sp", lambda e, r=st["r0"], S=S: e.dma_start(
                out=d_x1ff[r:r + 128, :], in_=S["w"][:]), waits=[tx])
            P.dve(lambda e, S=S: e.tensor_copy(S["xb"][:], S["w"][:]), drain=True)
            tv2 = P.tok("dve")
            transpose_chunks(S["xb"][:], x1Tstg, st["i"] * 128, [tv2])

        for i in range(NTILE):
            r0 = (be * NTILE + i) * 128
            gt = be * NTILE + i
            par = i % 2
            S = SCR[par]
            co = 16 * par
            mu = s_tiny[:, co + 0:co + 1]; s2 = s_tiny[:, co + 1:co + 2]
            l2 = s_tiny[:, co + 2:co + 3]; rstd = s_tiny[:, co + 3:co + 4]
            tld = P.dma("sp", lambda e, r=r0, p=par: e.dma_start(
                out=sx_t[p][:], in_=d_src[r:r + 128, :]),
                waits=[P.tok("dve"), P.tok("pe")] if i == 0 else [P.tok("dve")])
            for cb in range(2):
                bnk = 2 * par + cb
                w0 = [bank_wait(bnk)]
                if i == 0 and cb == 0:
                    w0 += [tw, bo_b, g1_b, b1_b, rw_b, rbc, attnT_tok]
                for ccc in range(CCH):
                    P.pe(lambda e, i=i, cb=cb, ccc=ccc, bnk=bnk:
                         e.matmul(psum[bnk][:],
                                  attnT[:, ccc * T + i * 128:ccc * T + i * 128 + 128],
                                  sWo[:, ccc * 1024 + cb * 512:
                                      ccc * 1024 + cb * 512 + 512],
                                  start=(ccc == 0), stop=(ccc == CCH - 1)),
                         waits=w0 if ccc == 0 else [],
                         inc="pe" if ccc == CCH - 1 else None)
                tp = P.tok("pe")
                te = P.dve(lambda e, cb=cb, bnk=bnk, S=S: e.tensor_tensor(
                    S["t"][:, cb * 512:cb * 512 + 512], psum[bnk][:],
                    s_cb[0][:, cb * 512:cb * 512 + 512], op=ALU.add),
                    waits=[tp], drain=True)
                bank_done(bnk, te)
            P.dve(lambda e, p=par, S=S: e.tensor_tensor(
                S["t"][:], S["t"][:], sx_t[p][:], op=ALU.add), waits=[tld],
                drain=True)
            P.dve(lambda e, S=S, mu=mu: e.reduce_sum(mu, S["t"][:], axis=AX.X),
                  drain=True)
            P.dve(lambda e, mu=mu: e.tensor_scalar_mul(mu, mu, float(1.0 / C)),
                  drain=True)
            P.dve(lambda e, S=S, mu=mu: e.tensor_scalar(
                S["u"][:], S["t"][:], mu, None, op0=ALU.subtract), drain=True)
            tv = P.tok("dve")
            pw = [tv] + ([prev_st[par]] if prev_st[par] else [])
            tq = P.act(lambda e, S=S: e.activation(S["w"][:], S["u"][:],
                                                   AF.Square), waits=pw)
            if backs:
                emit_back(backs.pop(0))
            tr = P.dve(lambda e, S=S, s2=s2: e.reduce_sum(
                s2, S["w"][:], axis=AX.X), waits=[tq], drain=True)
            P.act(lambda e, s2=s2, l2=l2: e.activation(
                l2, s2, AF.Ln, bias=f32(1e-5), scale=f32(1.0 / C)),
                waits=[tr], drain=True)
            P.act(lambda e, l2=l2, rstd=rstd: e.activation(
                rstd, l2, AF.Exp, scale=f32(-0.5)), drain=True)
            ta = P.tok("act")
            backs.append(dict(i=i, par=par, ta=ta, r0=r0, gt=gt))
        while backs:
            emit_back(backs.pop(0))
        tdone = P.tok("dve")
        return P.dma("sp", lambda e, b=be: e.dma_start(
            out=d_x1T[b].rearrange("c p t -> p c t"),
            in_=x1Tstg[:].rearrange("p (c t) -> p c t", c=CCH)), waits=[tdone])

    def emit_moe():
        x1t_tok = P.tok("ld")
        mask_tok = P.tok("dve")
        pe_flush = P.tok("pe")
        last_acc = None
        gidx = 0
        for e_i in range(NE):
            b2tok = ld_bcast_row(wf["b2"][e_i:e_i + 1, :], s_b2b,
                                 waits=[pe_flush if e_i == 0 else P.tok("dve")])
            b1tok = P.dma("sp", lambda e, e_i=e_i: e.dma_start(
                out=s_cols[:, 16:48], in_=wf["b1"][e_i]),
                waits=[pe_flush] if e_i == 0 else [])
            for g in range(NE):
                par = gidx % 2
                wv = [P.tok("pe"), x1t_tok]
                tw1 = P.dma("pool", lambda e, e_i=e_i, g=g, par=par: e.dma_start(
                    out=sW1g[par][:].rearrange("p (c f) -> p c f", c=CCH),
                    in_=wf["W1"][e_i].rearrange("(c p) f -> p c f", p=128)
                    [:, :, g * 1024:(g + 1) * 1024]), waits=wv)
                tw2 = P.dma("pool", lambda e, e_i=e_i, g=g, par=par: e.dma_start(
                    out=sW2g[par][:].rearrange("p (f c) -> p f c", f=8),
                    in_=wf["W2"][e_i][g * 1024:(g + 1) * 1024, :]
                    .rearrange("(fc p) c -> p fc c", p=128)), waits=wv)
                for tb in range(16):
                    be = tb // 4
                    tl = tb % 4
                    xp = tb % 2
                    tx = P.dma("sp", lambda e, be=be, tl=tl, xp=xp: e.dma_start(
                        out=sxt[xp][:].rearrange("p (c t) -> p c t", c=CCH),
                        in_=d_x1T[be].rearrange("c p t -> p c t")
                        [:, :, tl * 512:tl * 512 + 512]),
                        waits=[P.tok("pe"), x1t_tok])
                    for ffc in range(8):
                        bnk = 6 + (ffc % 2)
                        w0 = [bank_wait(bnk), tw1, tx, b1tok]
                        for ccc in range(CCH):
                            P.pe(lambda e, ffc=ffc, ccc=ccc, bnk=bnk, par=par, xp=xp:
                                 e.matmul(psum[bnk][:],
                                          sW1g[par][:, ccc * 1024 + ffc * 128:
                                                    ccc * 1024 + ffc * 128 + 128],
                                          sxt[xp][:, ccc * 512:ccc * 512 + 512],
                                          start=(ccc == 0), stop=(ccc == CCH - 1)),
                                 waits=w0 if ccc == 0 else [],
                                 inc="pe" if ccc == CCH - 1 else None)
                        tp = P.tok("pe")
                        th = P.act(lambda e, ffc=ffc, bnk=bnk, g=g:
                                   e.activation(h_sb[:, ffc * 512:ffc * 512 + 512],
                                                psum[bnk][:], AF.Relu,
                                                bias=s_cols[:, 16 + g * 8 + ffc:
                                                            17 + g * 8 + ffc],
                                                scale=f32(1.0)), waits=[tp])
                        bank_done(bnk, th)
                    th_all = P.tok("act")
                    for tsub in range(4):
                        for cb in range(2):
                            bnk = (tsub * 2 + cb) % 4
                            w0 = [bank_wait(bnk), tw2, th_all]
                            if tsub == 0 and cb == 0:
                                w0.append(P.tok("sw"))
                            for ffc in range(8):
                                P.pe(lambda e, ts=tsub, cb=cb, ffc=ffc, bnk=bnk, par=par:
                                     e.matmul(psum[bnk][:],
                                              h_sb[:, ffc * 512 + ts * 128:
                                                   ffc * 512 + ts * 128 + 128],
                                              sW2g[par][:, ffc * 1024 + cb * 512:
                                                        ffc * 1024 + cb * 512 + 512],
                                              start=(ffc == 0), stop=(ffc == 7)),
                                     waits=w0 if ffc == 0 else [],
                                     inc="pe" if ffc == 7 else None)
                            tp = P.tok("pe")
                            gt = tb * 4 + tsub
                            mcol = s_masks[:, gt * 4 + e_i:gt * 4 + e_i + 1]
                            dsl = (tsub * C + cb * 512, tsub * C + cb * 512 + 512)
                            if g == NE - 1:
                                P.dve(lambda e, bnk=bnk, xp=xp, dsl=dsl:
                                      e.tensor_tensor(
                                          ytmp[xp][:, dsl[0]:dsl[1]], psum[bnk][:],
                                          s_b2b[:, (dsl[0] % C):(dsl[0] % C) + 512],
                                          op=ALU.add),
                                      waits=[tp, b2tok, mask_tok], drain=True)
                                te = P.dve(lambda e, xp=xp, dsl=dsl, m=mcol:
                                           e.tensor_scalar_mul(
                                               ytmp[xp][:, dsl[0]:dsl[1]],
                                               ytmp[xp][:, dsl[0]:dsl[1]], m))
                            else:
                                te = P.dve(lambda e, bnk=bnk, xp=xp, dsl=dsl, m=mcol:
                                           e.tensor_scalar(
                                               ytmp[xp][:, dsl[0]:dsl[1]],
                                               psum[bnk][:], m, None, op0=ALU.mult),
                                           waits=[tp, mask_tok])
                            bank_done(bnk, te)
                    ty = P.tok("dve")
                    r0 = tb * 512
                    last_acc = P.dma("pool", lambda e, r=r0, xp=xp: e.dma_start(
                        out=d_x1ff[r:r + 512, :].rearrange("(s p) c -> p s c", p=128),
                        in_=ytmp[xp][:].rearrange("p (s c) -> p s c", s=4),
                        accum_op=ALU.add), waits=[ty, x1t_tok])
                gidx += 1
        return last_acc

    def emit_ln2_resid(res_src, res_dst):
        acc_done = P.tok("sw")
        g2_b = ld_bcast_row(wf["ln2_g"], s_cb[1], waits=[P.tok("dve")])
        b2_b = ld_bcast_row(wf["ln2_b"], s_cb[2])
        prev_st = [None, None]
        backs = []
        last = [None]

        def emit_back(st):
            par = st["par"]
            S = SCR[par]
            co = 16 * par
            rstd = s_tiny[:, co + 3:co + 4]
            P.dve(lambda e, S=S: e.tensor_scalar_mul(S["t"][:], S["t"][:], rstd),
                  waits=[st["ta"], st["txl"]], drain=True)
            P.dve(lambda e, S=S: e.tensor_tensor(S["t"][:], S["t"][:], s_cb[1][:],
                                                 op=ALU.mult), drain=True)
            P.dve(lambda e, S=S: e.tensor_tensor(S["t"][:], S["t"][:], s_cb[2][:],
                                                 op=ALU.add), drain=True)
            tdv = P.dve(lambda e, S=S: e.scalar_tensor_tensor(
                S["t"][:], S["t"][:], 0.5, S["u"][:], op0=ALU.mult, op1=ALU.add),
                drain=True)
            prev_st[par] = P.dma("sp", lambda e, r=st["r0"], S=S: e.dma_start(
                out=res_dst[r:r + 128, :], in_=S["t"][:]), waits=[tdv], inc="st")
            last[0] = prev_st[par]

        for i in range(B * NTILE):
            r0 = i * 128
            par = i % 2
            S = SCR[par]
            co = 16 * par
            mu = s_tiny[:, co + 0:co + 1]; s2 = s_tiny[:, co + 1:co + 2]
            l2 = s_tiny[:, co + 2:co + 3]; rstd = s_tiny[:, co + 3:co + 4]
            lw = [acc_done, g2_b, b2_b] if i == 0 else [P.tok("dve")]
            tld = P.dma("sp", lambda e, r=r0, p=par: e.dma_start(
                out=sx_t[p][:], in_=d_x1ff[r:r + 128, :]), waits=lw)
            txl = P.dma("sp", lambda e, r=r0, S=S: e.dma_start(
                out=S["u"][:], in_=res_src[r:r + 128, :]), waits=[P.tok("dve")])
            pw = [tld] + ([prev_st[par]] if prev_st[par] else [])
            P.dve(lambda e, p=par, mu=mu: e.reduce_sum(mu, sx_t[p][:], axis=AX.X),
                  waits=pw, drain=True)
            P.dve(lambda e, mu=mu: e.tensor_scalar_mul(mu, mu, float(1.0 / C)),
                  drain=True)
            P.dve(lambda e, p=par, S=S, mu=mu: e.tensor_scalar(
                S["t"][:], sx_t[p][:], mu, None, op0=ALU.subtract), drain=True)
            tv = P.tok("dve")
            tq = P.act(lambda e, S=S: e.activation(S["w"][:], S["t"][:],
                                                   AF.Square), waits=[tv])
            if backs:
                emit_back(backs.pop(0))
            tr = P.dve(lambda e, S=S, s2=s2: e.reduce_sum(
                s2, S["w"][:], axis=AX.X), waits=[tq], drain=True)
            P.act(lambda e, s2=s2, l2=l2: e.activation(
                l2, s2, AF.Ln, bias=f32(1e-5), scale=f32(1.0 / C)),
                waits=[tr], drain=True)
            P.act(lambda e, l2=l2, rstd=rstd: e.activation(
                rstd, l2, AF.Exp, scale=f32(-0.5)), drain=True)
            ta = P.tok("act")
            backs.append(dict(i=i, par=par, ta=ta, r0=r0, txl=txl))
        while backs:
            emit_back(backs.pop(0))
        return last[0]

    def emit_attn_qkv(be):
        lnw_b = ld_bcast_row(wa["ln_w"], s_cb[0], waits=[P.tok("pe")])
        srcT_tok = P.tok("dve")
        for (col0, dst) in ((0, QTt), (C, KTt)):
            tw = stream_w(wa["w_qkv"], sWs, col0=col0,
                          waits=[P.tok("pe"), P.tok("dve")])
            backs = []

            def emit_back(st, dst=dst):
                par = st["par"]
                S = QSCR[par]
                co = 16 * par
                rstd = s_tiny[:, co + 3:co + 4]
                P.dve(lambda e, S=S, rstd=rstd: e.scalar_tensor_tensor(
                    S["xb"][:], S["t"][:], rstd, s_cb[0][:],
                    op0=ALU.mult, op1=ALU.mult), waits=[st["ta"]], drain=True)
                tv2 = P.tok("dve")
                transpose_chunks(S["xb"][:], dst, st["i"] * 128, [tv2])

            for i in range(NTILE):
                par = i % 2
                S = QSCR[par]
                co = 16 * par
                s2 = s_tiny[:, co + 1:co + 2]; l2 = s_tiny[:, co + 2:co + 3]
                rstd = s_tiny[:, co + 3:co + 4]
                for fb in range(2):
                    bnk = 2 * par + fb
                    w0 = [bank_wait(bnk)]
                    if i == 0 and fb == 0:
                        w0 += [tw, srcT_tok, lnw_b]
                    for ccc in range(CCH):
                        P.pe(lambda e, i=i, fb=fb, ccc=ccc, bnk=bnk:
                             e.matmul(psum[bnk][:],
                                      srcT[:, ccc * T + i * 128:ccc * T + i * 128 + 128],
                                      sWs[:, ccc * 1024 + fb * 512:
                                          ccc * 1024 + fb * 512 + 512],
                                      start=(ccc == 0), stop=(ccc == CCH - 1)),
                             waits=w0 if ccc == 0 else [],
                             inc="pe" if ccc == CCH - 1 else None)
                    tp = P.tok("pe")
                    te = P.dve(lambda e, fb=fb, bnk=bnk, S=S: e.tensor_copy(
                        S["t"][:, fb * 512:fb * 512 + 512], psum[bnk][:]),
                        waits=[tp], drain=True)
                    bank_done(bnk, te)
                tv = P.tok("dve")
                tq = P.act(lambda e, S=S: e.activation(S["u"][:], S["t"][:],
                                                       AF.Square), waits=[tv])
                if backs:
                    emit_back(backs.pop(0))
                tr = P.dve(lambda e, S=S, s2=s2: e.reduce_sum(
                    s2, S["u"][:], axis=AX.X), waits=[tq], drain=True)
                P.act(lambda e, s2=s2, l2=l2: e.activation(
                    l2, s2, AF.Ln, bias=f32(1e-6), scale=f32(1.0 / C)),
                    waits=[tr], drain=True)
                P.act(lambda e, l2=l2, rstd=rstd: e.activation(
                    rstd, l2, AF.Exp, scale=f32(-0.5)), drain=True)
                ta = P.tok("act")
                backs.append(dict(i=i, par=par, ta=ta))
            while backs:
                emit_back(backs.pop(0))
        tw = stream_w(wa["w_qkv"], sWs, col0=2 * C,
                      waits=[P.tok("pe"), P.tok("dve")])
        tmo = P.dve(lambda e: e.memset(
            VA[:].rearrange("p (t h e2) -> p t h e2", t=NTILE, h=H)
            [:, :, :, 64:65], 1.0))
        for i in range(NTILE):
            for fb in range(2):
                bnk = (i * 2 + fb) % 4
                w0 = [bank_wait(bnk)]
                if i == 0 and fb == 0:
                    w0 += [tw, tmo]
                for ccc in range(CCH):
                    P.pe(lambda e, i=i, fb=fb, ccc=ccc, bnk=bnk:
                         e.matmul(psum[bnk][:],
                                  srcT[:, ccc * T + i * 128:ccc * T + i * 128 + 128],
                                  sWs[:, ccc * 1024 + fb * 512:
                                      ccc * 1024 + fb * 512 + 512],
                                  start=(ccc == 0), stop=(ccc == CCH - 1)),
                         waits=w0 if ccc == 0 else [],
                         inc="pe" if ccc == CCH - 1 else None)
                tp = P.tok("pe")
                te = P.dve(lambda e, i=i, fb=fb, bnk=bnk: e.tensor_copy(
                    VA[:].rearrange("p (t h e2) -> p t h e2", t=NTILE, h=H)
                    [:, i, fb * 8:fb * 8 + 8, 0:64],
                    psum[bnk][:].rearrange("p (h d) -> p h d", h=8)),
                    waits=[tp])
                bank_done(bnk, te)
        return P.tok("dve")

    def emit_attn_proj(be):
        tw = stream_w(wa["w_proj"], sWo, waits=[P.tok("pe"), P.tok("dve")])
        attnT_tok = P.tok("dve")
        prev_st = None
        for i in range(NTILE):
            r0 = (be * NTILE + i) * 128
            p = i % 2
            tld = P.dma("sp", lambda e, r=r0, p=p: e.dma_start(
                out=sx_t[p][:], in_=d_resid[r:r + 128, :]), waits=[P.tok("dve")])
            for cb in range(2):
                bnk = cb
                w0 = [bank_wait(bnk)]
                if i == 0 and cb == 0:
                    w0 += [tw, attnT_tok]
                for ccc in range(CCH):
                    P.pe(lambda e, i=i, cb=cb, ccc=ccc, bnk=bnk:
                         e.matmul(psum[bnk][:],
                                  attnT[:, ccc * T + i * 128:ccc * T + i * 128 + 128],
                                  sWo[:, ccc * 1024 + cb * 512:
                                      ccc * 1024 + cb * 512 + 512],
                                  start=(ccc == 0), stop=(ccc == CCH - 1)),
                         waits=w0 if ccc == 0 else [],
                         inc="pe" if ccc == CCH - 1 else None)
                tp = P.tok("pe")
                pw = [tp, tld]
                if prev_st is not None:
                    pw.append(prev_st)
                te = P.dve(lambda e, cb=cb, bnk=bnk, p=p: e.tensor_tensor(
                    t_t[:, cb * 512:cb * 512 + 512], psum[bnk][:],
                    sx_t[p][:, cb * 512:cb * 512 + 512], op=ALU.add), waits=pw,
                    drain=True)
                bank_done(bnk, te)
            tdv = P.tok("dve")
            prev_st = P.dma("sp", lambda e, r=r0: e.dma_start(
                out=d_resid[r:r + 128, :], in_=t_t[:]), waits=[tdv], inc="st")
        return prev_st

    def dbg_dump(slabs=(), va=False, dram=None, masks=False):
        w = [P.tok("pe"), P.tok("dve"), P.tok("act"), P.tok("ld"), P.tok("sw")]
        for idx, s in enumerate(slabs):
            P.dma("sp", lambda e, s=s, i=idx: e.dma_start(
                out=dbg_slab[i], in_=s[:]), waits=w, inc="st")
        if va:
            P.dma("sp", lambda e: e.dma_start(out=dbg_va[:], in_=VA[:]),
                  waits=w, inc="st")
        if dram is not None:
            P.dma("sp", lambda e, d=dram: e.dma_start(out=dbg_f[:], in_=d[:]),
                  waits=w, inc="st")
        if masks:
            P.dma("sp", lambda e: e.dma_start(out=dbg_m[:], in_=s_masks[:]),
                  waits=w, inc="st")
        P.emit("sp", lambda e: e.nop(), waits=[P.tok("st")])
        P.realize()
        return nc

    P.dma("sp", lambda e: e.dma_start(out=s_ident[:], in_=ident_dr[:]))
    P.dma("sp", lambda e: e.dma_start(out=s_cmask[:], in_=cmask_dr[:]))
    ic = P.tok("ld")
    P.dve(lambda e: e.memset(s_ones[:], 1.0), waits=[ic])

    def ffn_call(res_src, res_dst, first_wait, tag=""):
        fw = first_wait
        for be in range(B):
            r0 = emit_lnp_src(res_src, be, spill=True, first_wait=fw)
            if upto == "f1_t0":
                return nc
            fw = None
            if upto == tag + "lnp":
                return dbg_dump(slabs=[srcT], dram=d_src)
            emit_ffn_qkv(be)
            if upto == tag + "qkv":
                return dbg_dump(slabs=[QTt, KTt], va=True)
            for h in range(H):
                stg = emit_stage_head(h, h % 2)
                emit_flash_head(h, h % 2, stg, local=False)
            if upto == tag + "flash":
                return dbg_dump(slabs=[attnT])
            emit_ffn_post(be)
            if upto == tag + "post":
                return dbg_dump(dram=d_x1ff, masks=True)
        if upto == tag + "post4":
            return dbg_dump(dram=d_x1ff, masks=True)
        emit_moe()
        if upto == tag + "moe":
            return dbg_dump(dram=d_x1ff)
        r = emit_ln2_resid(res_src, res_dst)
        if upto == tag + "ln2":
            return dbg_dump(dram=res_dst)
        return r

    def attn_call(first_wait):
        fw = first_wait
        for be in range(B):
            emit_lnp_src(d_resid, be, spill=False, first_wait=fw)
            fw = None
            if upto == "a_lnp":
                return dbg_dump(slabs=[srcT])
            emit_attn_qkv(be)
            if upto == "a_qkv":
                return dbg_dump(slabs=[QTt, KTt], va=True)
            flash_done = {0: None, 1: None}
            for h in range(H):
                stg = emit_stage_head(h, h % 2, prev=flash_done[h % 2])
                rr = emit_flash_head(h, h % 2, stg, local=True)
                flash_done[h % 2] = P.tok("pe")
                if rr is None:
                    return nc
            if upto == "a_flash":
                return dbg_dump(slabs=[attnT])
            emit_attn_proj(be)
        if upto == "a_proj":
            return dbg_dump(dram=d_resid)

    if upto is not None and upto == "aonly":
        import os as _os
        nheads = int(_os.environ.get("DBG_HEADS", "3"))
        tcp0 = P.dma("sp", lambda e: e.dma_start(out=d_resid[:], in_=xin[:]))
        emit_lnp_src(d_resid, 0, spill=False, first_wait=tcp0)
        emit_attn_qkv(0)
        for h in range(nheads):
            stg = emit_stage_head(h, h % 2)
            rr = emit_flash_head(h, h % 2, stg, local=True)
            if rr is None:
                return nc
        return dbg_dump(slabs=[attnT, QTt, KTt], va=True)

    r = ffn_call(xin, d_resid, P.tok("dve"), tag="f1_")
    if upto is not None and upto.startswith("f1_"):
        return r
    r = attn_call(P.tok("st"))
    if upto is not None and upto.startswith("a_"):
        return r
    r = ffn_call(d_resid, out, P.tok("st"), tag="f2_")
    if upto is not None and upto.startswith("f2_"):
        return r

    P.emit("sp", lambda e: e.nop(), waits=[P.tok("st"), P.tok("sw"), P.tok("ld")])
    P.realize()
    return nc


_NC_CACHE = {}


def kernel(**inputs):
    n = 8
    x = np.ascontiguousarray(np.asarray(inputs["L0_attn_x"], dtype=np.float32))
    shards = [np.ascontiguousarray(x[i * B:(i + 1) * B].reshape(NTOK, C))
              for i in range(n)]

    def g(name):
        return np.ascontiguousarray(np.asarray(inputs[name], dtype=np.float32))

    base = {}
    base["a_w_qkv"] = g("L0_attn_w_qkv")
    base["a_w_proj"] = g("L0_attn_w_proj")
    base["a_ln_w"] = g("L0_attn_ln_w").reshape(1, C)
    for nm in ("Wq", "Wk", "Wv", "Wo"):
        base["f_" + nm] = g("L0_ffn_" + nm)
    for nm in ("bv", "bo", "ln1_g", "ln1_b", "ln2_g", "ln2_b"):
        base["f_" + nm] = g("L0_ffn_" + nm).reshape(1, C)
    base["f_bq"] = np.ascontiguousarray(
        g("L0_ffn_bq").reshape(8, 128).T)
    base["f_bk"] = np.ascontiguousarray(
        g("L0_ffn_bk").reshape(8, 128).T)
    base["f_b1"] = np.ascontiguousarray(
        g("L0_ffn_b1").reshape(NE, 32, 128).transpose(0, 2, 1))
    base["f_router_w"] = g("L0_ffn_router_w").reshape(1, C)
    base["f_router_b"] = g("L0_ffn_router_b").reshape(1, 1)
    base["f_W1"] = g("L0_ffn_W1")
    base["f_W2"] = g("L0_ffn_W2")
    base["f_b2"] = g("L0_ffn_b2")

    if "nc" not in _NC_CACHE:
        _NC_CACHE["nc"] = build_program()
    nc = _NC_CACHE["nc"]
    in_maps = [dict(base, x=s) for s in shards]
    res = run_bass_kernel_spmd(nc, in_maps, core_ids=list(range(n)))
    outs = [np.asarray(res.results[i]["out"]).reshape(B, T, C) for i in range(n)]
    return np.concatenate(outs, axis=0).astype(np.float32)


# version 07 (final)
# speedup vs baseline: 1.0687x; 1.0247x over previous
import numpy as np
import ml_dtypes
import concourse.bass as bass
import concourse.mybir as mybir
from concourse.bass_utils import run_bass_kernel_spmd

F32 = mybir.dt.float32
BF16 = mybir.dt.bfloat16
I32 = mybir.dt.int32
AF = mybir.ActivationFunctionType
ALU = mybir.AluOpType
AX = mybir.AxisListType
f32 = np.float32

B = 4
T = 2048
C = 1024
H = 16
CCH = 8
NTILE = 16
DFF = 4096
NE = 4
LS = 256
NTOK = B * T

SBUF_BASE = 16384 + 768
SBUF_LIMIT = 16384 + 207 * 1024

ENGS = ("pe", "act", "dve", "pool", "sp")


class Prog:
    def __init__(self, nc):
        self.nc = nc
        self.ops = []
        self.count = {}
        self.sems = {}
        self.bank_tok = {}

    def sem(self, name):
        if name not in self.sems:
            self.sems[name] = self.nc.alloc_semaphore(name)
            self.count[name] = 0
        return self.sems[name]

    def emit(self, eng, fn, waits=(), inc=None, inc_n=1, drain=False):
        ws = []
        for w in waits:
            if w is None:
                continue
            s, v = w
            self.sem(s)
            assert v <= self.count[s], f"future wait {s}:{v}>{self.count[s]}"
            ws.append((s, v))
        self.ops.append((eng, fn, ws, inc, inc_n, drain))
        if inc is not None:
            self.sem(inc)
            self.count[inc] += inc_n
            return (inc, self.count[inc])
        return None

    def tok(self, sem):
        self.sem(sem)
        return (sem, self.count[sem])

    def pe(self, fn, waits=(), inc=None):
        return self.emit("pe", fn, waits, inc)

    def act(self, fn, waits=(), inc="act", drain=False):
        return self.emit("act", fn, waits, inc, 1, drain)

    def dve(self, fn, waits=(), inc="dve", drain=False):
        return self.emit("dve", fn, waits, inc, 1, drain)

    def dma(self, q, fn, waits=(), inc=None):
        inc = inc or ("ld" if q == "sp" else "sw")
        return self.emit(q, fn, waits, inc, 16)

    def realize(self):
        nc = self.nc
        last_wait = {}
        with nc.Block() as blk:
            def run_engine(e_name):
                def body(handle):
                    for (eng, fn, ws, inc, inc_n, drain) in self.ops:
                        if eng != e_name:
                            continue
                        for (s, v) in ws:
                            key = (e_name, s)
                            if last_wait.get(key, -1) >= v:
                                continue
                            handle.wait_ge(self.sems[s], v)
                            last_wait[key] = v
                        instr = fn(handle)
                        if inc is not None:
                            instr.then_inc(self.sems[inc], inc_n)
                        if drain:
                            handle.drain()
                return body
            blk.sync(run_engine("sp"))
            blk.gpsimd(run_engine("pool"))
            blk.tensor(run_engine("pe"))
            blk.scalar(run_engine("act"))
            blk.vector(run_engine("dve"))


def build_program(upto=None):
    nc = bass.Bass("TRN2", target_bir_lowering=False,
                   detect_race_conditions=False)

    xin = nc.dram_tensor("x", [NTOK, C], F32, kind="ExternalInput")
    out = nc.dram_tensor("out", [NTOK, C], F32, kind="ExternalOutput")
    wa, wf = {}, {}
    for n, shp in (("w_qkv", [C, 3 * C]), ("ln_w", [1, C]), ("w_proj", [C, C])):
        wa[n] = nc.dram_tensor("a_" + n, shp, F32, kind="ExternalInput")
    ffshapes = (("Wq", [C, C]), ("bq", [128, 8]), ("Wk", [C, C]), ("bk", [128, 8]),
                ("Wv", [C, C]), ("bv", [1, C]), ("Wo", [C, C]), ("bo", [1, C]),
                ("router_w", [1, C]), ("router_b", [1, 1]),
                ("W1", [NE, C, DFF]), ("b1", [NE, 128, 32]),
                ("W2", [NE, DFF, C]), ("b2", [NE, C]),
                ("ln1_g", [1, C]), ("ln1_b", [1, C]),
                ("ln2_g", [1, C]), ("ln2_b", [1, C]))
    for n, shp in ffshapes:
        wf[n] = nc.dram_tensor("f_" + n, shp, F32, kind="ExternalInput")

    d_resid = nc.dram_tensor("d_resid", [NTOK, C], F32, kind="Internal")
    d_src = nc.dram_tensor("d_src", [NTOK, C], F32, kind="Internal")
    d_x1ff = nc.dram_tensor("d_x1ff", [NTOK, C], F32, kind="Internal")
    d_x1T = nc.dram_tensor("d_x1T", [B, CCH, 128, T], BF16, kind="Internal")
    if upto is not None:
        dbg_slab = nc.dram_tensor("dbg_slab", [3, 128, CCH * T], BF16,
                                  kind="ExternalOutput")
        dbg_va = nc.dram_tensor("dbg_va", [128, NTILE * H * 65], BF16,
                                kind="ExternalOutput")
        dbg_f = nc.dram_tensor("dbg_f", [NTOK, C], F32, kind="ExternalOutput")
        dbg_m = nc.dram_tensor("dbg_m", [128, 256], F32, kind="ExternalOutput")

    ident_dr = nc.inline_tensor(np.eye(128, dtype=ml_dtypes.bfloat16),
                                name="ident_c")
    kk = np.arange(128)[:, None]
    qq = np.arange(LS)[None, :]
    cmask_np = np.concatenate([(kk <= qq), (kk + 128 <= qq)], axis=1)
    cmask_dr = nc.inline_tensor(
        np.ascontiguousarray(cmask_np.astype(ml_dtypes.bfloat16)), name="cmask_c")

    alloc_log = []

    def at(name, shape, dtype, off):
        per = int(np.prod(shape[1:])) * mybir.dt.size(dtype)
        assert SBUF_BASE <= off and off + per <= SBUF_LIMIT, \
            f"sbuf overflow {name}: [{off}, {off + per})"
        alloc_log.append((name, off, off + per))
        return nc.alloc_sbuf_tensor_at(name, list(shape), dtype, offset=off)

    KB = 1024
    O_PERS = SBUF_BASE
    s_ident = at("s_ident", [128, 128], BF16, O_PERS)
    s_cmask = at("s_cmask", [128, 512], BF16, O_PERS + 256)
    s_ones = at("s_ones", [128, 128], BF16, O_PERS + 1280)
    s_cols = at("s_cols", [128, 64], F32, O_PERS + 1536)
    s_tiny = at("s_tiny", [128, 32], F32, O_PERS + 1792)
    O_S0 = O_PERS + 2 * KB
    O_S1 = O_S0 + 32 * KB
    O_S2 = O_S1 + 32 * KB
    O_S3 = O_S2 + 32 * KB
    O_S4 = O_S3 + 34 * KB
    O_S5 = O_S4 + 29952
    O_S6 = O_S5 + 23 * KB
    O_END = O_S6 + 22 * KB
    assert O_END <= SBUF_LIMIT, O_END

    srcT = at("srcT", [128, CCH * T], BF16, O_S0)
    attnT = at("attnT", [128, CCH * T], BF16, O_S0)
    QTt = at("QTt", [128, CCH * T], BF16, O_S1)
    sW1g = [at("sW1g0", [128, CCH * 1024], BF16, O_S1),
            at("sW1g1", [128, CCH * 1024], BF16, O_S2)]
    sW2g = [at("sW2g0", [128, 8 * 1024], BF16, O_S1 + 16 * KB),
            at("sW2g1", [128, 8 * 1024], BF16, O_S2 + 16 * KB)]
    sWo = at("sWo", [128, CCH * 1024], BF16, O_S1)
    KTt = at("KTt", [128, CCH * T], BF16, O_S2)
    x1Tstg = at("x1Tstg", [128, CCH * T], BF16, O_S2)
    VA = at("VA", [128, NTILE * H * 65], BF16, O_S3)
    sxt = [at("sxt0", [128, CCH * 512], BF16, O_S3),
           at("sxt1", [128, CCH * 512], BF16, O_S3 + 8 * KB)]
    s_b2b = at("s_b2b", [128, C], F32, O_S3 + 16 * KB)
    h_sb = at("h_sb", [128, 8 * 512], BF16, O_S3 + 20 * KB)
    s_cb = [at(f"s_cb{i}", [128, C], F32, O_S3 + i * 4 * KB) for i in range(4)]
    sWs = at("sWs", [128, CCH * 1024], BF16, O_S4)
    rrow = at("rrow", [1, 512], F32, O_S4)
    sqh = at("sqh", [64, T], BF16, O_S4)
    lnq2 = at("lnq2", [1, T], BF16, O_S4 + 17 * KB)
    k2r = at("k2r", [1, T], BF16, O_S4 + 21 * KB)
    b8r = at("b8r", [1, T], BF16, O_S4 + 25 * KB)
    s_kv = at("s_kv", [1, 8], F32, O_S4 + 29 * KB)
    ytmp = [at("ytmp0", [128, 4 * C], F32, O_S4),
            at("ytmp1", [128, 4 * C], F32, O_S4 + 16 * KB)]
    stgQ = [at("stgQ0", [65, T], BF16, O_S5),
            at("stgQ1", [65, T], BF16, O_S5 + 4 * KB)]
    stgK = [at("stgK0", [65, T], BF16, O_S5 + 8 * KB),
            at("stgK1", [65, T], BF16, O_S5 + 12 * KB)]
    PTb = [at("PTb0", [128, 512], BF16, O_S5 + 16 * KB),
           at("PTb1", [128, 512], BF16, O_S5 + 17 * KB)]
    rb_sb = at("rb_sb", [64, 512], BF16, O_S5 + 18 * KB)
    rowb = at("rowb", [1, 512], BF16, O_S5 + 19 * KB)
    bv_bf = at("bv_bf", [1, C], BF16, O_S5 + 20 * KB)
    s_masks = at("s_masks", [128, 256], F32, O_S5 + 22 * KB)
    sx_t = [at("sx_t0", [128, C], F32, O_S6),
            at("sx_t1", [128, C], F32, O_S6 + 4 * KB)]
    t_t = at("t_t", [128, C], F32, O_S6 + 8 * KB)
    u_t = at("u_t", [128, C], F32, O_S6 + 12 * KB)
    w_t = at("w_t", [128, C], F32, O_S6 + 16 * KB)
    xb_t = at("xb_t", [128, C], BF16, O_S6 + 20 * KB)
    t_t2 = at("t_t2", [128, C], F32, O_S4)
    u_t2 = at("u_t2", [128, C], F32, O_S4 + 4 * KB)
    w_t2 = at("w_t2", [128, C], F32, O_S4 + 8 * KB)
    xb_t2 = at("xb_t2", [128, C], BF16, O_S4 + 12 * KB)
    qt_t2 = at("qt_t2", [128, C], F32, O_S4 + 17 * KB)
    qu_t2 = at("qu_t2", [128, C], F32, O_S4 + 21 * KB)
    qxb_t2 = at("qxb_t2", [128, C], BF16, O_S4 + 25 * KB)
    rb_f32 = at("rb_f32", [64, 512], F32, O_S4 + 14 * KB)
    SCR = [dict(t=t_t, u=u_t, w=w_t, xb=xb_t),
           dict(t=t_t2, u=u_t2, w=w_t2, xb=xb_t2)]
    QSCR = [dict(t=t_t, u=u_t, xb=xb_t),
            dict(t=qt_t2, u=qu_t2, xb=qxb_t2)]

    psum = [nc.alloc_psum_tensor(f"pb{i}", [128, 512], F32) for i in range(8)]

    P = Prog(nc)
    BT = P.bank_tok

    def bank_wait(b):
        return BT.get(b)

    def bank_done(b, tok):
        BT[b] = tok

    def ld_bcast_row(dram_row, dest, waits=()):
        n = dram_row.shape[-1]
        return P.dma("sp", lambda e, d=dram_row, o=dest, n=n: e.dma_start(
            out=o[:, 0:n], in_=d[0:1, 0:n].broadcast_to([128, n])), waits=waits)

    def ln_cols(x_ap, eps, waits=()):
        mu = s_tiny[:, 0:1]
        s2 = s_tiny[:, 8:9]
        l2 = s_tiny[:, 16:17]
        rstd = s_tiny[:, 24:25]
        P.dve(lambda e: e.reduce_sum(mu, x_ap, axis=AX.X), waits=list(waits),
              drain=True)
        P.dve(lambda e: e.tensor_scalar_mul(mu, mu, float(1.0 / C)), drain=True)
        P.dve(lambda e: e.tensor_scalar(t_t[:], x_ap, mu, None, op0=ALU.subtract),
              drain=True)
        tv = P.tok("dve")
        tq = P.act(lambda e: e.activation(u_t[:], t_t[:], AF.Square), waits=[tv])
        tr = P.dve(lambda e: e.reduce_sum(s2, u_t[:], axis=AX.X), waits=[tq])
        P.act(lambda e: e.activation(l2, s2, AF.Ln, bias=f32(eps),
                                     scale=f32(1.0 / C)), waits=[tr], drain=True)
        P.act(lambda e: e.activation(rstd, l2, AF.Exp, scale=f32(-0.5)),
              drain=True)
        return P.tok("act")

    def transpose_chunks(src_bf, dst, dst_col0, waits):
        for cc in range(CCH):
            b = 4 + (cc % 4)
            pv = psum[b][:, 0:128].bitcast(BF16)[:, 0:128]
            w = [bank_wait(b)] + list(waits)
            tk = P.pe(lambda e, pv=pv, cc=cc, s=src_bf: e.transpose(
                pv, s[:, cc * 128:(cc + 1) * 128], s_ident[:]),
                waits=w, inc="pe")
            tc = P.dve(lambda e, pv=pv, c=dst_col0 + cc * T, d=dst:
                       e.tensor_copy(d[:, c:c + 128], pv), waits=[tk])
            bank_done(b, tc)
        return P.tok("dve")

    def stream_w(dram_w, dest, ncols=1024, col0=0, waits=()):
        return P.dma("pool", lambda e: e.dma_start(
            out=dest[:, 0:CCH * ncols].rearrange("p (c f) -> p c f", c=CCH),
            in_=dram_w[:, col0:col0 + ncols].rearrange("(c p) f -> p c f", p=128)),
            waits=waits)

    def emit_lnp_src(src_dram, be, spill, first_wait=None):
        backs = []

        def emit_back(st):
            par = st["par"]
            S = SCR[par]
            co = 16 * par
            rstd = s_tiny[:, co + 3:co + 4]
            P.dve(lambda e, S=S, rstd=rstd: e.tensor_scalar_mul(
                S["xb"][:], S["t"][:], rstd), waits=[st["ta"]])
            if spill:
                tsp = P.dve(lambda e, S=S, rstd=rstd: e.tensor_scalar_mul(
                    S["w"][:], S["t"][:], rstd))
                P.dma("sp", lambda e, r=st["r0"], S=S: e.dma_start(
                    out=d_src[r:r + 128, :], in_=S["w"][:]), waits=[tsp])
            tv = P.tok("dve")
            transpose_chunks(S["xb"][:], srcT, st["i"] * 128, [tv])

        for i in range(NTILE):
            r0 = (be * NTILE + i) * 128
            par = i % 2
            S = SCR[par]
            co = 16 * par
            mu = s_tiny[:, co + 0:co + 1]
            s2 = s_tiny[:, co + 1:co + 2]
            l2 = s_tiny[:, co + 2:co + 3]
            rstd = s_tiny[:, co + 3:co + 4]
            lw = [P.tok("dve")]
            if first_wait is not None and i == 0:
                lw.append(first_wait)
            tld = P.dma("sp", lambda e, r=r0, p=par: e.dma_start(
                out=sx_t[p][:], in_=src_dram[r:r + 128, :]), waits=lw)
            P.dve(lambda e, p=par, mu=mu: e.reduce_sum(mu, sx_t[p][:], axis=AX.X),
                  waits=[tld], drain=True)
            P.dve(lambda e, mu=mu: e.tensor_scalar_mul(mu, mu, float(1.0 / C)),
                  drain=True)
            P.dve(lambda e, p=par, S=S, mu=mu: e.tensor_scalar(
                S["t"][:], sx_t[p][:], mu, None, op0=ALU.subtract), drain=True)
            tv = P.tok("dve")
            tq = P.act(lambda e, S=S: e.activation(S["u"][:], S["t"][:],
                                                   AF.Square), waits=[tv])
            if backs:
                emit_back(backs.pop(0))
            tr = P.dve(lambda e, S=S, s2=s2: e.reduce_sum(
                s2, S["u"][:], axis=AX.X), waits=[tq], drain=True)
            P.act(lambda e, s2=s2, l2=l2: e.activation(
                l2, s2, AF.Ln, bias=f32(1e-5), scale=f32(1.0 / C)),
                waits=[tr], drain=True)
            P.act(lambda e, l2=l2, rstd=rstd: e.activation(
                rstd, l2, AF.Exp, scale=f32(-0.5)), drain=True)
            ta = P.tok("act")
            backs.append(dict(i=i, par=par, ta=ta, r0=r0))
        while backs:
            emit_back(backs.pop(0))
        return P.tok("dve")

    def emit_ffn_qkv(be):
        pe0 = P.tok("pe")
        bvl = P.dma("pool", lambda e: e.dma_start(out=bv_bf[:], in_=wf["bv"][:]),
                    waits=[pe0])
        c0 = P.dma("sp", lambda e: e.dma_start(
            out=s_cols[:, 0:8], in_=wf["bq"][:]), waits=[pe0])
        c1 = P.dma("sp", lambda e: e.dma_start(
            out=s_cols[:, 8:16], in_=wf["bk"][:]), waits=[pe0])
        srcT_tok = P.tok("dve")
        for (wname, dst, b0) in (("Wq", QTt, 0), ("Wk", KTt, 8)):
            tw = stream_w(wf[wname], sWs, waits=[P.tok("pe")])
            for fc in range(CCH):
                for tb in range(4):
                    bnk = (fc * 4 + tb) % 4
                    w0 = [bank_wait(bnk)]
                    if fc == 0 and tb == 0:
                        w0 += [tw, srcT_tok, c0, c1]
                    for cc in range(CCH):
                        P.pe(lambda e, fc=fc, tb=tb, cc=cc, bnk=bnk:
                             e.matmul(psum[bnk][:],
                                      sWs[:, cc * 1024 + fc * 128:
                                          cc * 1024 + fc * 128 + 128],
                                      srcT[:, cc * T + tb * 512:
                                           cc * T + tb * 512 + 512],
                                      start=(cc == 0), stop=(cc == CCH - 1)),
                             waits=w0 if cc == 0 else [],
                             inc="pe" if cc == CCH - 1 else None)
                    tp = P.tok("pe")
                    te = P.dve(lambda e, fc=fc, tb=tb, bnk=bnk, dst=dst, b0=b0:
                               e.tensor_scalar(
                                   dst[:, fc * T + tb * 512:fc * T + tb * 512 + 512],
                                   psum[bnk][:], s_cols[:, b0 + fc:b0 + fc + 1],
                                   None, op0=ALU.add), waits=[tp])
                    bank_done(bnk, te)
        tw = stream_w(wf["Wv"], sWs, waits=[P.tok("pe")])
        tmo = P.dve(lambda e: e.memset(
            VA[:].rearrange("p (t h e2) -> p t h e2", t=NTILE, h=H)
            [:, :, :, 64:65], 1.0))
        for i in range(NTILE):
            for fb in range(2):
                bnk = (i * 2 + fb) % 4
                w0 = [bank_wait(bnk)]
                if i == 0 and fb == 0:
                    w0 += [tw, tmo, bvl]
                for cc in range(CCH):
                    P.pe(lambda e, i=i, fb=fb, cc=cc, bnk=bnk:
                         e.matmul(psum[bnk][:],
                                  srcT[:, cc * T + i * 128:cc * T + i * 128 + 128],
                                  sWs[:, cc * 1024 + fb * 512:
                                      cc * 1024 + fb * 512 + 512],
                                  start=(cc == 0), stop=False),
                         waits=w0 if cc == 0 else [],
                         inc=None)
                P.pe(lambda e, fb=fb, bnk=bnk:
                     e.matmul(psum[bnk][:], s_ones[0:1, 0:128],
                              bv_bf[0:1, fb * 512:fb * 512 + 512],
                              start=False, stop=True), inc="pe")
                tp = P.tok("pe")
                te = P.dve(lambda e, i=i, fb=fb, bnk=bnk: e.tensor_copy(
                    VA[:].rearrange("p (t h e2) -> p t h e2", t=NTILE, h=H)
                    [:, i, fb * 8:fb * 8 + 8, 0:64],
                    psum[bnk][:].rearrange("p (h d) -> p h d", h=8)),
                    waits=[tp])
                bank_done(bnk, te)
        return P.tok("dve")

    def emit_stage_head(h, par, shift=20.0, prev=None):
        cc = h // 2
        half = 64 * (h % 2)
        if prev is None:
            prev = P.tok("pe")
        P.dve(lambda e: e.tensor_copy(
            stgQ[par][0:64, :], QTt[half:half + 64, cc * T:(cc + 1) * T]),
            waits=[prev])
        P.dve(lambda e: e.tensor_copy(
            stgK[par][0:64, :], KTt[half:half + 64, cc * T:(cc + 1) * T]))
        P.dve(lambda e: e.memset(stgK[par][64:65, :], 1.0))
        tcp = P.tok("dve")
        for (stg, dstrow, ln_out) in ((stgQ, lnq2, True), (stgK, k2r, False)):
            tsq = P.act(lambda e, stg=stg: e.activation(
                sqh[0:64, :], stg[par][0:64, :], AF.Square),
                waits=[tcp, P.tok("pe")])
            for jq in range(4):
                pm = P.pe(lambda e, jq=jq: e.matmul(
                    psum[7][0:1, jq * 128:jq * 128 + 512][:, 0:512]
                    if False else psum[7][0:1, 0:512],
                    s_ones[0:64, 0:1], sqh[0:64, jq * 512:(jq + 1) * 512],
                    start=True, stop=True),
                    waits=[tsq, bank_wait(7)], inc="pe")
                if ln_out:
                    tc = P.act(lambda e, jq=jq: e.activation(
                        lnq2[0:1, jq * 512:(jq + 1) * 512], psum[7][0:1, 0:512],
                        AF.Ln), waits=[pm], drain=(jq == 3))
                else:
                    tc = P.act(lambda e, jq=jq: e.activation(
                        k2r[0:1, jq * 512:(jq + 1) * 512], psum[7][0:1, 0:512],
                        AF.Copy, bias=0.0), waits=[pm], drain=(jq == 3))
                bank_done(7, tc)
        tk2 = P.tok("act")
        k2m = s_kv[0:1, 0:1]
        hlf = s_kv[0:1, 1:2]
        P.dve(lambda e: e.reduce_max(k2m, k2r[:], axis=AX.X), waits=[tk2], drain=True)
        tvm = P.tok("dve")
        P.act(lambda e: e.activation(hlf, k2m, AF.Ln), waits=[tvm], drain=True)
        ta = P.tok("act")
        P.dve(lambda e: e.tensor_scalar_mul(hlf, hlf, float(0.5)), waits=[ta], drain=True)
        tvh = P.tok("dve")
        P.act(lambda e: e.activation(b8r[:], lnq2[:], AF.Exp, scale=f32(0.5),
                                     bias=hlf), waits=[tvh], drain=True)
        tb8 = P.tok("act")
        P.dve(lambda e: e.tensor_scalar(
            stgQ[par][64:65, :], b8r[:], float(-1.0), float(8.0 * shift),
            op0=ALU.mult, op1=ALU.add), waits=[tb8], drain=True)
        return P.tok("dve")

    def emit_flash_head(h, par, stg_tok, local):
        cc = h // 2
        half = 64 * (h % 2)

        def evac_chain(ob, qlo, N, tpv, trb_prev=None):
            w0 = [tpv]
            if trb_prev is not None:
                w0.append(trb_prev)
            tvr = P.dve(lambda e, ob=ob, N=N: e.tensor_copy(
                rowb[0:1, 0:N], psum[ob][64:65, 0:N]), waits=w0, drain=True)
            trb = P.pe(lambda e, N=N: e.matmul(
                psum[6][0:64, 0:N], s_ones[0:1, 0:64], rowb[0:1, 0:N],
                start=True, stop=True), waits=[tvr, bank_wait(6)], inc="pe")
            tcb = P.dve(lambda e, N=N: e.reciprocal(
                rb_f32[:, 0:N], psum[6][0:64, 0:N]), waits=[trb], drain=True)
            bank_done(6, tcb)
            ev = P.dve(lambda e, ob=ob, N=N, qlo=qlo: e.tensor_tensor(
                attnT[half:half + 64, cc * T + qlo:cc * T + qlo + N],
                psum[ob][0:64, 0:N], rb_f32[:, 0:N], op=ALU.mult),
                waits=[tcb])
            bank_done(ob, ev)
            return trb

        if not local:
            pending = None
            for (u, qlo, N, kcs) in [(qb, qb * 512, 512, list(range(16)))
                                     for qb in range(4)]:
                ob = 2 + (u % 2)
                exp_toks = {}
                pv_toks = {}
                for j, kc in enumerate(kcs):
                    sb = j % 2
                    stw = [bank_wait(sb)]
                    if j == 0:
                        stw.append(stg_tok)
                    st_tok = P.pe(
                        lambda e, kc=kc, sb=sb, qlo=qlo, N=N: e.matmul(
                            psum[sb][0:128, 0:N],
                            stgK[par][:, kc * 128:(kc + 1) * 128],
                            stgQ[par][:, qlo:qlo + N],
                            start=True, stop=True),
                        waits=stw, inc="pe")
                    ew = [st_tok]
                    if j >= 2:
                        ew.append(pv_toks[kcs[j - 2]])
                    tE = P.act(lambda e, sb=sb, N=N: e.activation(
                        PTb[sb][:, 0:N], psum[sb][0:128, 0:N], AF.Exp,
                        scale=f32(0.125)), waits=ew)
                    bank_done(sb, tE)
                    exp_toks[kc] = tE
                    pw = [tE]
                    if j == 0:
                        pw.append(bank_wait(ob))
                    pv_toks[kc] = P.pe(
                        lambda e, kc=kc, sb=sb, ob=ob, N=N,
                        st_=(kc == kcs[0]), sp_=(kc == kcs[-1]): e.matmul(
                            psum[ob][0:65, 0:N],
                            VA[:, kc * (H * 65) + h * 65:kc * (H * 65) + h * 65 + 65],
                            PTb[sb][:, 0:N],
                            start=st_, stop=sp_),
                        waits=pw, inc="pe")
                tpv = P.tok("pe")
                if pending is not None:
                    pending()
                pending = (lambda ob=ob, qlo=qlo, N=N, tpv=tpv:
                           evac_chain(ob, qlo, N, tpv))
            pending()
            return P.tok("dve")

        N = LS
        pt_slot = {}
        for gpar in range(2):
            for j in range(2):
                pt_slot[(gpar, j)] = PTb[gpar][:, j * 256:j * 256 + 256]
        pv_last = {}
        pending = []
        trb_prev = None
        for gp in range(0, 8, 2):
            st_toks = {}
            for g in (gp, gp + 1):
                stb = (0, 1) if g % 2 == 0 else (4, 5)
                for j in range(2):
                    kc = g * 2 + j
                    qlo = g * LS
                    stw = [bank_wait(stb[j])]
                    if g == 0 and j == 0:
                        stw.append(stg_tok)
                    st_toks[(g, j)] = P.pe(
                        lambda e, kc=kc, b=stb[j], qlo=qlo: e.matmul(
                            psum[b][0:128, 0:N],
                            stgK[par][:, kc * 128:(kc + 1) * 128],
                            stgQ[par][:, qlo:qlo + N],
                            start=True, stop=True),
                        waits=stw, inc="pe")
            mask_toks = {}
            for g in (gp, gp + 1):
                stb = (0, 1) if g % 2 == 0 else (4, 5)
                for j in range(2):
                    slot = (g % 2, j)
                    ew = [st_toks[(g, j)]]
                    if slot in pv_last:
                        ew.append(pv_last[slot])
                    tE = P.act(lambda e, b=stb[j], s=pt_slot[slot]: e.activation(
                        s, psum[b][0:128, 0:N], AF.Exp, scale=f32(0.125)),
                        waits=ew)
                    bank_done(stb[j], tE)
                    mask_toks[(g, j)] = P.dve(
                        lambda e, s=pt_slot[slot], j=j: e.tensor_tensor(
                            s, s, s_cmask[:, j * 256:j * 256 + 256], op=ALU.mult),
                        waits=[tE])
            for (ob_, qlo_, tpv_) in pending:
                trb_prev = evac_chain(ob_, qlo_, N, tpv_, trb_prev)
            pending = []
            for g in (gp, gp + 1):
                ob = 2 + (g % 2)
                for j in range(2):
                    kc = g * 2 + j
                    slot = (g % 2, j)
                    pw = [mask_toks[(g, j)]]
                    if j == 0:
                        pw.append(bank_wait(ob))
                    pv_last[slot] = P.pe(
                        lambda e, kc=kc, ob=ob, s=pt_slot[slot],
                        st_=(j == 0), sp_=(j == 1): e.matmul(
                            psum[ob][0:65, 0:N],
                            VA[:, kc * (H * 65) + h * 65:kc * (H * 65) + h * 65 + 65],
                            s, start=st_, stop=sp_),
                        waits=pw, inc="pe")
            tpv = P.tok("pe")
            pending = [(2 + (g % 2), g * LS, tpv) for g in (gp, gp + 1)]
        for (ob_, qlo_, tpv_) in pending:
            trb_prev = evac_chain(ob_, qlo_, N, tpv_, trb_prev)
        return P.tok("dve")

    def emit_ffn_post(be):
        tw = stream_w(wf["Wo"], sWo, waits=[P.tok("pe"), P.tok("dve")])
        bo_b = ld_bcast_row(wf["bo"], s_cb[0], waits=[P.tok("pe")])
        g1_b = ld_bcast_row(wf["ln1_g"], s_cb[1])
        b1_b = ld_bcast_row(wf["ln1_b"], s_cb[2])
        rw_b = ld_bcast_row(wf["router_w"], s_cb[3])
        rbc = P.dma("sp", lambda e: e.dma_start(
            out=s_cols[:, 48:49],
            in_=wf["router_b"][0:1, 0:1].broadcast_to([128, 1])))
        attnT_tok = P.tok("dve")
        prev_st = [None, None]
        backs = []

        def emit_back(st):
            par = st["par"]
            S = SCR[par]
            co = 16 * par
            rstd = s_tiny[:, co + 3:co + 4]
            keys = s_tiny[:, co + 4:co + 5]
            kq = s_tiny[:, co + 5:co + 6]; fi = s_tiny[:, co + 6:co + 7]
            fc_ = s_tiny[:, co + 7:co + 8]; km = s_tiny[:, co + 8:co + 9]
            ii = s_tiny[:, co + 9:co + 10].bitcast(I32)[:, 0:1]
            a1 = s_tiny[:, co + 10:co + 11]; a2 = s_tiny[:, co + 11:co + 12]
            a3 = s_tiny[:, co + 12:co + 13]
            gt = st["gt"]
            P.dve(lambda e, S=S: e.tensor_scalar_mul(S["u"][:], S["u"][:], rstd),
                  waits=[st["ta"]], drain=True)
            P.dve(lambda e, S=S: e.tensor_tensor(S["u"][:], S["u"][:], s_cb[1][:],
                                                 op=ALU.mult), drain=True)
            P.dve(lambda e, S=S: e.tensor_tensor(S["w"][:], S["u"][:], s_cb[2][:],
                                                 op=ALU.add), drain=True)
            P.dve(lambda e, S=S: e.tensor_tensor(S["u"][:], S["w"][:], s_cb[3][:],
                                                 op=ALU.mult), drain=True)
            P.dve(lambda e, S=S: e.reduce_sum(keys, S["u"][:], axis=AX.X),
                  drain=True)
            P.dve(lambda e: e.tensor_scalar_add(keys, keys, s_cols[:, 48:49]),
                  drain=True)
            P.dve(lambda e: e.tensor_scalar_mul(kq, keys, float(0.25)), drain=True)
            P.dve(lambda e: e.tensor_copy(ii, kq), drain=True)
            P.dve(lambda e: e.tensor_copy(fi, ii), drain=True)
            P.dve(lambda e: e.tensor_tensor(fc_, fi, kq, op=ALU.is_gt), drain=True)
            P.dve(lambda e: e.tensor_sub(fi, fi, fc_), drain=True)
            P.dve(lambda e: e.tensor_scalar(km, fi, float(-4.0), keys,
                                            op0=ALU.mult, op1=ALU.add), drain=True)
            P.dve(lambda e: e.tensor_single_scalar(a1, km, float(1.0),
                                                   op=ALU.is_ge))
            P.dve(lambda e: e.tensor_single_scalar(a2, km, float(2.0),
                                                   op=ALU.is_ge))
            P.dve(lambda e: e.tensor_single_scalar(a3, km, float(3.0),
                                                   op=ALU.is_ge), drain=True)
            P.dve(lambda e, g=gt: e.tensor_scalar(
                s_masks[:, g * 4:g * 4 + 1], a1, float(-1.0), float(1.0),
                op0=ALU.mult, op1=ALU.add))
            P.dve(lambda e, g=gt: e.tensor_sub(
                s_masks[:, g * 4 + 1:g * 4 + 2], a1, a2))
            P.dve(lambda e, g=gt: e.tensor_sub(
                s_masks[:, g * 4 + 2:g * 4 + 3], a2, a3))
            P.dve(lambda e, g=gt: e.tensor_copy(
                s_masks[:, g * 4 + 3:g * 4 + 4], a3), drain=True)
            tx = P.tok("dve")
            prev_st[par] = P.dma("sp", lambda e, r=st["r0"], S=S: e.dma_start(
                out=d_x1ff[r:r + 128, :], in_=S["w"][:]), waits=[tx])
            P.dve(lambda e, S=S: e.tensor_copy(S["xb"][:], S["w"][:]), drain=True)
            tv2 = P.tok("dve")
            transpose_chunks(S["xb"][:], x1Tstg, st["i"] * 128, [tv2])

        for i in range(NTILE):
            r0 = (be * NTILE + i) * 128
            gt = be * NTILE + i
            par = i % 2
            S = SCR[par]
            co = 16 * par
            mu = s_tiny[:, co + 0:co + 1]; s2 = s_tiny[:, co + 1:co + 2]
            l2 = s_tiny[:, co + 2:co + 3]; rstd = s_tiny[:, co + 3:co + 4]
            tld = P.dma("sp", lambda e, r=r0, p=par: e.dma_start(
                out=sx_t[p][:], in_=d_src[r:r + 128, :]),
                waits=[P.tok("dve"), P.tok("pe")] if i == 0 else [P.tok("dve")])
            for cb in range(2):
                bnk = 2 * par + cb
                w0 = [bank_wait(bnk)]
                if i == 0 and cb == 0:
                    w0 += [tw, bo_b, g1_b, b1_b, rw_b, rbc, attnT_tok]
                for ccc in range(CCH):
                    P.pe(lambda e, i=i, cb=cb, ccc=ccc, bnk=bnk:
                         e.matmul(psum[bnk][:],
                                  attnT[:, ccc * T + i * 128:ccc * T + i * 128 + 128],
                                  sWo[:, ccc * 1024 + cb * 512:
                                      ccc * 1024 + cb * 512 + 512],
                                  start=(ccc == 0), stop=(ccc == CCH - 1)),
                         waits=w0 if ccc == 0 else [],
                         inc="pe" if ccc == CCH - 1 else None)
                tp = P.tok("pe")
                te = P.dve(lambda e, cb=cb, bnk=bnk, S=S: e.tensor_tensor(
                    S["t"][:, cb * 512:cb * 512 + 512], psum[bnk][:],
                    s_cb[0][:, cb * 512:cb * 512 + 512], op=ALU.add),
                    waits=[tp], drain=True)
                bank_done(bnk, te)
            P.dve(lambda e, p=par, S=S: e.tensor_tensor(
                S["t"][:], S["t"][:], sx_t[p][:], op=ALU.add), waits=[tld],
                drain=True)
            P.dve(lambda e, S=S, mu=mu: e.reduce_sum(mu, S["t"][:], axis=AX.X),
                  drain=True)
            P.dve(lambda e, mu=mu: e.tensor_scalar_mul(mu, mu, float(1.0 / C)),
                  drain=True)
            P.dve(lambda e, S=S, mu=mu: e.tensor_scalar(
                S["u"][:], S["t"][:], mu, None, op0=ALU.subtract), drain=True)
            tv = P.tok("dve")
            pw = [tv] + ([prev_st[par]] if prev_st[par] else [])
            tq = P.act(lambda e, S=S: e.activation(S["w"][:], S["u"][:],
                                                   AF.Square), waits=pw)
            if backs:
                emit_back(backs.pop(0))
            tr = P.dve(lambda e, S=S, s2=s2: e.reduce_sum(
                s2, S["w"][:], axis=AX.X), waits=[tq], drain=True)
            P.act(lambda e, s2=s2, l2=l2: e.activation(
                l2, s2, AF.Ln, bias=f32(1e-5), scale=f32(1.0 / C)),
                waits=[tr], drain=True)
            P.act(lambda e, l2=l2, rstd=rstd: e.activation(
                rstd, l2, AF.Exp, scale=f32(-0.5)), drain=True)
            ta = P.tok("act")
            backs.append(dict(i=i, par=par, ta=ta, r0=r0, gt=gt))
        while backs:
            emit_back(backs.pop(0))
        tdone = P.tok("dve")
        return P.dma("sp", lambda e, b=be: e.dma_start(
            out=d_x1T[b].rearrange("c p t -> p c t"),
            in_=x1Tstg[:].rearrange("p (c t) -> p c t", c=CCH)), waits=[tdone])

    def emit_moe():
        x1t_tok = P.tok("ld")
        mask_tok = P.tok("dve")
        pe_flush = P.tok("pe")
        last_acc = None
        gidx = 0
        for e_i in range(NE):
            b2tok = ld_bcast_row(wf["b2"][e_i:e_i + 1, :], s_b2b,
                                 waits=[pe_flush if e_i == 0 else P.tok("dve")])
            b1tok = P.dma("sp", lambda e, e_i=e_i: e.dma_start(
                out=s_cols[:, 16:48], in_=wf["b1"][e_i]),
                waits=[pe_flush] if e_i == 0 else [])
            for g in range(NE):
                par = gidx % 2
                wv = [P.tok("pe"), x1t_tok]
                tw1 = P.dma("pool", lambda e, e_i=e_i, g=g, par=par: e.dma_start(
                    out=sW1g[par][:].rearrange("p (c f) -> p c f", c=CCH),
                    in_=wf["W1"][e_i].rearrange("(c p) f -> p c f", p=128)
                    [:, :, g * 1024:(g + 1) * 1024]), waits=wv)
                tw2 = P.dma("pool", lambda e, e_i=e_i, g=g, par=par: e.dma_start(
                    out=sW2g[par][:].rearrange("p (f c) -> p f c", f=8),
                    in_=wf["W2"][e_i][g * 1024:(g + 1) * 1024, :]
                    .rearrange("(fc p) c -> p fc c", p=128)), waits=wv)
                for tb in range(16):
                    be = tb // 4
                    tl = tb % 4
                    xp = tb % 2
                    tx = P.dma("sp", lambda e, be=be, tl=tl, xp=xp: e.dma_start(
                        out=sxt[xp][:].rearrange("p (c t) -> p c t", c=CCH),
                        in_=d_x1T[be].rearrange("c p t -> p c t")
                        [:, :, tl * 512:tl * 512 + 512]),
                        waits=[P.tok("pe"), x1t_tok])
                    for ffc in range(8):
                        bnk = 5 + (ffc % 3)
                        w0 = [bank_wait(bnk), tw1, tx, b1tok]
                        for ccc in range(CCH):
                            P.pe(lambda e, ffc=ffc, ccc=ccc, bnk=bnk, par=par, xp=xp:
                                 e.matmul(psum[bnk][:],
                                          sW1g[par][:, ccc * 1024 + ffc * 128:
                                                    ccc * 1024 + ffc * 128 + 128],
                                          sxt[xp][:, ccc * 512:ccc * 512 + 512],
                                          start=(ccc == 0), stop=(ccc == CCH - 1)),
                                 waits=w0 if ccc == 0 else [],
                                 inc="pe" if ccc == CCH - 1 else None)
                        tp = P.tok("pe")
                        th = P.act(lambda e, ffc=ffc, bnk=bnk, g=g:
                                   e.activation(h_sb[:, ffc * 512:ffc * 512 + 512],
                                                psum[bnk][:], AF.Relu,
                                                bias=s_cols[:, 16 + g * 8 + ffc:
                                                            17 + g * 8 + ffc],
                                                scale=f32(1.0)), waits=[tp])
                        bank_done(bnk, th)
                    th_all = P.tok("act")
                    for tsub in range(4):
                        for cb in range(2):
                            bnk = (tsub * 2 + cb) % 4
                            w0 = [bank_wait(bnk), tw2, th_all]
                            if tsub == 0 and cb == 0:
                                w0.append(P.tok("sw"))
                            for ffc in range(8):
                                P.pe(lambda e, ts=tsub, cb=cb, ffc=ffc, bnk=bnk, par=par:
                                     e.matmul(psum[bnk][:],
                                              h_sb[:, ffc * 512 + ts * 128:
                                                   ffc * 512 + ts * 128 + 128],
                                              sW2g[par][:, ffc * 1024 + cb * 512:
                                                        ffc * 1024 + cb * 512 + 512],
                                              start=(ffc == 0), stop=(ffc == 7)),
                                     waits=w0 if ffc == 0 else [],
                                     inc="pe" if ffc == 7 else None)
                            tp = P.tok("pe")
                            gt = tb * 4 + tsub
                            mcol = s_masks[:, gt * 4 + e_i:gt * 4 + e_i + 1]
                            dsl = (tsub * C + cb * 512, tsub * C + cb * 512 + 512)
                            if g == NE - 1:
                                P.dve(lambda e, bnk=bnk, xp=xp, dsl=dsl:
                                      e.tensor_tensor(
                                          ytmp[xp][:, dsl[0]:dsl[1]], psum[bnk][:],
                                          s_b2b[:, (dsl[0] % C):(dsl[0] % C) + 512],
                                          op=ALU.add),
                                      waits=[tp, b2tok, mask_tok], drain=True)
                                te = P.dve(lambda e, xp=xp, dsl=dsl, m=mcol:
                                           e.tensor_scalar_mul(
                                               ytmp[xp][:, dsl[0]:dsl[1]],
                                               ytmp[xp][:, dsl[0]:dsl[1]], m))
                            else:
                                te = P.dve(lambda e, bnk=bnk, xp=xp, dsl=dsl, m=mcol:
                                           e.tensor_scalar(
                                               ytmp[xp][:, dsl[0]:dsl[1]],
                                               psum[bnk][:], m, None, op0=ALU.mult),
                                           waits=[tp, mask_tok])
                            bank_done(bnk, te)
                    ty = P.tok("dve")
                    r0 = tb * 512
                    last_acc = P.dma("pool", lambda e, r=r0, xp=xp: e.dma_start(
                        out=d_x1ff[r:r + 512, :].rearrange("(s p) c -> p s c", p=128),
                        in_=ytmp[xp][:].rearrange("p (s c) -> p s c", s=4),
                        accum_op=ALU.add), waits=[ty, x1t_tok])
                gidx += 1
        return last_acc

    def emit_ln2_resid(res_src, res_dst):
        acc_done = P.tok("sw")
        g2_b = ld_bcast_row(wf["ln2_g"], s_cb[1], waits=[P.tok("dve")])
        b2_b = ld_bcast_row(wf["ln2_b"], s_cb[2])
        prev_st = [None, None]
        backs = []
        last = [None]

        def emit_back(st):
            par = st["par"]
            S = SCR[par]
            co = 16 * par
            rstd = s_tiny[:, co + 3:co + 4]
            P.dve(lambda e, S=S: e.tensor_scalar_mul(S["t"][:], S["t"][:], rstd),
                  waits=[st["ta"], st["txl"]], drain=True)
            P.dve(lambda e, S=S: e.tensor_tensor(S["t"][:], S["t"][:], s_cb[1][:],
                                                 op=ALU.mult), drain=True)
            P.dve(lambda e, S=S: e.tensor_tensor(S["t"][:], S["t"][:], s_cb[2][:],
                                                 op=ALU.add), drain=True)
            tdv = P.dve(lambda e, S=S: e.scalar_tensor_tensor(
                S["t"][:], S["t"][:], 0.5, S["u"][:], op0=ALU.mult, op1=ALU.add),
                drain=True)
            prev_st[par] = P.dma("sp", lambda e, r=st["r0"], S=S: e.dma_start(
                out=res_dst[r:r + 128, :], in_=S["t"][:]), waits=[tdv], inc="st")
            last[0] = prev_st[par]

        for i in range(B * NTILE):
            r0 = i * 128
            par = i % 2
            S = SCR[par]
            co = 16 * par
            mu = s_tiny[:, co + 0:co + 1]; s2 = s_tiny[:, co + 1:co + 2]
            l2 = s_tiny[:, co + 2:co + 3]; rstd = s_tiny[:, co + 3:co + 4]
            lw = [acc_done, g2_b, b2_b] if i == 0 else [P.tok("dve")]
            tld = P.dma("sp", lambda e, r=r0, p=par: e.dma_start(
                out=sx_t[p][:], in_=d_x1ff[r:r + 128, :]), waits=lw)
            txl = P.dma("sp", lambda e, r=r0, S=S: e.dma_start(
                out=S["u"][:], in_=res_src[r:r + 128, :]), waits=[P.tok("dve")])
            pw = [tld] + ([prev_st[par]] if prev_st[par] else [])
            P.dve(lambda e, p=par, mu=mu: e.reduce_sum(mu, sx_t[p][:], axis=AX.X),
                  waits=pw, drain=True)
            P.dve(lambda e, mu=mu: e.tensor_scalar_mul(mu, mu, float(1.0 / C)),
                  drain=True)
            P.dve(lambda e, p=par, S=S, mu=mu: e.tensor_scalar(
                S["t"][:], sx_t[p][:], mu, None, op0=ALU.subtract), drain=True)
            tv = P.tok("dve")
            tq = P.act(lambda e, S=S: e.activation(S["w"][:], S["t"][:],
                                                   AF.Square), waits=[tv])
            if backs:
                emit_back(backs.pop(0))
            tr = P.dve(lambda e, S=S, s2=s2: e.reduce_sum(
                s2, S["w"][:], axis=AX.X), waits=[tq], drain=True)
            P.act(lambda e, s2=s2, l2=l2: e.activation(
                l2, s2, AF.Ln, bias=f32(1e-5), scale=f32(1.0 / C)),
                waits=[tr], drain=True)
            P.act(lambda e, l2=l2, rstd=rstd: e.activation(
                rstd, l2, AF.Exp, scale=f32(-0.5)), drain=True)
            ta = P.tok("act")
            backs.append(dict(i=i, par=par, ta=ta, r0=r0, txl=txl))
        while backs:
            emit_back(backs.pop(0))
        return last[0]

    def emit_attn_qkv(be):
        lnw_b = ld_bcast_row(wa["ln_w"], s_cb[0], waits=[P.tok("pe")])
        srcT_tok = P.tok("dve")
        for (col0, dst) in ((0, QTt), (C, KTt)):
            tw = stream_w(wa["w_qkv"], sWs, col0=col0,
                          waits=[P.tok("pe"), P.tok("dve")])
            backs = []

            def emit_back(st, dst=dst):
                par = st["par"]
                S = QSCR[par]
                co = 16 * par
                rstd = s_tiny[:, co + 3:co + 4]
                P.dve(lambda e, S=S, rstd=rstd: e.scalar_tensor_tensor(
                    S["xb"][:], S["t"][:], rstd, s_cb[0][:],
                    op0=ALU.mult, op1=ALU.mult), waits=[st["ta"]], drain=True)
                tv2 = P.tok("dve")
                transpose_chunks(S["xb"][:], dst, st["i"] * 128, [tv2])

            for i in range(NTILE):
                par = i % 2
                S = QSCR[par]
                co = 16 * par
                s2 = s_tiny[:, co + 1:co + 2]; l2 = s_tiny[:, co + 2:co + 3]
                rstd = s_tiny[:, co + 3:co + 4]
                for fb in range(2):
                    bnk = 2 * par + fb
                    w0 = [bank_wait(bnk)]
                    if i == 0 and fb == 0:
                        w0 += [tw, srcT_tok, lnw_b]
                    for ccc in range(CCH):
                        P.pe(lambda e, i=i, fb=fb, ccc=ccc, bnk=bnk:
                             e.matmul(psum[bnk][:],
                                      srcT[:, ccc * T + i * 128:ccc * T + i * 128 + 128],
                                      sWs[:, ccc * 1024 + fb * 512:
                                          ccc * 1024 + fb * 512 + 512],
                                      start=(ccc == 0), stop=(ccc == CCH - 1)),
                             waits=w0 if ccc == 0 else [],
                             inc="pe" if ccc == CCH - 1 else None)
                    tp = P.tok("pe")
                    te = P.dve(lambda e, fb=fb, bnk=bnk, S=S: e.tensor_copy(
                        S["t"][:, fb * 512:fb * 512 + 512], psum[bnk][:]),
                        waits=[tp])
                    bank_done(bnk, te)
                tv = P.tok("dve")
                tq = P.act(lambda e, S=S: e.activation(S["u"][:], S["t"][:],
                                                       AF.Square), waits=[tv])
                if backs:
                    emit_back(backs.pop(0))
                tr = P.dve(lambda e, S=S, s2=s2: e.reduce_sum(
                    s2, S["u"][:], axis=AX.X), waits=[tq], drain=True)
                P.act(lambda e, s2=s2, l2=l2: e.activation(
                    l2, s2, AF.Ln, bias=f32(1e-6), scale=f32(1.0 / C)),
                    waits=[tr], drain=True)
                P.act(lambda e, l2=l2, rstd=rstd: e.activation(
                    rstd, l2, AF.Exp, scale=f32(-0.5)), drain=True)
                ta = P.tok("act")
                backs.append(dict(i=i, par=par, ta=ta))
            while backs:
                emit_back(backs.pop(0))
        tw = stream_w(wa["w_qkv"], sWs, col0=2 * C,
                      waits=[P.tok("pe"), P.tok("dve")])
        tmo = P.dve(lambda e: e.memset(
            VA[:].rearrange("p (t h e2) -> p t h e2", t=NTILE, h=H)
            [:, :, :, 64:65], 1.0))
        for i in range(NTILE):
            for fb in range(2):
                bnk = (i * 2 + fb) % 4
                w0 = [bank_wait(bnk)]
                if i == 0 and fb == 0:
                    w0 += [tw, tmo]
                for ccc in range(CCH):
                    P.pe(lambda e, i=i, fb=fb, ccc=ccc, bnk=bnk:
                         e.matmul(psum[bnk][:],
                                  srcT[:, ccc * T + i * 128:ccc * T + i * 128 + 128],
                                  sWs[:, ccc * 1024 + fb * 512:
                                      ccc * 1024 + fb * 512 + 512],
                                  start=(ccc == 0), stop=(ccc == CCH - 1)),
                         waits=w0 if ccc == 0 else [],
                         inc="pe" if ccc == CCH - 1 else None)
                tp = P.tok("pe")
                te = P.dve(lambda e, i=i, fb=fb, bnk=bnk: e.tensor_copy(
                    VA[:].rearrange("p (t h e2) -> p t h e2", t=NTILE, h=H)
                    [:, i, fb * 8:fb * 8 + 8, 0:64],
                    psum[bnk][:].rearrange("p (h d) -> p h d", h=8)),
                    waits=[tp])
                bank_done(bnk, te)
        return P.tok("dve")

    def emit_attn_proj(be):
        tw = stream_w(wa["w_proj"], sWo, waits=[P.tok("pe"), P.tok("dve")])
        attnT_tok = P.tok("dve")
        prev_st = None
        for i in range(NTILE):
            r0 = (be * NTILE + i) * 128
            p = i % 2
            tld = P.dma("sp", lambda e, r=r0, p=p: e.dma_start(
                out=sx_t[p][:], in_=d_resid[r:r + 128, :]), waits=[P.tok("dve")])
            for cb in range(2):
                bnk = cb
                w0 = [bank_wait(bnk)]
                if i == 0 and cb == 0:
                    w0 += [tw, attnT_tok]
                for ccc in range(CCH):
                    P.pe(lambda e, i=i, cb=cb, ccc=ccc, bnk=bnk:
                         e.matmul(psum[bnk][:],
                                  attnT[:, ccc * T + i * 128:ccc * T + i * 128 + 128],
                                  sWo[:, ccc * 1024 + cb * 512:
                                      ccc * 1024 + cb * 512 + 512],
                                  start=(ccc == 0), stop=(ccc == CCH - 1)),
                         waits=w0 if ccc == 0 else [],
                         inc="pe" if ccc == CCH - 1 else None)
                tp = P.tok("pe")
                pw = [tp, tld]
                if prev_st is not None:
                    pw.append(prev_st)
                te = P.dve(lambda e, cb=cb, bnk=bnk, p=p: e.tensor_tensor(
                    t_t[:, cb * 512:cb * 512 + 512], psum[bnk][:],
                    sx_t[p][:, cb * 512:cb * 512 + 512], op=ALU.add), waits=pw,
                    drain=True)
                bank_done(bnk, te)
            tdv = P.tok("dve")
            prev_st = P.dma("sp", lambda e, r=r0: e.dma_start(
                out=d_resid[r:r + 128, :], in_=t_t[:]), waits=[tdv], inc="st")
        return prev_st

    def dbg_dump(slabs=(), va=False, dram=None, masks=False):
        w = [P.tok("pe"), P.tok("dve"), P.tok("act"), P.tok("ld"), P.tok("sw")]
        for idx, s in enumerate(slabs):
            P.dma("sp", lambda e, s=s, i=idx: e.dma_start(
                out=dbg_slab[i], in_=s[:]), waits=w, inc="st")
        if va:
            P.dma("sp", lambda e: e.dma_start(out=dbg_va[:], in_=VA[:]),
                  waits=w, inc="st")
        if dram is not None:
            P.dma("sp", lambda e, d=dram: e.dma_start(out=dbg_f[:], in_=d[:]),
                  waits=w, inc="st")
        if masks:
            P.dma("sp", lambda e: e.dma_start(out=dbg_m[:], in_=s_masks[:]),
                  waits=w, inc="st")
        P.emit("sp", lambda e: e.nop(), waits=[P.tok("st")])
        P.realize()
        return nc

    P.dma("sp", lambda e: e.dma_start(out=s_ident[:], in_=ident_dr[:]))
    P.dma("sp", lambda e: e.dma_start(out=s_cmask[:], in_=cmask_dr[:]))
    ic = P.tok("ld")
    P.dve(lambda e: e.memset(s_ones[:], 1.0), waits=[ic])

    def ffn_call(res_src, res_dst, first_wait, tag=""):
        fw = first_wait
        for be in range(B):
            r0 = emit_lnp_src(res_src, be, spill=True, first_wait=fw)
            if upto == "f1_t0":
                return nc
            fw = None
            if upto == tag + "lnp":
                return dbg_dump(slabs=[srcT], dram=d_src)
            emit_ffn_qkv(be)
            if upto == tag + "qkv":
                return dbg_dump(slabs=[QTt, KTt], va=True)
            for h in range(H):
                stg = emit_stage_head(h, h % 2)
                emit_flash_head(h, h % 2, stg, local=False)
            if upto == tag + "flash":
                return dbg_dump(slabs=[attnT])
            emit_ffn_post(be)
            if upto == tag + "post":
                return dbg_dump(dram=d_x1ff, masks=True)
        if upto == tag + "post4":
            return dbg_dump(dram=d_x1ff, masks=True)
        emit_moe()
        if upto == tag + "moe":
            return dbg_dump(dram=d_x1ff)
        r = emit_ln2_resid(res_src, res_dst)
        if upto == tag + "ln2":
            return dbg_dump(dram=res_dst)
        return r

    def attn_call(first_wait):
        fw = first_wait
        for be in range(B):
            emit_lnp_src(d_resid, be, spill=False, first_wait=fw)
            fw = None
            if upto == "a_lnp":
                return dbg_dump(slabs=[srcT])
            emit_attn_qkv(be)
            if upto == "a_qkv":
                return dbg_dump(slabs=[QTt, KTt], va=True)
            flash_done = {0: None, 1: None}
            for h in range(H):
                stg = emit_stage_head(h, h % 2, prev=flash_done[h % 2])
                rr = emit_flash_head(h, h % 2, stg, local=True)
                flash_done[h % 2] = P.tok("pe")
                if rr is None:
                    return nc
            if upto == "a_flash":
                return dbg_dump(slabs=[attnT])
            emit_attn_proj(be)
        if upto == "a_proj":
            return dbg_dump(dram=d_resid)

    if upto is not None and upto == "aonly":
        import os as _os
        nheads = int(_os.environ.get("DBG_HEADS", "3"))
        tcp0 = P.dma("sp", lambda e: e.dma_start(out=d_resid[:], in_=xin[:]))
        emit_lnp_src(d_resid, 0, spill=False, first_wait=tcp0)
        emit_attn_qkv(0)
        for h in range(nheads):
            stg = emit_stage_head(h, h % 2)
            rr = emit_flash_head(h, h % 2, stg, local=True)
            if rr is None:
                return nc
        return dbg_dump(slabs=[attnT, QTt, KTt], va=True)

    r = ffn_call(xin, d_resid, P.tok("dve"), tag="f1_")
    if upto is not None and upto.startswith("f1_"):
        return r
    r = attn_call(P.tok("st"))
    if upto is not None and upto.startswith("a_"):
        return r
    r = ffn_call(d_resid, out, P.tok("st"), tag="f2_")
    if upto is not None and upto.startswith("f2_"):
        return r

    P.emit("sp", lambda e: e.nop(), waits=[P.tok("st"), P.tok("sw"), P.tok("ld")])
    P.realize()
    return nc


_NC_CACHE = {}


def kernel(**inputs):
    n = 8
    x = np.ascontiguousarray(np.asarray(inputs["L0_attn_x"], dtype=np.float32))
    shards = [np.ascontiguousarray(x[i * B:(i + 1) * B].reshape(NTOK, C))
              for i in range(n)]

    def g(name):
        return np.ascontiguousarray(np.asarray(inputs[name], dtype=np.float32))

    base = {}
    base["a_w_qkv"] = g("L0_attn_w_qkv")
    base["a_w_proj"] = g("L0_attn_w_proj")
    base["a_ln_w"] = g("L0_attn_ln_w").reshape(1, C)
    for nm in ("Wq", "Wk", "Wv", "Wo"):
        base["f_" + nm] = g("L0_ffn_" + nm)
    for nm in ("bv", "bo", "ln1_g", "ln1_b", "ln2_g", "ln2_b"):
        base["f_" + nm] = g("L0_ffn_" + nm).reshape(1, C)
    base["f_bq"] = np.ascontiguousarray(
        g("L0_ffn_bq").reshape(8, 128).T)
    base["f_bk"] = np.ascontiguousarray(
        g("L0_ffn_bk").reshape(8, 128).T)
    base["f_b1"] = np.ascontiguousarray(
        g("L0_ffn_b1").reshape(NE, 32, 128).transpose(0, 2, 1))
    base["f_router_w"] = g("L0_ffn_router_w").reshape(1, C)
    base["f_router_b"] = g("L0_ffn_router_b").reshape(1, 1)
    base["f_W1"] = g("L0_ffn_W1")
    base["f_W2"] = g("L0_ffn_W2")
    base["f_b2"] = g("L0_ffn_b2")

    if "nc" not in _NC_CACHE:
        _NC_CACHE["nc"] = build_program()
    nc = _NC_CACHE["nc"]
    in_maps = [dict(base, x=s) for s in shards]
    res = run_bass_kernel_spmd(nc, in_maps, core_ids=list(range(n)))
    outs = [np.asarray(res.results[i]["out"]).reshape(B, T, C) for i in range(n)]
    return np.concatenate(outs, axis=0).astype(np.float32)


# version 08
# speedup vs baseline: 1.0955x; 1.0251x over previous
import numpy as np
import ml_dtypes
import concourse.bass as bass
import concourse.mybir as mybir
from concourse.bass_utils import run_bass_kernel_spmd

F32 = mybir.dt.float32
BF16 = mybir.dt.bfloat16
I32 = mybir.dt.int32
AF = mybir.ActivationFunctionType
ALU = mybir.AluOpType
AX = mybir.AxisListType
f32 = np.float32

B = 4
T = 2048
C = 1024
H = 16
CCH = 8
NTILE = 16
DFF = 4096
NE = 4
LS = 256
NTOK = B * T

SBUF_BASE = 16384 + 768
SBUF_LIMIT = 16384 + 207 * 1024

ENGS = ("pe", "act", "dve", "pool", "sp")


class Prog:
    def __init__(self, nc):
        self.nc = nc
        self.ops = []
        self.count = {}
        self.sems = {}
        self.bank_tok = {}

    def sem(self, name):
        if name not in self.sems:
            self.sems[name] = self.nc.alloc_semaphore(name)
            self.count[name] = 0
        return self.sems[name]

    def emit(self, eng, fn, waits=(), inc=None, inc_n=1, drain=False):
        ws = []
        for w in waits:
            if w is None:
                continue
            s, v = w
            self.sem(s)
            assert v <= self.count[s], f"future wait {s}:{v}>{self.count[s]}"
            ws.append((s, v))
        self.ops.append((eng, fn, ws, inc, inc_n, drain))
        if inc is not None:
            self.sem(inc)
            self.count[inc] += inc_n
            return (inc, self.count[inc])
        return None

    def tok(self, sem):
        self.sem(sem)
        return (sem, self.count[sem])

    def pe(self, fn, waits=(), inc=None):
        return self.emit("pe", fn, waits, inc)

    def act(self, fn, waits=(), inc="act", drain=False):
        return self.emit("act", fn, waits, inc, 1, drain)

    def dve(self, fn, waits=(), inc="dve", drain=False):
        return self.emit("dve", fn, waits, inc, 1, drain)

    def dma(self, q, fn, waits=(), inc=None):
        inc = inc or ("ld" if q == "sp" else "sw")
        return self.emit(q, fn, waits, inc, 16)

    def realize(self):
        nc = self.nc
        last_wait = {}
        with nc.Block() as blk:
            def run_engine(e_name):
                def body(handle):
                    for (eng, fn, ws, inc, inc_n, drain) in self.ops:
                        if eng != e_name:
                            continue
                        for (s, v) in ws:
                            key = (e_name, s)
                            if last_wait.get(key, -1) >= v:
                                continue
                            handle.wait_ge(self.sems[s], v)
                            last_wait[key] = v
                        instr = fn(handle)
                        if inc is not None:
                            instr.then_inc(self.sems[inc], inc_n)
                        if drain:
                            handle.drain()
                return body
            blk.sync(run_engine("sp"))
            blk.gpsimd(run_engine("pool"))
            blk.tensor(run_engine("pe"))
            blk.scalar(run_engine("act"))
            blk.vector(run_engine("dve"))


def build_program(upto=None):
    nc = bass.Bass("TRN2", target_bir_lowering=False,
                   detect_race_conditions=False)

    xin = nc.dram_tensor("x", [NTOK, C], F32, kind="ExternalInput")
    out = nc.dram_tensor("out", [NTOK, C], F32, kind="ExternalOutput")
    wa, wf = {}, {}
    for n, shp in (("w_qkv", [C, 3 * C]), ("ln_w", [1, C]), ("w_proj", [C, C])):
        wa[n] = nc.dram_tensor("a_" + n, shp, F32, kind="ExternalInput")
    ffshapes = (("Wq", [C, C]), ("bq", [128, 8]), ("Wk", [C, C]), ("bk", [128, 8]),
                ("Wv", [C, C]), ("bv", [1, C]), ("Wo", [C, C]), ("bo", [1, C]),
                ("router_w", [1, C]), ("router_b", [1, 1]),
                ("W1", [NE, C, DFF]), ("b1", [NE, 128, 32]),
                ("W2", [NE, DFF, C]), ("b2", [NE, C]),
                ("ln1_g", [1, C]), ("ln1_b", [1, C]),
                ("ln2_g", [1, C]), ("ln2_b", [1, C]))
    for n, shp in ffshapes:
        wf[n] = nc.dram_tensor("f_" + n, shp, F32, kind="ExternalInput")

    d_resid = nc.dram_tensor("d_resid", [NTOK, C], F32, kind="Internal")
    d_src = nc.dram_tensor("d_src", [NTOK, C], F32, kind="Internal")
    d_x1ff = nc.dram_tensor("d_x1ff", [NTOK, C], F32, kind="Internal")
    d_x1T = nc.dram_tensor("d_x1T", [B, CCH, 128, T], BF16, kind="Internal")
    if upto is not None:
        dbg_slab = nc.dram_tensor("dbg_slab", [3, 128, CCH * T], BF16,
                                  kind="ExternalOutput")
        dbg_va = nc.dram_tensor("dbg_va", [128, NTILE * H * 65], BF16,
                                kind="ExternalOutput")
        dbg_f = nc.dram_tensor("dbg_f", [NTOK, C], F32, kind="ExternalOutput")
        dbg_m = nc.dram_tensor("dbg_m", [128, 256], F32, kind="ExternalOutput")

    ident_dr = nc.inline_tensor(np.eye(128, dtype=ml_dtypes.bfloat16),
                                name="ident_c")
    kk = np.arange(128)[:, None]
    qq = np.arange(LS)[None, :]
    cmask_np = np.concatenate([(kk <= qq), (kk + 128 <= qq)], axis=1)
    cmask_dr = nc.inline_tensor(
        np.ascontiguousarray(cmask_np.astype(ml_dtypes.bfloat16)), name="cmask_c")

    alloc_log = []

    def at(name, shape, dtype, off):
        per = int(np.prod(shape[1:])) * mybir.dt.size(dtype)
        assert SBUF_BASE <= off and off + per <= SBUF_LIMIT, \
            f"sbuf overflow {name}: [{off}, {off + per})"
        alloc_log.append((name, off, off + per))
        return nc.alloc_sbuf_tensor_at(name, list(shape), dtype, offset=off)

    KB = 1024
    O_PERS = SBUF_BASE
    s_ident = at("s_ident", [128, 128], BF16, O_PERS)
    s_cmask = at("s_cmask", [128, 512], BF16, O_PERS + 256)
    s_ones = at("s_ones", [128, 128], BF16, O_PERS + 1280)
    s_cols = at("s_cols", [128, 64], F32, O_PERS + 1536)
    s_tiny = at("s_tiny", [128, 32], F32, O_PERS + 1792)
    O_S0 = O_PERS + 2 * KB
    O_S1 = O_S0 + 32 * KB
    O_S2 = O_S1 + 32 * KB
    O_S3 = O_S2 + 32 * KB
    O_S4 = O_S3 + 34 * KB
    O_S5 = O_S4 + 29952
    O_S6 = O_S5 + 23 * KB
    O_END = O_S6 + 22 * KB
    assert O_END <= SBUF_LIMIT, O_END

    srcT = at("srcT", [128, CCH * T], BF16, O_S0)
    attnT = at("attnT", [128, CCH * T], BF16, O_S0)
    QTt = at("QTt", [128, CCH * T], BF16, O_S1)
    sW1g = [at("sW1g0", [128, CCH * 1024], BF16, O_S1),
            at("sW1g1", [128, CCH * 1024], BF16, O_S2)]
    sW2g = [at("sW2g0", [128, 8 * 1024], BF16, O_S1 + 16 * KB),
            at("sW2g1", [128, 8 * 1024], BF16, O_S2 + 16 * KB)]
    sWo = at("sWo", [128, CCH * 1024], BF16, O_S1)
    KTt = at("KTt", [128, CCH * T], BF16, O_S2)
    x1Tstg = at("x1Tstg", [128, CCH * T], BF16, O_S2)
    VA = at("VA", [128, NTILE * H * 65], BF16, O_S3)
    sxt = [at("sxt0", [128, CCH * 512], BF16, O_S3),
           at("sxt1", [128, CCH * 512], BF16, O_S3 + 8 * KB)]
    s_b2b = at("s_b2b", [128, C], F32, O_S3 + 16 * KB)
    h_sb = at("h_sb", [128, 8 * 512], BF16, O_S3 + 20 * KB)
    s_cb = [at(f"s_cb{i}", [128, C], F32, O_S3 + i * 4 * KB) for i in range(4)]
    sWs = at("sWs", [128, CCH * 1024], BF16, O_S4)
    rrow = at("rrow", [1, 512], F32, O_S4)
    sqh = at("sqh", [64, T], BF16, O_S4)
    lnq2 = at("lnq2", [1, T], BF16, O_S4 + 17 * KB)
    k2r = at("k2r", [1, T], BF16, O_S4 + 21 * KB)
    b8r = at("b8r", [1, T], BF16, O_S4 + 25 * KB)
    s_kv = at("s_kv", [1, 8], F32, O_S4 + 29 * KB)
    ytmp = [at("ytmp0", [128, 4 * C], F32, O_S4),
            at("ytmp1", [128, 4 * C], F32, O_S4 + 16 * KB)]
    stgQ = [at("stgQ0", [65, T], BF16, O_S5),
            at("stgQ1", [65, T], BF16, O_S5 + 4 * KB)]
    stgK = [at("stgK0", [65, T], BF16, O_S5 + 8 * KB),
            at("stgK1", [65, T], BF16, O_S5 + 12 * KB)]
    PTb = [at("PTb0", [128, 512], BF16, O_S5 + 16 * KB),
           at("PTb1", [128, 512], BF16, O_S5 + 17 * KB)]
    rb_sb = at("rb_sb", [64, 512], BF16, O_S5 + 18 * KB)
    rowb = at("rowb", [1, 512], BF16, O_S5 + 19 * KB)
    bv_bf = at("bv_bf", [1, C], BF16, O_S5 + 20 * KB)
    s_masks = at("s_masks", [128, 256], F32, O_S5 + 22 * KB)
    sx_t = [at("sx_t0", [128, C], F32, O_S6),
            at("sx_t1", [128, C], F32, O_S6 + 4 * KB)]
    t_t = at("t_t", [128, C], F32, O_S6 + 8 * KB)
    u_t = at("u_t", [128, C], F32, O_S6 + 12 * KB)
    w_t = at("w_t", [128, C], F32, O_S6 + 16 * KB)
    xb_t = at("xb_t", [128, C], BF16, O_S6 + 20 * KB)
    t_t2 = at("t_t2", [128, C], F32, O_S4)
    u_t2 = at("u_t2", [128, C], F32, O_S4 + 4 * KB)
    w_t2 = at("w_t2", [128, C], F32, O_S4 + 8 * KB)
    xb_t2 = at("xb_t2", [128, C], BF16, O_S4 + 12 * KB)
    qt_t2 = at("qt_t2", [128, C], F32, O_S4 + 17 * KB)
    qu_t2 = at("qu_t2", [128, C], F32, O_S4 + 21 * KB)
    qxb_t2 = at("qxb_t2", [128, C], BF16, O_S4 + 25 * KB)
    rb_f32 = at("rb_f32", [64, 512], F32, O_S4 + 14 * KB)
    SCR = [dict(t=t_t, u=u_t, w=w_t, xb=xb_t),
           dict(t=t_t2, u=u_t2, w=w_t2, xb=xb_t2)]
    QSCR = [dict(t=t_t, u=u_t, xb=xb_t),
            dict(t=qt_t2, u=qu_t2, xb=qxb_t2)]

    psum = [nc.alloc_psum_tensor(f"pb{i}", [128, 512], F32) for i in range(8)]

    P = Prog(nc)
    BT = P.bank_tok

    def bank_wait(b):
        return BT.get(b)

    def bank_done(b, tok):
        BT[b] = tok

    def ld_bcast_row(dram_row, dest, waits=()):
        n = dram_row.shape[-1]
        return P.dma("sp", lambda e, d=dram_row, o=dest, n=n: e.dma_start(
            out=o[:, 0:n], in_=d[0:1, 0:n].broadcast_to([128, n])), waits=waits)

    def ln_cols(x_ap, eps, waits=()):
        mu = s_tiny[:, 0:1]
        s2 = s_tiny[:, 8:9]
        l2 = s_tiny[:, 16:17]
        rstd = s_tiny[:, 24:25]
        P.dve(lambda e: e.reduce_sum(mu, x_ap, axis=AX.X), waits=list(waits),
              drain=True)
        P.dve(lambda e: e.tensor_scalar_mul(mu, mu, float(1.0 / C)), drain=True)
        P.dve(lambda e: e.tensor_scalar(t_t[:], x_ap, mu, None, op0=ALU.subtract),
              drain=True)
        tv = P.tok("dve")
        tq = P.act(lambda e: e.activation(u_t[:], t_t[:], AF.Square), waits=[tv])
        tr = P.dve(lambda e: e.reduce_sum(s2, u_t[:], axis=AX.X), waits=[tq])
        P.act(lambda e: e.activation(l2, s2, AF.Ln, bias=f32(eps),
                                     scale=f32(1.0 / C)), waits=[tr], drain=True)
        P.act(lambda e: e.activation(rstd, l2, AF.Exp, scale=f32(-0.5)),
              drain=True)
        return P.tok("act")

    def transpose_chunks(src_bf, dst, dst_col0, waits):
        for cc in range(CCH):
            b = 4 + (cc % 4)
            pv = psum[b][:, 0:128].bitcast(BF16)[:, 0:128]
            w = [bank_wait(b)] + list(waits)
            tk = P.pe(lambda e, pv=pv, cc=cc, s=src_bf: e.transpose(
                pv, s[:, cc * 128:(cc + 1) * 128], s_ident[:]),
                waits=w, inc="pe")
            tc = P.dve(lambda e, pv=pv, c=dst_col0 + cc * T, d=dst:
                       e.tensor_copy(d[:, c:c + 128], pv), waits=[tk])
            bank_done(b, tc)
        return P.tok("dve")

    def stream_w(dram_w, dest, ncols=1024, col0=0, waits=()):
        return P.dma("pool", lambda e: e.dma_start(
            out=dest[:, 0:CCH * ncols].rearrange("p (c f) -> p c f", c=CCH),
            in_=dram_w[:, col0:col0 + ncols].rearrange("(c p) f -> p c f", p=128)),
            waits=waits)

    def emit_lnp_src(src_dram, be, spill, first_wait=None):
        backs = []

        def emit_back(st):
            par = st["par"]
            S = SCR[par]
            co = 16 * par
            rstd = s_tiny[:, co + 3:co + 4]
            P.dve(lambda e, S=S, rstd=rstd: e.tensor_scalar_mul(
                S["xb"][:], S["t"][:], rstd), waits=[st["ta"]])
            if spill:
                tsp = P.dve(lambda e, S=S, rstd=rstd: e.tensor_scalar_mul(
                    S["w"][:], S["t"][:], rstd))
                P.dma("sp", lambda e, r=st["r0"], S=S: e.dma_start(
                    out=d_src[r:r + 128, :], in_=S["w"][:]), waits=[tsp])
            tv = P.tok("dve")
            transpose_chunks(S["xb"][:], srcT, st["i"] * 128, [tv])

        for i in range(NTILE):
            r0 = (be * NTILE + i) * 128
            par = i % 2
            S = SCR[par]
            co = 16 * par
            mu = s_tiny[:, co + 0:co + 1]
            s2 = s_tiny[:, co + 1:co + 2]
            l2 = s_tiny[:, co + 2:co + 3]
            rstd = s_tiny[:, co + 3:co + 4]
            lw = [P.tok("dve")]
            if first_wait is not None and i == 0:
                lw.append(first_wait)
            tld = P.dma("sp", lambda e, r=r0, p=par: e.dma_start(
                out=sx_t[p][:], in_=src_dram[r:r + 128, :]), waits=lw)
            P.dve(lambda e, p=par, mu=mu: e.reduce_sum(mu, sx_t[p][:], axis=AX.X),
                  waits=[tld], drain=True)
            P.dve(lambda e, mu=mu: e.tensor_scalar_mul(mu, mu, float(1.0 / C)),
                  drain=True)
            P.dve(lambda e, p=par, S=S, mu=mu: e.tensor_scalar(
                S["t"][:], sx_t[p][:], mu, None, op0=ALU.subtract), drain=True)
            tv = P.tok("dve")
            tq = P.act(lambda e, S=S: e.activation(S["u"][:], S["t"][:],
                                                   AF.Square), waits=[tv])
            if backs:
                emit_back(backs.pop(0))
            tr = P.dve(lambda e, S=S, s2=s2: e.reduce_sum(
                s2, S["u"][:], axis=AX.X), waits=[tq], drain=True)
            P.act(lambda e, s2=s2, l2=l2: e.activation(
                l2, s2, AF.Ln, bias=f32(1e-5), scale=f32(1.0 / C)),
                waits=[tr], drain=True)
            P.act(lambda e, l2=l2, rstd=rstd: e.activation(
                rstd, l2, AF.Exp, scale=f32(-0.5)), drain=True)
            ta = P.tok("act")
            backs.append(dict(i=i, par=par, ta=ta, r0=r0))
        while backs:
            emit_back(backs.pop(0))
        return P.tok("dve")

    def emit_ffn_qkv(be):
        pe0 = P.tok("pe")
        bvl = P.dma("pool", lambda e: e.dma_start(out=bv_bf[:], in_=wf["bv"][:]),
                    waits=[pe0])
        c0 = P.dma("sp", lambda e: e.dma_start(
            out=s_cols[:, 0:8], in_=wf["bq"][:]), waits=[pe0])
        c1 = P.dma("sp", lambda e: e.dma_start(
            out=s_cols[:, 8:16], in_=wf["bk"][:]), waits=[pe0])
        srcT_tok = P.tok("dve")
        for (wname, dst, b0) in (("Wq", QTt, 0), ("Wk", KTt, 8)):
            tw = stream_w(wf[wname], sWs, waits=[P.tok("pe")])
            for fc in range(CCH):
                for tb in range(4):
                    bnk = (fc * 4 + tb) % 4
                    w0 = [bank_wait(bnk)]
                    if fc == 0 and tb == 0:
                        w0 += [tw, srcT_tok, c0, c1]
                    for cc in range(CCH):
                        P.pe(lambda e, fc=fc, tb=tb, cc=cc, bnk=bnk:
                             e.matmul(psum[bnk][:],
                                      sWs[:, cc * 1024 + fc * 128:
                                          cc * 1024 + fc * 128 + 128],
                                      srcT[:, cc * T + tb * 512:
                                           cc * T + tb * 512 + 512],
                                      start=(cc == 0), stop=(cc == CCH - 1)),
                             waits=w0 if cc == 0 else [],
                             inc="pe" if cc == CCH - 1 else None)
                    tp = P.tok("pe")
                    te = P.dve(lambda e, fc=fc, tb=tb, bnk=bnk, dst=dst, b0=b0:
                               e.tensor_scalar(
                                   dst[:, fc * T + tb * 512:fc * T + tb * 512 + 512],
                                   psum[bnk][:], s_cols[:, b0 + fc:b0 + fc + 1],
                                   None, op0=ALU.add), waits=[tp])
                    bank_done(bnk, te)
        tw = stream_w(wf["Wv"], sWs, waits=[P.tok("pe")])
        tmo = P.dve(lambda e: e.memset(
            VA[:].rearrange("p (t h e2) -> p t h e2", t=NTILE, h=H)
            [:, :, :, 64:65], 1.0))
        for i in range(NTILE):
            for fb in range(2):
                bnk = (i * 2 + fb) % 4
                w0 = [bank_wait(bnk)]
                if i == 0 and fb == 0:
                    w0 += [tw, tmo, bvl]
                for cc in range(CCH):
                    P.pe(lambda e, i=i, fb=fb, cc=cc, bnk=bnk:
                         e.matmul(psum[bnk][:],
                                  srcT[:, cc * T + i * 128:cc * T + i * 128 + 128],
                                  sWs[:, cc * 1024 + fb * 512:
                                      cc * 1024 + fb * 512 + 512],
                                  start=(cc == 0), stop=False),
                         waits=w0 if cc == 0 else [],
                         inc=None)
                P.pe(lambda e, fb=fb, bnk=bnk:
                     e.matmul(psum[bnk][:], s_ones[0:1, 0:128],
                              bv_bf[0:1, fb * 512:fb * 512 + 512],
                              start=False, stop=True), inc="pe")
                tp = P.tok("pe")
                te = P.dve(lambda e, i=i, fb=fb, bnk=bnk: e.tensor_copy(
                    VA[:].rearrange("p (t h e2) -> p t h e2", t=NTILE, h=H)
                    [:, i, fb * 8:fb * 8 + 8, 0:64],
                    psum[bnk][:].rearrange("p (h d) -> p h d", h=8)),
                    waits=[tp])
                bank_done(bnk, te)
        return P.tok("dve")

    def emit_stage_head(h, par, shift=20.0, prev=None):
        cc = h // 2
        half = 64 * (h % 2)
        if prev is None:
            prev = P.tok("pe")
        P.dve(lambda e: e.tensor_copy(
            stgQ[par][0:64, :], QTt[half:half + 64, cc * T:(cc + 1) * T]),
            waits=[prev])
        P.dve(lambda e: e.tensor_copy(
            stgK[par][0:64, :], KTt[half:half + 64, cc * T:(cc + 1) * T]))
        P.dve(lambda e: e.memset(stgK[par][64:65, :], 1.0))
        tcp = P.tok("dve")
        for (stg, dstrow, ln_out) in ((stgQ, lnq2, True), (stgK, k2r, False)):
            tsq = P.act(lambda e, stg=stg: e.activation(
                sqh[0:64, :], stg[par][0:64, :], AF.Square),
                waits=[tcp, P.tok("pe")])
            for jq in range(4):
                pm = P.pe(lambda e, jq=jq: e.matmul(
                    psum[7][0:1, jq * 128:jq * 128 + 512][:, 0:512]
                    if False else psum[7][0:1, 0:512],
                    s_ones[0:64, 0:1], sqh[0:64, jq * 512:(jq + 1) * 512],
                    start=True, stop=True),
                    waits=[tsq, bank_wait(7)], inc="pe")
                if ln_out:
                    tc = P.act(lambda e, jq=jq: e.activation(
                        lnq2[0:1, jq * 512:(jq + 1) * 512], psum[7][0:1, 0:512],
                        AF.Ln), waits=[pm], drain=(jq == 3))
                else:
                    tc = P.act(lambda e, jq=jq: e.activation(
                        k2r[0:1, jq * 512:(jq + 1) * 512], psum[7][0:1, 0:512],
                        AF.Copy, bias=0.0), waits=[pm], drain=(jq == 3))
                bank_done(7, tc)
        tk2 = P.tok("act")
        k2m = s_kv[0:1, 0:1]
        hlf = s_kv[0:1, 1:2]
        P.dve(lambda e: e.reduce_max(k2m, k2r[:], axis=AX.X), waits=[tk2], drain=True)
        tvm = P.tok("dve")
        P.act(lambda e: e.activation(hlf, k2m, AF.Ln), waits=[tvm], drain=True)
        ta = P.tok("act")
        P.dve(lambda e: e.tensor_scalar_mul(hlf, hlf, float(0.5)), waits=[ta], drain=True)
        tvh = P.tok("dve")
        P.act(lambda e: e.activation(b8r[:], lnq2[:], AF.Exp, scale=f32(0.5),
                                     bias=hlf), waits=[tvh], drain=True)
        tb8 = P.tok("act")
        P.dve(lambda e: e.tensor_scalar(
            stgQ[par][64:65, :], b8r[:], float(-1.0), float(8.0 * shift),
            op0=ALU.mult, op1=ALU.add), waits=[tb8], drain=True)
        return P.tok("dve")

    def emit_flash_head(h, par, stg_tok, local):
        cc = h // 2
        half = 64 * (h % 2)

        def evac_chain(ob, qlo, N, tpv, trb_prev=None):
            w0 = [tpv]
            if trb_prev is not None:
                w0.append(trb_prev)
            tvr = P.dve(lambda e, ob=ob, N=N: e.tensor_copy(
                rowb[0:1, 0:N], psum[ob][64:65, 0:N]), waits=w0, drain=True)
            trb = P.pe(lambda e, N=N: e.matmul(
                psum[6][0:64, 0:N], s_ones[0:1, 0:64], rowb[0:1, 0:N],
                start=True, stop=True), waits=[tvr, bank_wait(6)], inc="pe")
            tcb = P.dve(lambda e, N=N: e.reciprocal(
                rb_f32[:, 0:N], psum[6][0:64, 0:N]), waits=[trb], drain=True)
            bank_done(6, tcb)
            ev = P.dve(lambda e, ob=ob, N=N, qlo=qlo: e.tensor_tensor(
                attnT[half:half + 64, cc * T + qlo:cc * T + qlo + N],
                psum[ob][0:64, 0:N], rb_f32[:, 0:N], op=ALU.mult),
                waits=[tcb])
            bank_done(ob, ev)
            return trb

        if not local:
            pending = None
            for (u, qlo, N, kcs) in [(qb, qb * 512, 512, list(range(16)))
                                     for qb in range(4)]:
                ob = 2 + (u % 2)
                exp_toks = {}
                pv_toks = {}
                for j, kc in enumerate(kcs):
                    sb = j % 2
                    stw = [bank_wait(sb)]
                    if j == 0:
                        stw.append(stg_tok)
                    st_tok = P.pe(
                        lambda e, kc=kc, sb=sb, qlo=qlo, N=N: e.matmul(
                            psum[sb][0:128, 0:N],
                            stgK[par][:, kc * 128:(kc + 1) * 128],
                            stgQ[par][:, qlo:qlo + N],
                            start=True, stop=True),
                        waits=stw, inc="pe")
                    ew = [st_tok]
                    if j >= 2:
                        ew.append(pv_toks[kcs[j - 2]])
                    tE = P.act(lambda e, sb=sb, N=N: e.activation(
                        PTb[sb][:, 0:N], psum[sb][0:128, 0:N], AF.Exp,
                        scale=f32(0.125)), waits=ew)
                    bank_done(sb, tE)
                    exp_toks[kc] = tE
                    pw = [tE]
                    if j == 0:
                        pw.append(bank_wait(ob))
                    pv_toks[kc] = P.pe(
                        lambda e, kc=kc, sb=sb, ob=ob, N=N,
                        st_=(kc == kcs[0]), sp_=(kc == kcs[-1]): e.matmul(
                            psum[ob][0:65, 0:N],
                            VA[:, kc * (H * 65) + h * 65:kc * (H * 65) + h * 65 + 65],
                            PTb[sb][:, 0:N],
                            start=st_, stop=sp_),
                        waits=pw, inc="pe")
                tpv = P.tok("pe")
                if pending is not None:
                    pending()
                pending = (lambda ob=ob, qlo=qlo, N=N, tpv=tpv:
                           evac_chain(ob, qlo, N, tpv))
            pending()
            return P.tok("dve")

        N = LS
        pt_slot = {}
        for gpar in range(2):
            for j in range(2):
                pt_slot[(gpar, j)] = PTb[gpar][:, j * 256:j * 256 + 256]
        pv_last = {}
        pending = []
        trb_prev = None
        for gp in range(0, 8, 2):
            st_toks = {}
            for g in (gp, gp + 1):
                stb = (0, 1) if g % 2 == 0 else (4, 5)
                for j in range(2):
                    kc = g * 2 + j
                    qlo = g * LS
                    stw = [bank_wait(stb[j])]
                    if g == 0 and j == 0:
                        stw.append(stg_tok)
                    st_toks[(g, j)] = P.pe(
                        lambda e, kc=kc, b=stb[j], qlo=qlo: e.matmul(
                            psum[b][0:128, 0:N],
                            stgK[par][:, kc * 128:(kc + 1) * 128],
                            stgQ[par][:, qlo:qlo + N],
                            start=True, stop=True),
                        waits=stw, inc="pe")
            mask_toks = {}
            for g in (gp, gp + 1):
                stb = (0, 1) if g % 2 == 0 else (4, 5)
                for j in range(2):
                    slot = (g % 2, j)
                    ew = [st_toks[(g, j)]]
                    if slot in pv_last:
                        ew.append(pv_last[slot])
                    tE = P.act(lambda e, b=stb[j], s=pt_slot[slot]: e.activation(
                        s, psum[b][0:128, 0:N], AF.Exp, scale=f32(0.125)),
                        waits=ew)
                    bank_done(stb[j], tE)
                    mask_toks[(g, j)] = P.dve(
                        lambda e, s=pt_slot[slot], j=j: e.tensor_tensor(
                            s, s, s_cmask[:, j * 256:j * 256 + 256], op=ALU.mult),
                        waits=[tE])
            for (ob_, qlo_, tpv_) in pending:
                trb_prev = evac_chain(ob_, qlo_, N, tpv_, trb_prev)
            pending = []
            for g in (gp, gp + 1):
                ob = 2 + (g % 2)
                for j in range(2):
                    kc = g * 2 + j
                    slot = (g % 2, j)
                    pw = [mask_toks[(g, j)]]
                    if j == 0:
                        pw.append(bank_wait(ob))
                    pv_last[slot] = P.pe(
                        lambda e, kc=kc, ob=ob, s=pt_slot[slot],
                        st_=(j == 0), sp_=(j == 1): e.matmul(
                            psum[ob][0:65, 0:N],
                            VA[:, kc * (H * 65) + h * 65:kc * (H * 65) + h * 65 + 65],
                            s, start=st_, stop=sp_),
                        waits=pw, inc="pe")
            tpv = P.tok("pe")
            pending = [(2 + (g % 2), g * LS, tpv) for g in (gp, gp + 1)]
        for (ob_, qlo_, tpv_) in pending:
            trb_prev = evac_chain(ob_, qlo_, N, tpv_, trb_prev)
        return P.tok("dve")

    def emit_ffn_post(be):
        tw = stream_w(wf["Wo"], sWo, waits=[P.tok("pe"), P.tok("dve")])
        bo_b = ld_bcast_row(wf["bo"], s_cb[0], waits=[P.tok("pe")])
        g1_b = ld_bcast_row(wf["ln1_g"], s_cb[1])
        b1_b = ld_bcast_row(wf["ln1_b"], s_cb[2])
        rw_b = ld_bcast_row(wf["router_w"], s_cb[3])
        rbc = P.dma("sp", lambda e: e.dma_start(
            out=s_cols[:, 48:49],
            in_=wf["router_b"][0:1, 0:1].broadcast_to([128, 1])))
        attnT_tok = P.tok("dve")
        prev_st = [None, None]
        backs = []

        def emit_back(st):
            par = st["par"]
            S = SCR[par]
            co = 16 * par
            rstd = s_tiny[:, co + 3:co + 4]
            keys = s_tiny[:, co + 4:co + 5]
            kq = s_tiny[:, co + 5:co + 6]; fi = s_tiny[:, co + 6:co + 7]
            fc_ = s_tiny[:, co + 7:co + 8]; km = s_tiny[:, co + 8:co + 9]
            ii = s_tiny[:, co + 9:co + 10].bitcast(I32)[:, 0:1]
            a1 = s_tiny[:, co + 10:co + 11]; a2 = s_tiny[:, co + 11:co + 12]
            a3 = s_tiny[:, co + 12:co + 13]
            gt = st["gt"]
            P.dve(lambda e, S=S: e.tensor_scalar_mul(S["u"][:], S["u"][:], rstd),
                  waits=[st["ta"]], drain=True)
            P.dve(lambda e, S=S: e.tensor_tensor(S["u"][:], S["u"][:], s_cb[1][:],
                                                 op=ALU.mult), drain=True)
            P.dve(lambda e, S=S: e.tensor_tensor(S["w"][:], S["u"][:], s_cb[2][:],
                                                 op=ALU.add), drain=True)
            P.dve(lambda e, S=S: e.tensor_tensor(S["u"][:], S["w"][:], s_cb[3][:],
                                                 op=ALU.mult), drain=True)
            P.dve(lambda e, S=S: e.reduce_sum(keys, S["u"][:], axis=AX.X),
                  drain=True)
            P.dve(lambda e: e.tensor_scalar_add(keys, keys, s_cols[:, 48:49]),
                  drain=True)
            P.dve(lambda e: e.tensor_scalar_mul(kq, keys, float(0.25)), drain=True)
            P.dve(lambda e: e.tensor_copy(ii, kq), drain=True)
            P.dve(lambda e: e.tensor_copy(fi, ii), drain=True)
            P.dve(lambda e: e.tensor_tensor(fc_, fi, kq, op=ALU.is_gt), drain=True)
            P.dve(lambda e: e.tensor_sub(fi, fi, fc_), drain=True)
            P.dve(lambda e: e.tensor_scalar(km, fi, float(-4.0), keys,
                                            op0=ALU.mult, op1=ALU.add), drain=True)
            P.dve(lambda e: e.tensor_single_scalar(a1, km, float(1.0),
                                                   op=ALU.is_ge))
            P.dve(lambda e: e.tensor_single_scalar(a2, km, float(2.0),
                                                   op=ALU.is_ge))
            P.dve(lambda e: e.tensor_single_scalar(a3, km, float(3.0),
                                                   op=ALU.is_ge), drain=True)
            P.dve(lambda e, g=gt: e.tensor_scalar(
                s_masks[:, g * 4:g * 4 + 1], a1, float(-1.0), float(1.0),
                op0=ALU.mult, op1=ALU.add))
            P.dve(lambda e, g=gt: e.tensor_sub(
                s_masks[:, g * 4 + 1:g * 4 + 2], a1, a2))
            P.dve(lambda e, g=gt: e.tensor_sub(
                s_masks[:, g * 4 + 2:g * 4 + 3], a2, a3))
            P.dve(lambda e, g=gt: e.tensor_copy(
                s_masks[:, g * 4 + 3:g * 4 + 4], a3), drain=True)
            tx = P.tok("dve")
            prev_st[par] = P.dma("sp", lambda e, r=st["r0"], S=S: e.dma_start(
                out=d_x1ff[r:r + 128, :], in_=S["w"][:]), waits=[tx])
            P.dve(lambda e, S=S: e.tensor_copy(S["xb"][:], S["w"][:]), drain=True)
            tv2 = P.tok("dve")
            transpose_chunks(S["xb"][:], x1Tstg, st["i"] * 128, [tv2])

        for i in range(NTILE):
            r0 = (be * NTILE + i) * 128
            gt = be * NTILE + i
            par = i % 2
            S = SCR[par]
            co = 16 * par
            mu = s_tiny[:, co + 0:co + 1]; s2 = s_tiny[:, co + 1:co + 2]
            l2 = s_tiny[:, co + 2:co + 3]; rstd = s_tiny[:, co + 3:co + 4]
            tld = P.dma("sp", lambda e, r=r0, p=par: e.dma_start(
                out=sx_t[p][:], in_=d_src[r:r + 128, :]),
                waits=[P.tok("dve"), P.tok("pe")] if i == 0 else [P.tok("dve")])
            for cb in range(2):
                bnk = 2 * par + cb
                w0 = [bank_wait(bnk)]
                if i == 0 and cb == 0:
                    w0 += [tw, bo_b, g1_b, b1_b, rw_b, rbc, attnT_tok]
                for ccc in range(CCH):
                    P.pe(lambda e, i=i, cb=cb, ccc=ccc, bnk=bnk:
                         e.matmul(psum[bnk][:],
                                  attnT[:, ccc * T + i * 128:ccc * T + i * 128 + 128],
                                  sWo[:, ccc * 1024 + cb * 512:
                                      ccc * 1024 + cb * 512 + 512],
                                  start=(ccc == 0), stop=(ccc == CCH - 1)),
                         waits=w0 if ccc == 0 else [],
                         inc="pe" if ccc == CCH - 1 else None)
                tp = P.tok("pe")
                te = P.dve(lambda e, cb=cb, bnk=bnk, S=S: e.tensor_tensor(
                    S["t"][:, cb * 512:cb * 512 + 512], psum[bnk][:],
                    s_cb[0][:, cb * 512:cb * 512 + 512], op=ALU.add),
                    waits=[tp], drain=True)
                bank_done(bnk, te)
            P.dve(lambda e, p=par, S=S: e.tensor_tensor(
                S["t"][:], S["t"][:], sx_t[p][:], op=ALU.add), waits=[tld],
                drain=True)
            P.dve(lambda e, S=S, mu=mu: e.reduce_sum(mu, S["t"][:], axis=AX.X),
                  drain=True)
            P.dve(lambda e, mu=mu: e.tensor_scalar_mul(mu, mu, float(1.0 / C)),
                  drain=True)
            P.dve(lambda e, S=S, mu=mu: e.tensor_scalar(
                S["u"][:], S["t"][:], mu, None, op0=ALU.subtract), drain=True)
            tv = P.tok("dve")
            pw = [tv] + ([prev_st[par]] if prev_st[par] else [])
            tq = P.act(lambda e, S=S: e.activation(S["w"][:], S["u"][:],
                                                   AF.Square), waits=pw)
            if backs:
                emit_back(backs.pop(0))
            tr = P.dve(lambda e, S=S, s2=s2: e.reduce_sum(
                s2, S["w"][:], axis=AX.X), waits=[tq], drain=True)
            P.act(lambda e, s2=s2, l2=l2: e.activation(
                l2, s2, AF.Ln, bias=f32(1e-5), scale=f32(1.0 / C)),
                waits=[tr], drain=True)
            P.act(lambda e, l2=l2, rstd=rstd: e.activation(
                rstd, l2, AF.Exp, scale=f32(-0.5)), drain=True)
            ta = P.tok("act")
            backs.append(dict(i=i, par=par, ta=ta, r0=r0, gt=gt))
        while backs:
            emit_back(backs.pop(0))
        tdone = P.tok("dve")
        return P.dma("sp", lambda e, b=be: e.dma_start(
            out=d_x1T[b].rearrange("c p t -> p c t"),
            in_=x1Tstg[:].rearrange("p (c t) -> p c t", c=CCH)), waits=[tdone])

    def emit_moe():
        x1t_tok = P.tok("ld")
        mask_tok = P.tok("dve")
        pe_flush = P.tok("pe")
        last_acc = None
        gidx = 0
        for e_i in range(NE):
            b2tok = ld_bcast_row(wf["b2"][e_i:e_i + 1, :], s_b2b,
                                 waits=[pe_flush if e_i == 0 else P.tok("dve")])
            b1tok = P.dma("sp", lambda e, e_i=e_i: e.dma_start(
                out=s_cols[:, 16:48], in_=wf["b1"][e_i]),
                waits=[pe_flush] if e_i == 0 else [])
            for g in range(NE):
                par = gidx % 2
                wv = [P.tok("pe"), x1t_tok]
                tw1 = P.dma("pool", lambda e, e_i=e_i, g=g, par=par: e.dma_start(
                    out=sW1g[par][:].rearrange("p (c f) -> p c f", c=CCH),
                    in_=wf["W1"][e_i].rearrange("(c p) f -> p c f", p=128)
                    [:, :, g * 1024:(g + 1) * 1024]), waits=wv)
                tw2 = P.dma("pool", lambda e, e_i=e_i, g=g, par=par: e.dma_start(
                    out=sW2g[par][:].rearrange("p (f c) -> p f c", f=8),
                    in_=wf["W2"][e_i][g * 1024:(g + 1) * 1024, :]
                    .rearrange("(fc p) c -> p fc c", p=128)), waits=wv)
                for tb in range(16):
                    be = tb // 4
                    tl = tb % 4
                    xp = tb % 2
                    tx = P.dma("sp", lambda e, be=be, tl=tl, xp=xp: e.dma_start(
                        out=sxt[xp][:].rearrange("p (c t) -> p c t", c=CCH),
                        in_=d_x1T[be].rearrange("c p t -> p c t")
                        [:, :, tl * 512:tl * 512 + 512]),
                        waits=[P.tok("pe"), x1t_tok])
                    th_ffc = {}
                    for ffc in range(8):
                        bnk = 5 + (ffc % 3)
                        w0 = [bank_wait(bnk), tw1, tx, b1tok]
                        for ccc in range(CCH):
                            P.pe(lambda e, ffc=ffc, ccc=ccc, bnk=bnk, par=par, xp=xp:
                                 e.matmul(psum[bnk][:],
                                          sW1g[par][:, ccc * 1024 + ffc * 128:
                                                    ccc * 1024 + ffc * 128 + 128],
                                          sxt[xp][:, ccc * 512:ccc * 512 + 512],
                                          start=(ccc == 0), stop=(ccc == CCH - 1)),
                                 waits=w0 if ccc == 0 else [],
                                 inc="pe" if ccc == CCH - 1 else None)
                        tp = P.tok("pe")
                        th = P.act(lambda e, ffc=ffc, bnk=bnk, g=g:
                                   e.activation(h_sb[:, ffc * 512:ffc * 512 + 512],
                                                psum[bnk][:], AF.Relu,
                                                bias=s_cols[:, 16 + g * 8 + ffc:
                                                            17 + g * 8 + ffc],
                                                scale=f32(1.0)), waits=[tp])
                        bank_done(bnk, th)
                        th_ffc[ffc] = th
                    th_all = P.tok("act")
                    for tsub in range(4):
                        for cb in range(2):
                            bnk = (tsub * 2 + cb) % 4
                            w0 = [bank_wait(bnk), tw2]
                            if tsub == 0 and cb == 0:
                                w0.append(P.tok("sw"))
                            for ffc in range(8):
                                fw = list(w0) if ffc == 0 else []
                                if ffc == 0:
                                    fw.append(th_ffc[0])
                                elif tsub == 0 and cb == 0:
                                    fw.append(th_ffc[ffc])
                                P.pe(lambda e, ts=tsub, cb=cb, ffc=ffc, bnk=bnk, par=par:
                                     e.matmul(psum[bnk][:],
                                              h_sb[:, ffc * 512 + ts * 128:
                                                   ffc * 512 + ts * 128 + 128],
                                              sW2g[par][:, ffc * 1024 + cb * 512:
                                                        ffc * 1024 + cb * 512 + 512],
                                              start=(ffc == 0), stop=(ffc == 7)),
                                     waits=fw,
                                     inc="pe" if ffc == 7 else None)
                            tp = P.tok("pe")
                            gt = tb * 4 + tsub
                            mcol = s_masks[:, gt * 4 + e_i:gt * 4 + e_i + 1]
                            dsl = (tsub * C + cb * 512, tsub * C + cb * 512 + 512)
                            if g == NE - 1:
                                P.dve(lambda e, bnk=bnk, xp=xp, dsl=dsl:
                                      e.tensor_tensor(
                                          ytmp[xp][:, dsl[0]:dsl[1]], psum[bnk][:],
                                          s_b2b[:, (dsl[0] % C):(dsl[0] % C) + 512],
                                          op=ALU.add),
                                      waits=[tp, b2tok, mask_tok], drain=True)
                                te = P.dve(lambda e, xp=xp, dsl=dsl, m=mcol:
                                           e.tensor_scalar_mul(
                                               ytmp[xp][:, dsl[0]:dsl[1]],
                                               ytmp[xp][:, dsl[0]:dsl[1]], m))
                            else:
                                te = P.dve(lambda e, bnk=bnk, xp=xp, dsl=dsl, m=mcol:
                                           e.tensor_scalar(
                                               ytmp[xp][:, dsl[0]:dsl[1]],
                                               psum[bnk][:], m, None, op0=ALU.mult),
                                           waits=[tp, mask_tok])
                            bank_done(bnk, te)
                    ty = P.tok("dve")
                    r0 = tb * 512
                    last_acc = P.dma("pool", lambda e, r=r0, xp=xp: e.dma_start(
                        out=d_x1ff[r:r + 512, :].rearrange("(s p) c -> p s c", p=128),
                        in_=ytmp[xp][:].rearrange("p (s c) -> p s c", s=4),
                        accum_op=ALU.add), waits=[ty, x1t_tok])
                gidx += 1
        return last_acc

    def emit_ln2_resid(res_src, res_dst):
        acc_done = P.tok("sw")
        g2_b = ld_bcast_row(wf["ln2_g"], s_cb[1], waits=[P.tok("dve")])
        b2_b = ld_bcast_row(wf["ln2_b"], s_cb[2])
        prev_st = [None, None]
        backs = []
        last = [None]

        def emit_back(st):
            par = st["par"]
            S = SCR[par]
            co = 16 * par
            rstd = s_tiny[:, co + 3:co + 4]
            P.dve(lambda e, S=S: e.tensor_scalar_mul(S["t"][:], S["t"][:], rstd),
                  waits=[st["ta"], st["txl"]], drain=True)
            P.dve(lambda e, S=S: e.tensor_tensor(S["t"][:], S["t"][:], s_cb[1][:],
                                                 op=ALU.mult), drain=True)
            P.dve(lambda e, S=S: e.tensor_tensor(S["t"][:], S["t"][:], s_cb[2][:],
                                                 op=ALU.add), drain=True)
            tdv = P.dve(lambda e, S=S: e.scalar_tensor_tensor(
                S["t"][:], S["t"][:], 0.5, S["u"][:], op0=ALU.mult, op1=ALU.add),
                drain=True)
            prev_st[par] = P.dma("sp", lambda e, r=st["r0"], S=S: e.dma_start(
                out=res_dst[r:r + 128, :], in_=S["t"][:]), waits=[tdv], inc="st")
            last[0] = prev_st[par]

        for i in range(B * NTILE):
            r0 = i * 128
            par = i % 2
            S = SCR[par]
            co = 16 * par
            mu = s_tiny[:, co + 0:co + 1]; s2 = s_tiny[:, co + 1:co + 2]
            l2 = s_tiny[:, co + 2:co + 3]; rstd = s_tiny[:, co + 3:co + 4]
            lw = [acc_done, g2_b, b2_b] if i == 0 else [P.tok("dve")]
            tld = P.dma("sp", lambda e, r=r0, p=par: e.dma_start(
                out=sx_t[p][:], in_=d_x1ff[r:r + 128, :]), waits=lw)
            txl = P.dma("sp", lambda e, r=r0, S=S: e.dma_start(
                out=S["u"][:], in_=res_src[r:r + 128, :]), waits=[P.tok("dve")])
            pw = [tld] + ([prev_st[par]] if prev_st[par] else [])
            P.dve(lambda e, p=par, mu=mu: e.reduce_sum(mu, sx_t[p][:], axis=AX.X),
                  waits=pw, drain=True)
            P.dve(lambda e, mu=mu: e.tensor_scalar_mul(mu, mu, float(1.0 / C)),
                  drain=True)
            P.dve(lambda e, p=par, S=S, mu=mu: e.tensor_scalar(
                S["t"][:], sx_t[p][:], mu, None, op0=ALU.subtract), drain=True)
            tv = P.tok("dve")
            tq = P.act(lambda e, S=S: e.activation(S["w"][:], S["t"][:],
                                                   AF.Square), waits=[tv])
            if backs:
                emit_back(backs.pop(0))
            tr = P.dve(lambda e, S=S, s2=s2: e.reduce_sum(
                s2, S["w"][:], axis=AX.X), waits=[tq], drain=True)
            P.act(lambda e, s2=s2, l2=l2: e.activation(
                l2, s2, AF.Ln, bias=f32(1e-5), scale=f32(1.0 / C)),
                waits=[tr], drain=True)
            P.act(lambda e, l2=l2, rstd=rstd: e.activation(
                rstd, l2, AF.Exp, scale=f32(-0.5)), drain=True)
            ta = P.tok("act")
            backs.append(dict(i=i, par=par, ta=ta, r0=r0, txl=txl))
        while backs:
            emit_back(backs.pop(0))
        return last[0]

    def emit_attn_qkv(be):
        lnw_b = ld_bcast_row(wa["ln_w"], s_cb[0], waits=[P.tok("pe")])
        srcT_tok = P.tok("dve")
        for (col0, dst) in ((0, QTt), (C, KTt)):
            tw = stream_w(wa["w_qkv"], sWs, col0=col0,
                          waits=[P.tok("pe"), P.tok("dve")])
            backs = []

            def emit_back(st, dst=dst):
                par = st["par"]
                S = QSCR[par]
                co = 16 * par
                rstd = s_tiny[:, co + 3:co + 4]
                P.dve(lambda e, S=S, rstd=rstd: e.scalar_tensor_tensor(
                    S["xb"][:], S["t"][:], rstd, s_cb[0][:],
                    op0=ALU.mult, op1=ALU.mult), waits=[st["ta"]], drain=True)
                tv2 = P.tok("dve")
                transpose_chunks(S["xb"][:], dst, st["i"] * 128, [tv2])

            for i in range(NTILE):
                par = i % 2
                S = QSCR[par]
                co = 16 * par
                s2 = s_tiny[:, co + 1:co + 2]; l2 = s_tiny[:, co + 2:co + 3]
                rstd = s_tiny[:, co + 3:co + 4]
                for fb in range(2):
                    bnk = 2 * par + fb
                    w0 = [bank_wait(bnk)]
                    if i == 0 and fb == 0:
                        w0 += [tw, srcT_tok, lnw_b]
                    for ccc in range(CCH):
                        P.pe(lambda e, i=i, fb=fb, ccc=ccc, bnk=bnk:
                             e.matmul(psum[bnk][:],
                                      srcT[:, ccc * T + i * 128:ccc * T + i * 128 + 128],
                                      sWs[:, ccc * 1024 + fb * 512:
                                          ccc * 1024 + fb * 512 + 512],
                                      start=(ccc == 0), stop=(ccc == CCH - 1)),
                             waits=w0 if ccc == 0 else [],
                             inc="pe" if ccc == CCH - 1 else None)
                    tp = P.tok("pe")
                    te = P.dve(lambda e, fb=fb, bnk=bnk, S=S: e.tensor_copy(
                        S["t"][:, fb * 512:fb * 512 + 512], psum[bnk][:]),
                        waits=[tp])
                    bank_done(bnk, te)
                tv = P.tok("dve")
                tq = P.act(lambda e, S=S: e.activation(S["u"][:], S["t"][:],
                                                       AF.Square), waits=[tv])
                if backs:
                    emit_back(backs.pop(0))
                tr = P.dve(lambda e, S=S, s2=s2: e.reduce_sum(
                    s2, S["u"][:], axis=AX.X), waits=[tq], drain=True)
                P.act(lambda e, s2=s2, l2=l2: e.activation(
                    l2, s2, AF.Ln, bias=f32(1e-6), scale=f32(1.0 / C)),
                    waits=[tr], drain=True)
                P.act(lambda e, l2=l2, rstd=rstd: e.activation(
                    rstd, l2, AF.Exp, scale=f32(-0.5)), drain=True)
                ta = P.tok("act")
                backs.append(dict(i=i, par=par, ta=ta))
            while backs:
                emit_back(backs.pop(0))
        tw = stream_w(wa["w_qkv"], sWs, col0=2 * C,
                      waits=[P.tok("pe"), P.tok("dve")])
        tmo = P.dve(lambda e: e.memset(
            VA[:].rearrange("p (t h e2) -> p t h e2", t=NTILE, h=H)
            [:, :, :, 64:65], 1.0))
        for i in range(NTILE):
            for fb in range(2):
                bnk = (i * 2 + fb) % 4
                w0 = [bank_wait(bnk)]
                if i == 0 and fb == 0:
                    w0 += [tw, tmo]
                for ccc in range(CCH):
                    P.pe(lambda e, i=i, fb=fb, ccc=ccc, bnk=bnk:
                         e.matmul(psum[bnk][:],
                                  srcT[:, ccc * T + i * 128:ccc * T + i * 128 + 128],
                                  sWs[:, ccc * 1024 + fb * 512:
                                      ccc * 1024 + fb * 512 + 512],
                                  start=(ccc == 0), stop=(ccc == CCH - 1)),
                         waits=w0 if ccc == 0 else [],
                         inc="pe" if ccc == CCH - 1 else None)
                tp = P.tok("pe")
                te = P.dve(lambda e, i=i, fb=fb, bnk=bnk: e.tensor_copy(
                    VA[:].rearrange("p (t h e2) -> p t h e2", t=NTILE, h=H)
                    [:, i, fb * 8:fb * 8 + 8, 0:64],
                    psum[bnk][:].rearrange("p (h d) -> p h d", h=8)),
                    waits=[tp])
                bank_done(bnk, te)
        return P.tok("dve")

    def emit_attn_proj(be):
        tw = stream_w(wa["w_proj"], sWo, waits=[P.tok("pe"), P.tok("dve")])
        attnT_tok = P.tok("dve")
        prev_st = None
        for i in range(NTILE):
            r0 = (be * NTILE + i) * 128
            p = i % 2
            tld = P.dma("sp", lambda e, r=r0, p=p: e.dma_start(
                out=sx_t[p][:], in_=d_resid[r:r + 128, :]), waits=[P.tok("dve")])
            for cb in range(2):
                bnk = cb
                w0 = [bank_wait(bnk)]
                if i == 0 and cb == 0:
                    w0 += [tw, attnT_tok]
                for ccc in range(CCH):
                    P.pe(lambda e, i=i, cb=cb, ccc=ccc, bnk=bnk:
                         e.matmul(psum[bnk][:],
                                  attnT[:, ccc * T + i * 128:ccc * T + i * 128 + 128],
                                  sWo[:, ccc * 1024 + cb * 512:
                                      ccc * 1024 + cb * 512 + 512],
                                  start=(ccc == 0), stop=(ccc == CCH - 1)),
                         waits=w0 if ccc == 0 else [],
                         inc="pe" if ccc == CCH - 1 else None)
                tp = P.tok("pe")
                pw = [tp, tld]
                if prev_st is not None:
                    pw.append(prev_st)
                te = P.dve(lambda e, cb=cb, bnk=bnk, p=p: e.tensor_tensor(
                    t_t[:, cb * 512:cb * 512 + 512], psum[bnk][:],
                    sx_t[p][:, cb * 512:cb * 512 + 512], op=ALU.add), waits=pw,
                    drain=True)
                bank_done(bnk, te)
            tdv = P.tok("dve")
            prev_st = P.dma("sp", lambda e, r=r0: e.dma_start(
                out=d_resid[r:r + 128, :], in_=t_t[:]), waits=[tdv], inc="st")
        return prev_st

    def dbg_dump(slabs=(), va=False, dram=None, masks=False):
        w = [P.tok("pe"), P.tok("dve"), P.tok("act"), P.tok("ld"), P.tok("sw")]
        for idx, s in enumerate(slabs):
            P.dma("sp", lambda e, s=s, i=idx: e.dma_start(
                out=dbg_slab[i], in_=s[:]), waits=w, inc="st")
        if va:
            P.dma("sp", lambda e: e.dma_start(out=dbg_va[:], in_=VA[:]),
                  waits=w, inc="st")
        if dram is not None:
            P.dma("sp", lambda e, d=dram: e.dma_start(out=dbg_f[:], in_=d[:]),
                  waits=w, inc="st")
        if masks:
            P.dma("sp", lambda e: e.dma_start(out=dbg_m[:], in_=s_masks[:]),
                  waits=w, inc="st")
        P.emit("sp", lambda e: e.nop(), waits=[P.tok("st")])
        P.realize()
        return nc

    P.dma("sp", lambda e: e.dma_start(out=s_ident[:], in_=ident_dr[:]))
    P.dma("sp", lambda e: e.dma_start(out=s_cmask[:], in_=cmask_dr[:]))
    ic = P.tok("ld")
    P.dve(lambda e: e.memset(s_ones[:], 1.0), waits=[ic])

    def ffn_call(res_src, res_dst, first_wait, tag=""):
        fw = first_wait
        for be in range(B):
            r0 = emit_lnp_src(res_src, be, spill=True, first_wait=fw)
            if upto == "f1_t0":
                return nc
            fw = None
            if upto == tag + "lnp":
                return dbg_dump(slabs=[srcT], dram=d_src)
            emit_ffn_qkv(be)
            if upto == tag + "qkv":
                return dbg_dump(slabs=[QTt, KTt], va=True)
            for h in range(H):
                stg = emit_stage_head(h, h % 2)
                emit_flash_head(h, h % 2, stg, local=False)
            if upto == tag + "flash":
                return dbg_dump(slabs=[attnT])
            emit_ffn_post(be)
            if upto == tag + "post":
                return dbg_dump(dram=d_x1ff, masks=True)
        if upto == tag + "post4":
            return dbg_dump(dram=d_x1ff, masks=True)
        emit_moe()
        if upto == tag + "moe":
            return dbg_dump(dram=d_x1ff)
        r = emit_ln2_resid(res_src, res_dst)
        if upto == tag + "ln2":
            return dbg_dump(dram=res_dst)
        return r

    def attn_call(first_wait):
        fw = first_wait
        for be in range(B):
            emit_lnp_src(d_resid, be, spill=False, first_wait=fw)
            fw = None
            if upto == "a_lnp":
                return dbg_dump(slabs=[srcT])
            emit_attn_qkv(be)
            if upto == "a_qkv":
                return dbg_dump(slabs=[QTt, KTt], va=True)
            flash_done = {0: None, 1: None}
            for h in range(H):
                stg = emit_stage_head(h, h % 2, prev=flash_done[h % 2])
                rr = emit_flash_head(h, h % 2, stg, local=True)
                flash_done[h % 2] = P.tok("pe")
                if rr is None:
                    return nc
            if upto == "a_flash":
                return dbg_dump(slabs=[attnT])
            emit_attn_proj(be)
        if upto == "a_proj":
            return dbg_dump(dram=d_resid)

    if upto is not None and upto == "aonly":
        import os as _os
        nheads = int(_os.environ.get("DBG_HEADS", "3"))
        tcp0 = P.dma("sp", lambda e: e.dma_start(out=d_resid[:], in_=xin[:]))
        emit_lnp_src(d_resid, 0, spill=False, first_wait=tcp0)
        emit_attn_qkv(0)
        for h in range(nheads):
            stg = emit_stage_head(h, h % 2)
            rr = emit_flash_head(h, h % 2, stg, local=True)
            if rr is None:
                return nc
        return dbg_dump(slabs=[attnT, QTt, KTt], va=True)

    r = ffn_call(xin, d_resid, P.tok("dve"), tag="f1_")
    if upto is not None and upto.startswith("f1_"):
        return r
    r = attn_call(P.tok("st"))
    if upto is not None and upto.startswith("a_"):
        return r
    r = ffn_call(d_resid, out, P.tok("st"), tag="f2_")
    if upto is not None and upto.startswith("f2_"):
        return r

    P.emit("sp", lambda e: e.nop(), waits=[P.tok("st"), P.tok("sw"), P.tok("ld")])
    P.realize()
    return nc


_NC_CACHE = {}


def kernel(**inputs):
    n = 8
    x = np.ascontiguousarray(np.asarray(inputs["L0_attn_x"], dtype=np.float32))
    shards = [np.ascontiguousarray(x[i * B:(i + 1) * B].reshape(NTOK, C))
              for i in range(n)]

    def g(name):
        return np.ascontiguousarray(np.asarray(inputs[name], dtype=np.float32))

    base = {}
    base["a_w_qkv"] = g("L0_attn_w_qkv")
    base["a_w_proj"] = g("L0_attn_w_proj")
    base["a_ln_w"] = g("L0_attn_ln_w").reshape(1, C)
    for nm in ("Wq", "Wk", "Wv", "Wo"):
        base["f_" + nm] = g("L0_ffn_" + nm)
    for nm in ("bv", "bo", "ln1_g", "ln1_b", "ln2_g", "ln2_b"):
        base["f_" + nm] = g("L0_ffn_" + nm).reshape(1, C)
    base["f_bq"] = np.ascontiguousarray(
        g("L0_ffn_bq").reshape(8, 128).T)
    base["f_bk"] = np.ascontiguousarray(
        g("L0_ffn_bk").reshape(8, 128).T)
    base["f_b1"] = np.ascontiguousarray(
        g("L0_ffn_b1").reshape(NE, 32, 128).transpose(0, 2, 1))
    base["f_router_w"] = g("L0_ffn_router_w").reshape(1, C)
    base["f_router_b"] = g("L0_ffn_router_b").reshape(1, 1)
    base["f_W1"] = g("L0_ffn_W1")
    base["f_W2"] = g("L0_ffn_W2")
    base["f_b2"] = g("L0_ffn_b2")

    if "nc" not in _NC_CACHE:
        _NC_CACHE["nc"] = build_program()
    nc = _NC_CACHE["nc"]
    in_maps = [dict(base, x=s) for s in shards]
    res = run_bass_kernel_spmd(nc, in_maps, core_ids=list(range(n)))
    outs = [np.asarray(res.results[i]["out"]).reshape(B, T, C) for i in range(n)]
    return np.concatenate(outs, axis=0).astype(np.float32)


# version 09
# speedup vs baseline: 1.1210x; 1.0233x over previous
import numpy as np
import ml_dtypes
import concourse.bass as bass
import concourse.mybir as mybir
from concourse.bass_utils import run_bass_kernel_spmd

F32 = mybir.dt.float32
BF16 = mybir.dt.bfloat16
I32 = mybir.dt.int32
AF = mybir.ActivationFunctionType
ALU = mybir.AluOpType
AX = mybir.AxisListType
f32 = np.float32

B = 4
T = 2048
C = 1024
H = 16
CCH = 8
NTILE = 16
DFF = 4096
NE = 4
LS = 256
NTOK = B * T

SBUF_BASE = 16384 + 768
SBUF_LIMIT = 16384 + 207 * 1024

ENGS = ("pe", "act", "dve", "pool", "sp")


class Prog:
    def __init__(self, nc):
        self.nc = nc
        self.ops = []
        self.count = {}
        self.sems = {}
        self.bank_tok = {}

    def sem(self, name):
        if name not in self.sems:
            self.sems[name] = self.nc.alloc_semaphore(name)
            self.count[name] = 0
        return self.sems[name]

    def emit(self, eng, fn, waits=(), inc=None, inc_n=1, drain=False):
        ws = []
        for w in waits:
            if w is None:
                continue
            s, v = w
            self.sem(s)
            assert v <= self.count[s], f"future wait {s}:{v}>{self.count[s]}"
            ws.append((s, v))
        self.ops.append((eng, fn, ws, inc, inc_n, drain))
        if inc is not None:
            self.sem(inc)
            self.count[inc] += inc_n
            return (inc, self.count[inc])
        return None

    def tok(self, sem):
        self.sem(sem)
        return (sem, self.count[sem])

    def pe(self, fn, waits=(), inc=None):
        return self.emit("pe", fn, waits, inc)

    def act(self, fn, waits=(), inc="act", drain=False):
        return self.emit("act", fn, waits, inc, 1, drain)

    def dve(self, fn, waits=(), inc="dve", drain=False):
        return self.emit("dve", fn, waits, inc, 1, drain)

    def dma(self, q, fn, waits=(), inc=None):
        inc = inc or ("ld" if q == "sp" else "sw")
        return self.emit(q, fn, waits, inc, 16)

    def realize(self):
        nc = self.nc
        last_wait = {}
        with nc.Block() as blk:
            def run_engine(e_name):
                def body(handle):
                    for (eng, fn, ws, inc, inc_n, drain) in self.ops:
                        if eng != e_name:
                            continue
                        for (s, v) in ws:
                            key = (e_name, s)
                            if last_wait.get(key, -1) >= v:
                                continue
                            handle.wait_ge(self.sems[s], v)
                            last_wait[key] = v
                        instr = fn(handle)
                        if inc is not None:
                            instr.then_inc(self.sems[inc], inc_n)
                        if drain:
                            handle.drain()
                return body
            blk.sync(run_engine("sp"))
            blk.gpsimd(run_engine("pool"))
            blk.tensor(run_engine("pe"))
            blk.scalar(run_engine("act"))
            blk.vector(run_engine("dve"))


def build_program(upto=None):
    nc = bass.Bass("TRN2", target_bir_lowering=False,
                   detect_race_conditions=False)

    xin = nc.dram_tensor("x", [NTOK, C], F32, kind="ExternalInput")
    out = nc.dram_tensor("out", [NTOK, C], F32, kind="ExternalOutput")
    wa, wf = {}, {}
    for n, shp in (("w_qkv", [C, 3 * C]), ("ln_w", [1, C]), ("w_proj", [C, C])):
        wa[n] = nc.dram_tensor("a_" + n, shp, F32, kind="ExternalInput")
    ffshapes = (("Wq", [C, C]), ("bq", [128, 8]), ("Wk", [C, C]), ("bk", [128, 8]),
                ("Wv", [C, C]), ("bv", [1, C]), ("Wo", [C, C]), ("bo", [1, C]),
                ("router_w", [1, C]), ("router_b", [1, 1]),
                ("W1", [NE, C, DFF]), ("b1", [NE, 128, 32]),
                ("W2", [NE, DFF, C]), ("b2", [NE, C]),
                ("ln1_g", [1, C]), ("ln1_b", [1, C]),
                ("ln2_g", [1, C]), ("ln2_b", [1, C]))
    for n, shp in ffshapes:
        wf[n] = nc.dram_tensor("f_" + n, shp, F32, kind="ExternalInput")

    d_resid = nc.dram_tensor("d_resid", [NTOK, C], F32, kind="Internal")
    d_src = nc.dram_tensor("d_src", [NTOK, C], F32, kind="Internal")
    d_x1ff = nc.dram_tensor("d_x1ff", [NTOK, C], F32, kind="Internal")
    d_x1T = nc.dram_tensor("d_x1T", [B, CCH, 128, T], BF16, kind="Internal")
    if upto is not None:
        dbg_slab = nc.dram_tensor("dbg_slab", [3, 128, CCH * T], BF16,
                                  kind="ExternalOutput")
        dbg_va = nc.dram_tensor("dbg_va", [128, NTILE * H * 65], BF16,
                                kind="ExternalOutput")
        dbg_f = nc.dram_tensor("dbg_f", [NTOK, C], F32, kind="ExternalOutput")
        dbg_m = nc.dram_tensor("dbg_m", [128, 256], F32, kind="ExternalOutput")

    ident_dr = nc.inline_tensor(np.eye(128, dtype=ml_dtypes.bfloat16),
                                name="ident_c")
    kk = np.arange(128)[:, None]
    qq = np.arange(LS)[None, :]
    cmask_np = np.concatenate([(kk <= qq), (kk + 128 <= qq)], axis=1)
    cmask_dr = nc.inline_tensor(
        np.ascontiguousarray(cmask_np.astype(ml_dtypes.bfloat16)), name="cmask_c")

    alloc_log = []

    def at(name, shape, dtype, off):
        per = int(np.prod(shape[1:])) * mybir.dt.size(dtype)
        assert SBUF_BASE <= off and off + per <= SBUF_LIMIT, \
            f"sbuf overflow {name}: [{off}, {off + per})"
        alloc_log.append((name, off, off + per))
        return nc.alloc_sbuf_tensor_at(name, list(shape), dtype, offset=off)

    KB = 1024
    O_PERS = SBUF_BASE
    s_ident = at("s_ident", [128, 128], BF16, O_PERS)
    s_cmask = at("s_cmask", [128, 512], BF16, O_PERS + 256)
    s_ones = at("s_ones", [128, 128], BF16, O_PERS + 1280)
    s_cols = at("s_cols", [128, 64], F32, O_PERS + 1536)
    s_tiny = at("s_tiny", [128, 32], F32, O_PERS + 1792)
    O_S0 = O_PERS + 2 * KB
    O_S1 = O_S0 + 32 * KB
    O_S2 = O_S1 + 32 * KB
    O_S3 = O_S2 + 32 * KB
    O_S4 = O_S3 + 34 * KB
    O_S5 = O_S4 + 29952
    O_S6 = O_S5 + 23 * KB
    O_END = O_S6 + 22 * KB
    assert O_END <= SBUF_LIMIT, O_END

    srcT = at("srcT", [128, CCH * T], BF16, O_S0)
    attnT = at("attnT", [128, CCH * T], BF16, O_S0)
    QTt = at("QTt", [128, CCH * T], BF16, O_S1)
    sW1g = [at("sW1g0", [128, CCH * 1024], BF16, O_S1),
            at("sW1g1", [128, CCH * 1024], BF16, O_S2)]
    sW2g = [at("sW2g0", [128, 8 * 1024], BF16, O_S1 + 16 * KB),
            at("sW2g1", [128, 8 * 1024], BF16, O_S2 + 16 * KB)]
    sWo = at("sWo", [128, CCH * 1024], BF16, O_S1)
    KTt = at("KTt", [128, CCH * T], BF16, O_S2)
    x1Tstg = at("x1Tstg", [128, CCH * T], BF16, O_S2)
    VA = at("VA", [128, NTILE * H * 65], BF16, O_S3)
    sxt = [at("sxt0", [128, CCH * 512], BF16, O_S3),
           at("sxt1", [128, CCH * 512], BF16, O_S3 + 8 * KB)]
    s_b2b = at("s_b2b", [128, C], F32, O_S3 + 16 * KB)
    h_sb = at("h_sb", [128, 8 * 512], BF16, O_S3 + 20 * KB)
    s_cb = [at(f"s_cb{i}", [128, C], F32, O_S3 + i * 4 * KB) for i in range(4)]
    sWs = at("sWs", [128, CCH * 1024], BF16, O_S4)
    rrow = at("rrow", [1, 512], F32, O_S4)
    sqh = at("sqh", [64, T], BF16, O_S4)
    lnq2 = at("lnq2", [1, T], BF16, O_S4 + 17 * KB)
    k2r = at("k2r", [1, T], BF16, O_S4 + 21 * KB)
    b8r = at("b8r", [1, T], BF16, O_S4 + 25 * KB)
    s_kv = at("s_kv", [1, 8], F32, O_S4 + 29 * KB)
    ytmp = [at("ytmp0", [128, 4 * C], F32, O_S4),
            at("ytmp1", [128, 4 * C], F32, O_S4 + 16 * KB)]
    stgQ = [at("stgQ0", [65, T], BF16, O_S5),
            at("stgQ1", [65, T], BF16, O_S5 + 4 * KB)]
    stgK = [at("stgK0", [65, T], BF16, O_S5 + 8 * KB),
            at("stgK1", [65, T], BF16, O_S5 + 12 * KB)]
    PTb = [at("PTb0", [128, 512], BF16, O_S5 + 16 * KB),
           at("PTb1", [128, 512], BF16, O_S5 + 17 * KB)]
    rb_sb = at("rb_sb", [64, 512], BF16, O_S5 + 18 * KB)
    rowb = at("rowb", [1, 512], BF16, O_S5 + 19 * KB)
    bv_bf = at("bv_bf", [1, C], BF16, O_S5 + 20 * KB)
    s_masks = at("s_masks", [128, 256], F32, O_S5 + 22 * KB)
    sx_t = [at("sx_t0", [128, C], F32, O_S6),
            at("sx_t1", [128, C], F32, O_S6 + 4 * KB)]
    t_t = at("t_t", [128, C], F32, O_S6 + 8 * KB)
    u_t = at("u_t", [128, C], F32, O_S6 + 12 * KB)
    w_t = at("w_t", [128, C], F32, O_S6 + 16 * KB)
    xb_t = at("xb_t", [128, C], BF16, O_S6 + 20 * KB)
    t_t2 = at("t_t2", [128, C], F32, O_S4)
    u_t2 = at("u_t2", [128, C], F32, O_S4 + 4 * KB)
    w_t2 = at("w_t2", [128, C], F32, O_S4 + 8 * KB)
    xb_t2 = at("xb_t2", [128, C], BF16, O_S4 + 12 * KB)
    qt_t2 = at("qt_t2", [128, C], F32, O_S4 + 17 * KB)
    qu_t2 = at("qu_t2", [128, C], F32, O_S4 + 21 * KB)
    qxb_t2 = at("qxb_t2", [128, C], BF16, O_S4 + 25 * KB)
    rb_f32 = at("rb_f32", [64, 512], F32, O_S4 + 14 * KB)
    SCR = [dict(t=t_t, u=u_t, w=w_t, xb=xb_t),
           dict(t=t_t2, u=u_t2, w=w_t2, xb=xb_t2)]
    QSCR = [dict(t=t_t, u=u_t, xb=xb_t),
            dict(t=qt_t2, u=qu_t2, xb=qxb_t2)]

    psum = [nc.alloc_psum_tensor(f"pb{i}", [128, 512], F32) for i in range(8)]

    P = Prog(nc)
    BT = P.bank_tok

    def bank_wait(b):
        return BT.get(b)

    def bank_done(b, tok):
        BT[b] = tok

    def ld_bcast_row(dram_row, dest, waits=()):
        n = dram_row.shape[-1]
        return P.dma("sp", lambda e, d=dram_row, o=dest, n=n: e.dma_start(
            out=o[:, 0:n], in_=d[0:1, 0:n].broadcast_to([128, n])), waits=waits)

    def ln_cols(x_ap, eps, waits=()):
        mu = s_tiny[:, 0:1]
        s2 = s_tiny[:, 8:9]
        l2 = s_tiny[:, 16:17]
        rstd = s_tiny[:, 24:25]
        P.dve(lambda e: e.reduce_sum(mu, x_ap, axis=AX.X), waits=list(waits),
              drain=True)
        P.dve(lambda e: e.tensor_scalar_mul(mu, mu, float(1.0 / C)), drain=True)
        P.dve(lambda e: e.tensor_scalar(t_t[:], x_ap, mu, None, op0=ALU.subtract),
              drain=True)
        tv = P.tok("dve")
        tq = P.act(lambda e: e.activation(u_t[:], t_t[:], AF.Square), waits=[tv])
        tr = P.dve(lambda e: e.reduce_sum(s2, u_t[:], axis=AX.X), waits=[tq])
        P.act(lambda e: e.activation(l2, s2, AF.Ln, bias=f32(eps),
                                     scale=f32(1.0 / C)), waits=[tr], drain=True)
        P.act(lambda e: e.activation(rstd, l2, AF.Exp, scale=f32(-0.5)),
              drain=True)
        return P.tok("act")

    def transpose_chunks(src_bf, dst, dst_col0, waits):
        for cc in range(CCH):
            b = 4 + (cc % 4)
            pv = psum[b][:, 0:128].bitcast(BF16)[:, 0:128]
            w = [bank_wait(b)] + list(waits)
            tk = P.pe(lambda e, pv=pv, cc=cc, s=src_bf: e.transpose(
                pv, s[:, cc * 128:(cc + 1) * 128], s_ident[:]),
                waits=w, inc="pe")
            tc = P.dve(lambda e, pv=pv, c=dst_col0 + cc * T, d=dst:
                       e.tensor_copy(d[:, c:c + 128], pv), waits=[tk])
            bank_done(b, tc)
        return P.tok("dve")

    def stream_w(dram_w, dest, ncols=1024, col0=0, waits=()):
        return P.dma("pool", lambda e: e.dma_start(
            out=dest[:, 0:CCH * ncols].rearrange("p (c f) -> p c f", c=CCH),
            in_=dram_w[:, col0:col0 + ncols].rearrange("(c p) f -> p c f", p=128)),
            waits=waits)

    def emit_lnp_src(src_dram, be, spill, first_wait=None):
        backs = []

        def emit_back(st):
            par = st["par"]
            S = SCR[par]
            co = 16 * par
            rstd = s_tiny[:, co + 3:co + 4]
            P.dve(lambda e, S=S, rstd=rstd: e.tensor_scalar_mul(
                S["xb"][:], S["t"][:], rstd), waits=[st["ta"]])
            if spill:
                tsp = P.dve(lambda e, S=S, rstd=rstd: e.tensor_scalar_mul(
                    S["w"][:], S["t"][:], rstd))
                P.dma("sp", lambda e, r=st["r0"], S=S: e.dma_start(
                    out=d_src[r:r + 128, :], in_=S["w"][:]), waits=[tsp])
            tv = P.tok("dve")
            transpose_chunks(S["xb"][:], srcT, st["i"] * 128, [tv])

        for i in range(NTILE):
            r0 = (be * NTILE + i) * 128
            par = i % 2
            S = SCR[par]
            co = 16 * par
            mu = s_tiny[:, co + 0:co + 1]
            s2 = s_tiny[:, co + 1:co + 2]
            l2 = s_tiny[:, co + 2:co + 3]
            rstd = s_tiny[:, co + 3:co + 4]
            lw = [P.tok("dve")]
            if first_wait is not None and i == 0:
                lw.append(first_wait)
            tld = P.dma("sp", lambda e, r=r0, p=par: e.dma_start(
                out=sx_t[p][:], in_=src_dram[r:r + 128, :]), waits=lw)
            P.dve(lambda e, p=par, mu=mu: e.reduce_sum(mu, sx_t[p][:], axis=AX.X),
                  waits=[tld], drain=True)
            P.dve(lambda e, mu=mu: e.tensor_scalar_mul(mu, mu, float(1.0 / C)),
                  drain=True)
            P.dve(lambda e, p=par, S=S, mu=mu: e.tensor_scalar(
                S["t"][:], sx_t[p][:], mu, None, op0=ALU.subtract), drain=True)
            tv = P.tok("dve")
            tq = P.act(lambda e, S=S: e.activation(S["u"][:], S["t"][:],
                                                   AF.Square), waits=[tv])
            if backs:
                emit_back(backs.pop(0))
            tr = P.dve(lambda e, S=S, s2=s2: e.reduce_sum(
                s2, S["u"][:], axis=AX.X), waits=[tq], drain=True)
            P.act(lambda e, s2=s2, l2=l2: e.activation(
                l2, s2, AF.Ln, bias=f32(1e-5), scale=f32(1.0 / C)),
                waits=[tr], drain=True)
            P.act(lambda e, l2=l2, rstd=rstd: e.activation(
                rstd, l2, AF.Exp, scale=f32(-0.5)), drain=True)
            ta = P.tok("act")
            backs.append(dict(i=i, par=par, ta=ta, r0=r0))
        while backs:
            emit_back(backs.pop(0))
        return P.tok("dve")

    def emit_ffn_qkv(be):
        pe0 = P.tok("pe")
        bvl = P.dma("pool", lambda e: e.dma_start(out=bv_bf[:], in_=wf["bv"][:]),
                    waits=[pe0])
        c0 = P.dma("sp", lambda e: e.dma_start(
            out=s_cols[:, 0:8], in_=wf["bq"][:]), waits=[pe0])
        c1 = P.dma("sp", lambda e: e.dma_start(
            out=s_cols[:, 8:16], in_=wf["bk"][:]), waits=[pe0])
        srcT_tok = P.tok("dve")
        for (wname, dst, b0) in (("Wq", QTt, 0), ("Wk", KTt, 8)):
            tw = stream_w(wf[wname], sWs, waits=[P.tok("pe")])
            for fc in range(CCH):
                for tb in range(4):
                    bnk = (fc * 4 + tb) % 4
                    w0 = [bank_wait(bnk)]
                    if fc == 0 and tb == 0:
                        w0 += [tw, srcT_tok, c0, c1]
                    for cc in range(CCH):
                        P.pe(lambda e, fc=fc, tb=tb, cc=cc, bnk=bnk:
                             e.matmul(psum[bnk][:],
                                      sWs[:, cc * 1024 + fc * 128:
                                          cc * 1024 + fc * 128 + 128],
                                      srcT[:, cc * T + tb * 512:
                                           cc * T + tb * 512 + 512],
                                      start=(cc == 0), stop=(cc == CCH - 1)),
                             waits=w0 if cc == 0 else [],
                             inc="pe" if cc == CCH - 1 else None)
                    tp = P.tok("pe")
                    te = P.dve(lambda e, fc=fc, tb=tb, bnk=bnk, dst=dst, b0=b0:
                               e.tensor_scalar(
                                   dst[:, fc * T + tb * 512:fc * T + tb * 512 + 512],
                                   psum[bnk][:], s_cols[:, b0 + fc:b0 + fc + 1],
                                   None, op0=ALU.add), waits=[tp])
                    bank_done(bnk, te)
        tw = stream_w(wf["Wv"], sWs, waits=[P.tok("pe")])
        tmo = P.dve(lambda e: e.memset(
            VA[:].rearrange("p (t h e2) -> p t h e2", t=NTILE, h=H)
            [:, :, :, 64:65], 1.0))
        for i in range(NTILE):
            for fb in range(2):
                bnk = (i * 2 + fb) % 4
                w0 = [bank_wait(bnk)]
                if i == 0 and fb == 0:
                    w0 += [tw, tmo, bvl]
                for cc in range(CCH):
                    P.pe(lambda e, i=i, fb=fb, cc=cc, bnk=bnk:
                         e.matmul(psum[bnk][:],
                                  srcT[:, cc * T + i * 128:cc * T + i * 128 + 128],
                                  sWs[:, cc * 1024 + fb * 512:
                                      cc * 1024 + fb * 512 + 512],
                                  start=(cc == 0), stop=False),
                         waits=w0 if cc == 0 else [],
                         inc=None)
                P.pe(lambda e, fb=fb, bnk=bnk:
                     e.matmul(psum[bnk][:], s_ones[0:1, 0:128],
                              bv_bf[0:1, fb * 512:fb * 512 + 512],
                              start=False, stop=True), inc="pe")
                tp = P.tok("pe")
                te = P.dve(lambda e, i=i, fb=fb, bnk=bnk: e.tensor_copy(
                    VA[:].rearrange("p (t h e2) -> p t h e2", t=NTILE, h=H)
                    [:, i, fb * 8:fb * 8 + 8, 0:64],
                    psum[bnk][:].rearrange("p (h d) -> p h d", h=8)),
                    waits=[tp])
                bank_done(bnk, te)
        return P.tok("dve")

    def emit_stage_head(h, par, shift=20.0, prev=None):
        cc = h // 2
        half = 64 * (h % 2)
        if prev is None:
            prev = P.tok("pe")
        P.dve(lambda e: e.tensor_copy(
            stgQ[par][0:64, :], QTt[half:half + 64, cc * T:(cc + 1) * T]),
            waits=[prev])
        P.dve(lambda e: e.tensor_copy(
            stgK[par][0:64, :], KTt[half:half + 64, cc * T:(cc + 1) * T]))
        P.dve(lambda e: e.memset(stgK[par][64:65, :], 1.0))
        tcp = P.tok("dve")
        for (stg, dstrow, ln_out) in ((stgQ, lnq2, True), (stgK, k2r, False)):
            tsq = P.act(lambda e, stg=stg: e.activation(
                sqh[0:64, :], stg[par][0:64, :], AF.Square),
                waits=[tcp, P.tok("pe")])
            for jq in range(4):
                pm = P.pe(lambda e, jq=jq: e.matmul(
                    psum[7][0:1, jq * 128:jq * 128 + 512][:, 0:512]
                    if False else psum[7][0:1, 0:512],
                    s_ones[0:64, 0:1], sqh[0:64, jq * 512:(jq + 1) * 512],
                    start=True, stop=True),
                    waits=[tsq, bank_wait(7)], inc="pe")
                if ln_out:
                    tc = P.act(lambda e, jq=jq: e.activation(
                        lnq2[0:1, jq * 512:(jq + 1) * 512], psum[7][0:1, 0:512],
                        AF.Ln), waits=[pm], drain=(jq == 3))
                else:
                    tc = P.act(lambda e, jq=jq: e.activation(
                        k2r[0:1, jq * 512:(jq + 1) * 512], psum[7][0:1, 0:512],
                        AF.Copy, bias=0.0), waits=[pm], drain=(jq == 3))
                bank_done(7, tc)
        tk2 = P.tok("act")
        k2m = s_kv[0:1, 0:1]
        hlf = s_kv[0:1, 1:2]
        P.dve(lambda e: e.reduce_max(k2m, k2r[:], axis=AX.X), waits=[tk2], drain=True)
        tvm = P.tok("dve")
        P.act(lambda e: e.activation(hlf, k2m, AF.Ln), waits=[tvm], drain=True)
        ta = P.tok("act")
        P.dve(lambda e: e.tensor_scalar_mul(hlf, hlf, float(0.5)), waits=[ta], drain=True)
        tvh = P.tok("dve")
        P.act(lambda e: e.activation(b8r[:], lnq2[:], AF.Exp, scale=f32(0.5),
                                     bias=hlf), waits=[tvh], drain=True)
        tb8 = P.tok("act")
        P.dve(lambda e: e.tensor_scalar(
            stgQ[par][64:65, :], b8r[:], float(-1.0), float(8.0 * shift),
            op0=ALU.mult, op1=ALU.add), waits=[tb8], drain=True)
        return P.tok("dve")

    def emit_flash_head(h, par, stg_tok, local):
        cc = h // 2
        half = 64 * (h % 2)

        def evac_chain(ob, qlo, N, tpv, trb_prev=None):
            w0 = [tpv]
            if trb_prev is not None:
                w0.append(trb_prev)
            tvr = P.dve(lambda e, ob=ob, N=N: e.tensor_copy(
                rowb[0:1, 0:N], psum[ob][64:65, 0:N]), waits=w0, drain=True)
            trb = P.pe(lambda e, N=N: e.matmul(
                psum[6][0:64, 0:N], s_ones[0:1, 0:64], rowb[0:1, 0:N],
                start=True, stop=True), waits=[tvr, bank_wait(6)], inc="pe")
            tcb = P.dve(lambda e, N=N: e.reciprocal(
                rb_f32[:, 0:N], psum[6][0:64, 0:N]), waits=[trb], drain=True)
            bank_done(6, tcb)
            ev = P.dve(lambda e, ob=ob, N=N, qlo=qlo: e.tensor_tensor(
                attnT[half:half + 64, cc * T + qlo:cc * T + qlo + N],
                psum[ob][0:64, 0:N], rb_f32[:, 0:N], op=ALU.mult),
                waits=[tcb])
            bank_done(ob, ev)
            return trb

        if not local:
            pending = None
            for (u, qlo, N, kcs) in [(qb, qb * 512, 512, list(range(16)))
                                     for qb in range(4)]:
                ob = 2 + (u % 2)
                exp_toks = {}
                pv_toks = {}
                for j, kc in enumerate(kcs):
                    sb = j % 2
                    stw = [bank_wait(sb)]
                    if j == 0:
                        stw.append(stg_tok)
                    st_tok = P.pe(
                        lambda e, kc=kc, sb=sb, qlo=qlo, N=N: e.matmul(
                            psum[sb][0:128, 0:N],
                            stgK[par][:, kc * 128:(kc + 1) * 128],
                            stgQ[par][:, qlo:qlo + N],
                            start=True, stop=True),
                        waits=stw, inc="pe")
                    ew = [st_tok]
                    if j >= 2:
                        ew.append(pv_toks[kcs[j - 2]])
                    tE = P.act(lambda e, sb=sb, N=N: e.activation(
                        PTb[sb][:, 0:N], psum[sb][0:128, 0:N], AF.Exp,
                        scale=f32(0.125)), waits=ew)
                    bank_done(sb, tE)
                    exp_toks[kc] = tE
                    pw = [tE]
                    if j == 0:
                        pw.append(bank_wait(ob))
                    pv_toks[kc] = P.pe(
                        lambda e, kc=kc, sb=sb, ob=ob, N=N,
                        st_=(kc == kcs[0]), sp_=(kc == kcs[-1]): e.matmul(
                            psum[ob][0:65, 0:N],
                            VA[:, kc * (H * 65) + h * 65:kc * (H * 65) + h * 65 + 65],
                            PTb[sb][:, 0:N],
                            start=st_, stop=sp_),
                        waits=pw, inc="pe")
                tpv = P.tok("pe")
                if pending is not None:
                    pending()
                pending = (lambda ob=ob, qlo=qlo, N=N, tpv=tpv:
                           evac_chain(ob, qlo, N, tpv))
            pending()
            return P.tok("dve")

        N = LS
        pt_slot = {}
        for gpar in range(2):
            for j in range(2):
                pt_slot[(gpar, j)] = PTb[gpar][:, j * 256:j * 256 + 256]
        pv_last = {}
        pending = []
        trb_prev = None
        for gp in range(0, 8, 2):
            st_toks = {}
            for g in (gp, gp + 1):
                stb = (0, 1) if g % 2 == 0 else (4, 5)
                for j in range(2):
                    kc = g * 2 + j
                    qlo = g * LS
                    stw = [bank_wait(stb[j])]
                    if g == 0 and j == 0:
                        stw.append(stg_tok)
                    st_toks[(g, j)] = P.pe(
                        lambda e, kc=kc, b=stb[j], qlo=qlo: e.matmul(
                            psum[b][0:128, 0:N],
                            stgK[par][:, kc * 128:(kc + 1) * 128],
                            stgQ[par][:, qlo:qlo + N],
                            start=True, stop=True),
                        waits=stw, inc="pe")
            mask_toks = {}
            for g in (gp, gp + 1):
                stb = (0, 1) if g % 2 == 0 else (4, 5)
                for j in range(2):
                    slot = (g % 2, j)
                    ew = [st_toks[(g, j)]]
                    if slot in pv_last:
                        ew.append(pv_last[slot])
                    tE = P.act(lambda e, b=stb[j], s=pt_slot[slot]: e.activation(
                        s, psum[b][0:128, 0:N], AF.Exp, scale=f32(0.125)),
                        waits=ew)
                    bank_done(stb[j], tE)
                    mask_toks[(g, j)] = P.dve(
                        lambda e, s=pt_slot[slot], j=j: e.tensor_tensor(
                            s, s, s_cmask[:, j * 256:j * 256 + 256], op=ALU.mult),
                        waits=[tE])
            for (ob_, qlo_, tpv_) in pending:
                trb_prev = evac_chain(ob_, qlo_, N, tpv_, trb_prev)
            pending = []
            for g in (gp, gp + 1):
                ob = 2 + (g % 2)
                for j in range(2):
                    kc = g * 2 + j
                    slot = (g % 2, j)
                    pw = [mask_toks[(g, j)]]
                    if j == 0:
                        pw.append(bank_wait(ob))
                    pv_last[slot] = P.pe(
                        lambda e, kc=kc, ob=ob, s=pt_slot[slot],
                        st_=(j == 0), sp_=(j == 1): e.matmul(
                            psum[ob][0:65, 0:N],
                            VA[:, kc * (H * 65) + h * 65:kc * (H * 65) + h * 65 + 65],
                            s, start=st_, stop=sp_),
                        waits=pw, inc="pe")
            tpv = P.tok("pe")
            pending = [(2 + (g % 2), g * LS, tpv) for g in (gp, gp + 1)]
        for (ob_, qlo_, tpv_) in pending:
            trb_prev = evac_chain(ob_, qlo_, N, tpv_, trb_prev)
        return P.tok("dve")

    def emit_ffn_post(be):
        tw = stream_w(wf["Wo"], sWo, waits=[P.tok("pe"), P.tok("dve")])
        bo_b = ld_bcast_row(wf["bo"], s_cb[0], waits=[P.tok("pe")])
        g1_b = ld_bcast_row(wf["ln1_g"], s_cb[1])
        b1_b = ld_bcast_row(wf["ln1_b"], s_cb[2])
        rw_b = ld_bcast_row(wf["router_w"], s_cb[3])
        rbc = P.dma("sp", lambda e: e.dma_start(
            out=s_cols[:, 48:49],
            in_=wf["router_b"][0:1, 0:1].broadcast_to([128, 1])))
        attnT_tok = P.tok("dve")
        prev_st = [None, None]
        backs = []

        def emit_back(st):
            par = st["par"]
            S = SCR[par]
            co = 16 * par
            rstd = s_tiny[:, co + 3:co + 4]
            keys = s_tiny[:, co + 4:co + 5]
            kq = s_tiny[:, co + 5:co + 6]; fi = s_tiny[:, co + 6:co + 7]
            fc_ = s_tiny[:, co + 7:co + 8]; km = s_tiny[:, co + 8:co + 9]
            ii = s_tiny[:, co + 9:co + 10].bitcast(I32)[:, 0:1]
            a1 = s_tiny[:, co + 10:co + 11]; a2 = s_tiny[:, co + 11:co + 12]
            a3 = s_tiny[:, co + 12:co + 13]
            gt = st["gt"]
            P.dve(lambda e, S=S: e.tensor_scalar_mul(S["u"][:], S["u"][:], rstd),
                  waits=[st["ta"]], drain=True)
            P.dve(lambda e, S=S: e.tensor_tensor(S["u"][:], S["u"][:], s_cb[1][:],
                                                 op=ALU.mult), drain=True)
            P.dve(lambda e, S=S: e.tensor_tensor(S["w"][:], S["u"][:], s_cb[2][:],
                                                 op=ALU.add), drain=True)
            P.dve(lambda e, S=S: e.tensor_tensor(S["u"][:], S["w"][:], s_cb[3][:],
                                                 op=ALU.mult), drain=True)
            P.dve(lambda e, S=S: e.reduce_sum(keys, S["u"][:], axis=AX.X),
                  drain=True)
            P.dve(lambda e: e.tensor_scalar_add(keys, keys, s_cols[:, 48:49]),
                  drain=True)
            P.dve(lambda e: e.tensor_scalar_mul(kq, keys, float(0.25)), drain=True)
            P.dve(lambda e: e.tensor_copy(ii, kq), drain=True)
            P.dve(lambda e: e.tensor_copy(fi, ii), drain=True)
            P.dve(lambda e: e.tensor_tensor(fc_, fi, kq, op=ALU.is_gt), drain=True)
            P.dve(lambda e: e.tensor_sub(fi, fi, fc_), drain=True)
            P.dve(lambda e: e.tensor_scalar(km, fi, float(-4.0), keys,
                                            op0=ALU.mult, op1=ALU.add), drain=True)
            P.dve(lambda e: e.tensor_single_scalar(a1, km, float(1.0),
                                                   op=ALU.is_ge))
            P.dve(lambda e: e.tensor_single_scalar(a2, km, float(2.0),
                                                   op=ALU.is_ge))
            P.dve(lambda e: e.tensor_single_scalar(a3, km, float(3.0),
                                                   op=ALU.is_ge), drain=True)
            P.dve(lambda e, g=gt: e.tensor_scalar(
                s_masks[:, g * 4:g * 4 + 1], a1, float(-1.0), float(1.0),
                op0=ALU.mult, op1=ALU.add))
            P.dve(lambda e, g=gt: e.tensor_sub(
                s_masks[:, g * 4 + 1:g * 4 + 2], a1, a2))
            P.dve(lambda e, g=gt: e.tensor_sub(
                s_masks[:, g * 4 + 2:g * 4 + 3], a2, a3))
            P.dve(lambda e, g=gt: e.tensor_copy(
                s_masks[:, g * 4 + 3:g * 4 + 4], a3), drain=True)
            tx = P.tok("dve")
            prev_st[par] = P.dma("sp", lambda e, r=st["r0"], S=S: e.dma_start(
                out=d_x1ff[r:r + 128, :], in_=S["w"][:]), waits=[tx])
            P.dve(lambda e, S=S: e.tensor_copy(S["xb"][:], S["w"][:]), drain=True)
            tv2 = P.tok("dve")
            transpose_chunks(S["xb"][:], x1Tstg, st["i"] * 128, [tv2])

        for i in range(NTILE):
            r0 = (be * NTILE + i) * 128
            gt = be * NTILE + i
            par = i % 2
            S = SCR[par]
            co = 16 * par
            mu = s_tiny[:, co + 0:co + 1]; s2 = s_tiny[:, co + 1:co + 2]
            l2 = s_tiny[:, co + 2:co + 3]; rstd = s_tiny[:, co + 3:co + 4]
            tld = P.dma("sp", lambda e, r=r0, p=par: e.dma_start(
                out=sx_t[p][:], in_=d_src[r:r + 128, :]),
                waits=[P.tok("dve"), P.tok("pe")] if i == 0 else [P.tok("dve")])
            for cb in range(2):
                bnk = 2 * par + cb
                w0 = [bank_wait(bnk)]
                if i == 0 and cb == 0:
                    w0 += [tw, bo_b, g1_b, b1_b, rw_b, rbc, attnT_tok]
                for ccc in range(CCH):
                    P.pe(lambda e, i=i, cb=cb, ccc=ccc, bnk=bnk:
                         e.matmul(psum[bnk][:],
                                  attnT[:, ccc * T + i * 128:ccc * T + i * 128 + 128],
                                  sWo[:, ccc * 1024 + cb * 512:
                                      ccc * 1024 + cb * 512 + 512],
                                  start=(ccc == 0), stop=(ccc == CCH - 1)),
                         waits=w0 if ccc == 0 else [],
                         inc="pe" if ccc == CCH - 1 else None)
                tp = P.tok("pe")
                te = P.dve(lambda e, cb=cb, bnk=bnk, S=S: e.tensor_tensor(
                    S["t"][:, cb * 512:cb * 512 + 512], psum[bnk][:],
                    s_cb[0][:, cb * 512:cb * 512 + 512], op=ALU.add),
                    waits=[tp], drain=True)
                bank_done(bnk, te)
            P.dve(lambda e, p=par, S=S: e.tensor_tensor(
                S["t"][:], S["t"][:], sx_t[p][:], op=ALU.add), waits=[tld],
                drain=True)
            P.dve(lambda e, S=S, mu=mu: e.reduce_sum(mu, S["t"][:], axis=AX.X),
                  drain=True)
            P.dve(lambda e, mu=mu: e.tensor_scalar_mul(mu, mu, float(1.0 / C)),
                  drain=True)
            P.dve(lambda e, S=S, mu=mu: e.tensor_scalar(
                S["u"][:], S["t"][:], mu, None, op0=ALU.subtract), drain=True)
            tv = P.tok("dve")
            pw = [tv] + ([prev_st[par]] if prev_st[par] else [])
            tq = P.act(lambda e, S=S: e.activation(S["w"][:], S["u"][:],
                                                   AF.Square), waits=pw)
            if backs:
                emit_back(backs.pop(0))
            tr = P.dve(lambda e, S=S, s2=s2: e.reduce_sum(
                s2, S["w"][:], axis=AX.X), waits=[tq], drain=True)
            P.act(lambda e, s2=s2, l2=l2: e.activation(
                l2, s2, AF.Ln, bias=f32(1e-5), scale=f32(1.0 / C)),
                waits=[tr], drain=True)
            P.act(lambda e, l2=l2, rstd=rstd: e.activation(
                rstd, l2, AF.Exp, scale=f32(-0.5)), drain=True)
            ta = P.tok("act")
            backs.append(dict(i=i, par=par, ta=ta, r0=r0, gt=gt))
        while backs:
            emit_back(backs.pop(0))
        tdone = P.tok("dve")
        return P.dma("sp", lambda e, b=be: e.dma_start(
            out=d_x1T[b].rearrange("c p t -> p c t"),
            in_=x1Tstg[:].rearrange("p (c t) -> p c t", c=CCH)), waits=[tdone])

    def emit_moe():
        x1t_tok = P.tok("ld")
        mask_tok = P.tok("dve")
        pe_flush = P.tok("pe")
        last_acc = None
        gidx = 0
        for e_i in range(NE):
            b2tok = ld_bcast_row(wf["b2"][e_i:e_i + 1, :], s_b2b,
                                 waits=[pe_flush if e_i == 0 else P.tok("dve")])
            b1tok = P.dma("sp", lambda e, e_i=e_i: e.dma_start(
                out=s_cols[:, 16:48], in_=wf["b1"][e_i]),
                waits=[pe_flush] if e_i == 0 else [])
            for g in range(NE):
                par = gidx % 2
                wv = [P.tok("pe"), x1t_tok]
                tw1 = P.dma("pool", lambda e, e_i=e_i, g=g, par=par: e.dma_start(
                    out=sW1g[par][:].rearrange("p (c f) -> p c f", c=CCH),
                    in_=wf["W1"][e_i].rearrange("(c p) f -> p c f", p=128)
                    [:, :, g * 1024:(g + 1) * 1024]), waits=wv)
                tw2 = P.dma("pool", lambda e, e_i=e_i, g=g, par=par: e.dma_start(
                    out=sW2g[par][:].rearrange("p (f c) -> p f c", f=8),
                    in_=wf["W2"][e_i][g * 1024:(g + 1) * 1024, :]
                    .rearrange("(fc p) c -> p fc c", p=128)), waits=wv)
                h_done = getattr(emit_moe, "_h_done", {})
                emit_moe._h_done = h_done
                for tb in range(16):
                    be = tb // 4
                    tl = tb % 4
                    xp = tb % 2
                    war = h_done.get(xp, x1t_tok)
                    tx = P.dma("sp", lambda e, be=be, tl=tl, xp=xp: e.dma_start(
                        out=sxt[xp][:].rearrange("p (c t) -> p c t", c=CCH),
                        in_=d_x1T[be].rearrange("c p t -> p c t")
                        [:, :, tl * 512:tl * 512 + 512]),
                        waits=[war, x1t_tok])
                    for ffc in range(8):
                        bnk = 5 + (ffc % 3)
                        w0 = [bank_wait(bnk), tw1, tx, b1tok]
                        for ccc in range(CCH):
                            P.pe(lambda e, ffc=ffc, ccc=ccc, bnk=bnk, par=par, xp=xp:
                                 e.matmul(psum[bnk][:],
                                          sW1g[par][:, ccc * 1024 + ffc * 128:
                                                    ccc * 1024 + ffc * 128 + 128],
                                          sxt[xp][:, ccc * 512:ccc * 512 + 512],
                                          start=(ccc == 0), stop=(ccc == CCH - 1)),
                                 waits=w0 if ccc == 0 else [],
                                 inc="pe" if ccc == CCH - 1 else None)
                        tp = P.tok("pe")
                        th = P.act(lambda e, ffc=ffc, bnk=bnk, g=g:
                                   e.activation(h_sb[:, ffc * 512:ffc * 512 + 512],
                                                psum[bnk][:], AF.Relu,
                                                bias=s_cols[:, 16 + g * 8 + ffc:
                                                            17 + g * 8 + ffc],
                                                scale=f32(1.0)), waits=[tp])
                        bank_done(bnk, th)
                    h_done[xp] = P.tok("pe")
                    th_all = P.tok("act")
                    for tsub in range(4):
                        for cb in range(2):
                            bnk = (tsub * 2 + cb) % 4
                            w0 = [bank_wait(bnk), tw2, th_all]
                            if tsub == 0 and cb == 0:
                                w0.append(P.tok("sw"))
                            for ffc in range(8):
                                P.pe(lambda e, ts=tsub, cb=cb, ffc=ffc, bnk=bnk, par=par:
                                     e.matmul(psum[bnk][:],
                                              h_sb[:, ffc * 512 + ts * 128:
                                                   ffc * 512 + ts * 128 + 128],
                                              sW2g[par][:, ffc * 1024 + cb * 512:
                                                        ffc * 1024 + cb * 512 + 512],
                                              start=(ffc == 0), stop=(ffc == 7)),
                                     waits=w0 if ffc == 0 else [],
                                     inc="pe" if ffc == 7 else None)
                            tp = P.tok("pe")
                            gt = tb * 4 + tsub
                            mcol = s_masks[:, gt * 4 + e_i:gt * 4 + e_i + 1]
                            dsl = (tsub * C + cb * 512, tsub * C + cb * 512 + 512)
                            if g == NE - 1:
                                P.dve(lambda e, bnk=bnk, xp=xp, dsl=dsl:
                                      e.tensor_tensor(
                                          ytmp[xp][:, dsl[0]:dsl[1]], psum[bnk][:],
                                          s_b2b[:, (dsl[0] % C):(dsl[0] % C) + 512],
                                          op=ALU.add),
                                      waits=[tp, b2tok, mask_tok], drain=True)
                                te = P.dve(lambda e, xp=xp, dsl=dsl, m=mcol:
                                           e.tensor_scalar_mul(
                                               ytmp[xp][:, dsl[0]:dsl[1]],
                                               ytmp[xp][:, dsl[0]:dsl[1]], m))
                            else:
                                te = P.dve(lambda e, bnk=bnk, xp=xp, dsl=dsl, m=mcol:
                                           e.tensor_scalar(
                                               ytmp[xp][:, dsl[0]:dsl[1]],
                                               psum[bnk][:], m, None, op0=ALU.mult),
                                           waits=[tp, mask_tok])
                            bank_done(bnk, te)
                    ty = P.tok("dve")
                    r0 = tb * 512
                    last_acc = P.dma("pool", lambda e, r=r0, xp=xp: e.dma_start(
                        out=d_x1ff[r:r + 512, :].rearrange("(s p) c -> p s c", p=128),
                        in_=ytmp[xp][:].rearrange("p (s c) -> p s c", s=4),
                        accum_op=ALU.add), waits=[ty, x1t_tok])
                gidx += 1
        return last_acc

    def emit_ln2_resid(res_src, res_dst):
        acc_done = P.tok("sw")
        g2_b = ld_bcast_row(wf["ln2_g"], s_cb[1], waits=[P.tok("dve")])
        b2_b = ld_bcast_row(wf["ln2_b"], s_cb[2])
        prev_st = [None, None]
        backs = []
        last = [None]

        def emit_back(st):
            par = st["par"]
            S = SCR[par]
            co = 16 * par
            rstd = s_tiny[:, co + 3:co + 4]
            P.dve(lambda e, S=S: e.tensor_scalar_mul(S["t"][:], S["t"][:], rstd),
                  waits=[st["ta"], st["txl"]], drain=True)
            P.dve(lambda e, S=S: e.tensor_tensor(S["t"][:], S["t"][:], s_cb[1][:],
                                                 op=ALU.mult), drain=True)
            P.dve(lambda e, S=S: e.tensor_tensor(S["t"][:], S["t"][:], s_cb[2][:],
                                                 op=ALU.add), drain=True)
            tdv = P.dve(lambda e, S=S: e.scalar_tensor_tensor(
                S["t"][:], S["t"][:], 0.5, S["u"][:], op0=ALU.mult, op1=ALU.add),
                drain=True)
            prev_st[par] = P.dma("sp", lambda e, r=st["r0"], S=S: e.dma_start(
                out=res_dst[r:r + 128, :], in_=S["t"][:]), waits=[tdv], inc="st")
            last[0] = prev_st[par]

        for i in range(B * NTILE):
            r0 = i * 128
            par = i % 2
            S = SCR[par]
            co = 16 * par
            mu = s_tiny[:, co + 0:co + 1]; s2 = s_tiny[:, co + 1:co + 2]
            l2 = s_tiny[:, co + 2:co + 3]; rstd = s_tiny[:, co + 3:co + 4]
            lw = [acc_done, g2_b, b2_b] if i == 0 else [P.tok("dve")]
            tld = P.dma("sp", lambda e, r=r0, p=par: e.dma_start(
                out=sx_t[p][:], in_=d_x1ff[r:r + 128, :]), waits=lw)
            txl = P.dma("sp", lambda e, r=r0, S=S: e.dma_start(
                out=S["u"][:], in_=res_src[r:r + 128, :]), waits=[P.tok("dve")])
            pw = [tld] + ([prev_st[par]] if prev_st[par] else [])
            P.dve(lambda e, p=par, mu=mu: e.reduce_sum(mu, sx_t[p][:], axis=AX.X),
                  waits=pw, drain=True)
            P.dve(lambda e, mu=mu: e.tensor_scalar_mul(mu, mu, float(1.0 / C)),
                  drain=True)
            P.dve(lambda e, p=par, S=S, mu=mu: e.tensor_scalar(
                S["t"][:], sx_t[p][:], mu, None, op0=ALU.subtract), drain=True)
            tv = P.tok("dve")
            tq = P.act(lambda e, S=S: e.activation(S["w"][:], S["t"][:],
                                                   AF.Square), waits=[tv])
            if backs:
                emit_back(backs.pop(0))
            tr = P.dve(lambda e, S=S, s2=s2: e.reduce_sum(
                s2, S["w"][:], axis=AX.X), waits=[tq], drain=True)
            P.act(lambda e, s2=s2, l2=l2: e.activation(
                l2, s2, AF.Ln, bias=f32(1e-5), scale=f32(1.0 / C)),
                waits=[tr], drain=True)
            P.act(lambda e, l2=l2, rstd=rstd: e.activation(
                rstd, l2, AF.Exp, scale=f32(-0.5)), drain=True)
            ta = P.tok("act")
            backs.append(dict(i=i, par=par, ta=ta, r0=r0, txl=txl))
        while backs:
            emit_back(backs.pop(0))
        return last[0]

    def emit_attn_qkv(be):
        lnw_b = ld_bcast_row(wa["ln_w"], s_cb[0], waits=[P.tok("pe")])
        srcT_tok = P.tok("dve")
        for (col0, dst) in ((0, QTt), (C, KTt)):
            tw = stream_w(wa["w_qkv"], sWs, col0=col0,
                          waits=[P.tok("pe"), P.tok("dve")])
            backs = []

            def emit_back(st, dst=dst):
                par = st["par"]
                S = QSCR[par]
                co = 16 * par
                rstd = s_tiny[:, co + 3:co + 4]
                P.dve(lambda e, S=S, rstd=rstd: e.scalar_tensor_tensor(
                    S["xb"][:], S["t"][:], rstd, s_cb[0][:],
                    op0=ALU.mult, op1=ALU.mult), waits=[st["ta"]], drain=True)
                tv2 = P.tok("dve")
                transpose_chunks(S["xb"][:], dst, st["i"] * 128, [tv2])

            for i in range(NTILE):
                par = i % 2
                S = QSCR[par]
                co = 16 * par
                s2 = s_tiny[:, co + 1:co + 2]; l2 = s_tiny[:, co + 2:co + 3]
                rstd = s_tiny[:, co + 3:co + 4]
                for fb in range(2):
                    bnk = 2 * par + fb
                    w0 = [bank_wait(bnk)]
                    if i == 0 and fb == 0:
                        w0 += [tw, srcT_tok, lnw_b]
                    for ccc in range(CCH):
                        P.pe(lambda e, i=i, fb=fb, ccc=ccc, bnk=bnk:
                             e.matmul(psum[bnk][:],
                                      srcT[:, ccc * T + i * 128:ccc * T + i * 128 + 128],
                                      sWs[:, ccc * 1024 + fb * 512:
                                          ccc * 1024 + fb * 512 + 512],
                                      start=(ccc == 0), stop=(ccc == CCH - 1)),
                             waits=w0 if ccc == 0 else [],
                             inc="pe" if ccc == CCH - 1 else None)
                    tp = P.tok("pe")
                    te = P.dve(lambda e, fb=fb, bnk=bnk, S=S: e.tensor_copy(
                        S["t"][:, fb * 512:fb * 512 + 512], psum[bnk][:]),
                        waits=[tp])
                    bank_done(bnk, te)
                tv = P.tok("dve")
                tq = P.act(lambda e, S=S: e.activation(S["u"][:], S["t"][:],
                                                       AF.Square), waits=[tv])
                if backs:
                    emit_back(backs.pop(0))
                tr = P.dve(lambda e, S=S, s2=s2: e.reduce_sum(
                    s2, S["u"][:], axis=AX.X), waits=[tq], drain=True)
                P.act(lambda e, s2=s2, l2=l2: e.activation(
                    l2, s2, AF.Ln, bias=f32(1e-6), scale=f32(1.0 / C)),
                    waits=[tr], drain=True)
                P.act(lambda e, l2=l2, rstd=rstd: e.activation(
                    rstd, l2, AF.Exp, scale=f32(-0.5)), drain=True)
                ta = P.tok("act")
                backs.append(dict(i=i, par=par, ta=ta))
            while backs:
                emit_back(backs.pop(0))
        tw = stream_w(wa["w_qkv"], sWs, col0=2 * C,
                      waits=[P.tok("pe"), P.tok("dve")])
        tmo = P.dve(lambda e: e.memset(
            VA[:].rearrange("p (t h e2) -> p t h e2", t=NTILE, h=H)
            [:, :, :, 64:65], 1.0))
        for i in range(NTILE):
            for fb in range(2):
                bnk = (i * 2 + fb) % 4
                w0 = [bank_wait(bnk)]
                if i == 0 and fb == 0:
                    w0 += [tw, tmo]
                for ccc in range(CCH):
                    P.pe(lambda e, i=i, fb=fb, ccc=ccc, bnk=bnk:
                         e.matmul(psum[bnk][:],
                                  srcT[:, ccc * T + i * 128:ccc * T + i * 128 + 128],
                                  sWs[:, ccc * 1024 + fb * 512:
                                      ccc * 1024 + fb * 512 + 512],
                                  start=(ccc == 0), stop=(ccc == CCH - 1)),
                         waits=w0 if ccc == 0 else [],
                         inc="pe" if ccc == CCH - 1 else None)
                tp = P.tok("pe")
                te = P.dve(lambda e, i=i, fb=fb, bnk=bnk: e.tensor_copy(
                    VA[:].rearrange("p (t h e2) -> p t h e2", t=NTILE, h=H)
                    [:, i, fb * 8:fb * 8 + 8, 0:64],
                    psum[bnk][:].rearrange("p (h d) -> p h d", h=8)),
                    waits=[tp])
                bank_done(bnk, te)
        return P.tok("dve")

    def emit_attn_proj(be):
        tw = stream_w(wa["w_proj"], sWo, waits=[P.tok("pe"), P.tok("dve")])
        attnT_tok = P.tok("dve")
        prev_st = None
        for i in range(NTILE):
            r0 = (be * NTILE + i) * 128
            p = i % 2
            tld = P.dma("sp", lambda e, r=r0, p=p: e.dma_start(
                out=sx_t[p][:], in_=d_resid[r:r + 128, :]), waits=[P.tok("dve")])
            for cb in range(2):
                bnk = cb
                w0 = [bank_wait(bnk)]
                if i == 0 and cb == 0:
                    w0 += [tw, attnT_tok]
                for ccc in range(CCH):
                    P.pe(lambda e, i=i, cb=cb, ccc=ccc, bnk=bnk:
                         e.matmul(psum[bnk][:],
                                  attnT[:, ccc * T + i * 128:ccc * T + i * 128 + 128],
                                  sWo[:, ccc * 1024 + cb * 512:
                                      ccc * 1024 + cb * 512 + 512],
                                  start=(ccc == 0), stop=(ccc == CCH - 1)),
                         waits=w0 if ccc == 0 else [],
                         inc="pe" if ccc == CCH - 1 else None)
                tp = P.tok("pe")
                pw = [tp, tld]
                if prev_st is not None:
                    pw.append(prev_st)
                te = P.dve(lambda e, cb=cb, bnk=bnk, p=p: e.tensor_tensor(
                    t_t[:, cb * 512:cb * 512 + 512], psum[bnk][:],
                    sx_t[p][:, cb * 512:cb * 512 + 512], op=ALU.add), waits=pw,
                    drain=True)
                bank_done(bnk, te)
            tdv = P.tok("dve")
            prev_st = P.dma("sp", lambda e, r=r0: e.dma_start(
                out=d_resid[r:r + 128, :], in_=t_t[:]), waits=[tdv], inc="st")
        return prev_st

    def dbg_dump(slabs=(), va=False, dram=None, masks=False):
        w = [P.tok("pe"), P.tok("dve"), P.tok("act"), P.tok("ld"), P.tok("sw")]
        for idx, s in enumerate(slabs):
            P.dma("sp", lambda e, s=s, i=idx: e.dma_start(
                out=dbg_slab[i], in_=s[:]), waits=w, inc="st")
        if va:
            P.dma("sp", lambda e: e.dma_start(out=dbg_va[:], in_=VA[:]),
                  waits=w, inc="st")
        if dram is not None:
            P.dma("sp", lambda e, d=dram: e.dma_start(out=dbg_f[:], in_=d[:]),
                  waits=w, inc="st")
        if masks:
            P.dma("sp", lambda e: e.dma_start(out=dbg_m[:], in_=s_masks[:]),
                  waits=w, inc="st")
        P.emit("sp", lambda e: e.nop(), waits=[P.tok("st")])
        P.realize()
        return nc

    P.dma("sp", lambda e: e.dma_start(out=s_ident[:], in_=ident_dr[:]))
    P.dma("sp", lambda e: e.dma_start(out=s_cmask[:], in_=cmask_dr[:]))
    ic = P.tok("ld")
    P.dve(lambda e: e.memset(s_ones[:], 1.0), waits=[ic])

    def ffn_call(res_src, res_dst, first_wait, tag=""):
        fw = first_wait
        for be in range(B):
            r0 = emit_lnp_src(res_src, be, spill=True, first_wait=fw)
            if upto == "f1_t0":
                return nc
            fw = None
            if upto == tag + "lnp":
                return dbg_dump(slabs=[srcT], dram=d_src)
            emit_ffn_qkv(be)
            if upto == tag + "qkv":
                return dbg_dump(slabs=[QTt, KTt], va=True)
            for h in range(H):
                stg = emit_stage_head(h, h % 2)
                emit_flash_head(h, h % 2, stg, local=False)
            if upto == tag + "flash":
                return dbg_dump(slabs=[attnT])
            emit_ffn_post(be)
            if upto == tag + "post":
                return dbg_dump(dram=d_x1ff, masks=True)
        if upto == tag + "post4":
            return dbg_dump(dram=d_x1ff, masks=True)
        emit_moe()
        if upto == tag + "moe":
            return dbg_dump(dram=d_x1ff)
        r = emit_ln2_resid(res_src, res_dst)
        if upto == tag + "ln2":
            return dbg_dump(dram=res_dst)
        return r

    def attn_call(first_wait):
        fw = first_wait
        for be in range(B):
            emit_lnp_src(d_resid, be, spill=False, first_wait=fw)
            fw = None
            if upto == "a_lnp":
                return dbg_dump(slabs=[srcT])
            emit_attn_qkv(be)
            if upto == "a_qkv":
                return dbg_dump(slabs=[QTt, KTt], va=True)
            flash_done = {0: None, 1: None}
            for h in range(H):
                stg = emit_stage_head(h, h % 2, prev=flash_done[h % 2])
                rr = emit_flash_head(h, h % 2, stg, local=True)
                flash_done[h % 2] = P.tok("pe")
                if rr is None:
                    return nc
            if upto == "a_flash":
                return dbg_dump(slabs=[attnT])
            emit_attn_proj(be)
        if upto == "a_proj":
            return dbg_dump(dram=d_resid)

    if upto is not None and upto == "aonly":
        import os as _os
        nheads = int(_os.environ.get("DBG_HEADS", "3"))
        tcp0 = P.dma("sp", lambda e: e.dma_start(out=d_resid[:], in_=xin[:]))
        emit_lnp_src(d_resid, 0, spill=False, first_wait=tcp0)
        emit_attn_qkv(0)
        for h in range(nheads):
            stg = emit_stage_head(h, h % 2)
            rr = emit_flash_head(h, h % 2, stg, local=True)
            if rr is None:
                return nc
        return dbg_dump(slabs=[attnT, QTt, KTt], va=True)

    r = ffn_call(xin, d_resid, P.tok("dve"), tag="f1_")
    if upto is not None and upto.startswith("f1_"):
        return r
    r = attn_call(P.tok("st"))
    if upto is not None and upto.startswith("a_"):
        return r
    r = ffn_call(d_resid, out, P.tok("st"), tag="f2_")
    if upto is not None and upto.startswith("f2_"):
        return r

    P.emit("sp", lambda e: e.nop(), waits=[P.tok("st"), P.tok("sw"), P.tok("ld")])
    P.realize()
    return nc


_NC_CACHE = {}


def kernel(**inputs):
    n = 8
    x = np.ascontiguousarray(np.asarray(inputs["L0_attn_x"], dtype=np.float32))
    shards = [np.ascontiguousarray(x[i * B:(i + 1) * B].reshape(NTOK, C))
              for i in range(n)]

    def g(name):
        return np.ascontiguousarray(np.asarray(inputs[name], dtype=np.float32))

    base = {}
    base["a_w_qkv"] = g("L0_attn_w_qkv")
    base["a_w_proj"] = g("L0_attn_w_proj")
    base["a_ln_w"] = g("L0_attn_ln_w").reshape(1, C)
    for nm in ("Wq", "Wk", "Wv", "Wo"):
        base["f_" + nm] = g("L0_ffn_" + nm)
    for nm in ("bv", "bo", "ln1_g", "ln1_b", "ln2_g", "ln2_b"):
        base["f_" + nm] = g("L0_ffn_" + nm).reshape(1, C)
    base["f_bq"] = np.ascontiguousarray(
        g("L0_ffn_bq").reshape(8, 128).T)
    base["f_bk"] = np.ascontiguousarray(
        g("L0_ffn_bk").reshape(8, 128).T)
    base["f_b1"] = np.ascontiguousarray(
        g("L0_ffn_b1").reshape(NE, 32, 128).transpose(0, 2, 1))
    base["f_router_w"] = g("L0_ffn_router_w").reshape(1, C)
    base["f_router_b"] = g("L0_ffn_router_b").reshape(1, 1)
    base["f_W1"] = g("L0_ffn_W1")
    base["f_W2"] = g("L0_ffn_W2")
    base["f_b2"] = g("L0_ffn_b2")

    if "nc" not in _NC_CACHE:
        _NC_CACHE["nc"] = build_program()
    nc = _NC_CACHE["nc"]
    in_maps = [dict(base, x=s) for s in shards]
    res = run_bass_kernel_spmd(nc, in_maps, core_ids=list(range(n)))
    outs = [np.asarray(res.results[i]["out"]).reshape(B, T, C) for i in range(n)]
    return np.concatenate(outs, axis=0).astype(np.float32)
